# version 01
# baseline (speedup 1.0000x reference)
import contextlib
import numpy as np
import ml_dtypes
import concourse.bass as bass
import concourse.mybir as mybir
from concourse.bass_utils import run_bass_kernel_spmd

F32 = mybir.dt.float32
BF16 = mybir.dt.bfloat16
FP16 = mybir.dt.float16
AF = mybir.ActivationFunctionType
ALU = mybir.AluOpType
BFnp = ml_dtypes.bfloat16

S = 8192
H = 2048
NH = 16
DK = DV = 128
CH = 128
NCHUNK = S // CH
SLAB = 256
NSLAB = S // SLAB
PCOLS = 6176
INTER = 8192
TH = S // 2
MP = SLAB + 8

REPL = [[0, 1], [2, 3], [4, 5], [6, 7]]
LN128H = 0.5 * float(np.log(128.0))
RSQ128 = float(128.0 ** -0.5)


def _host_consts():
    ident = np.eye(128, dtype=BFnp)
    identf = np.eye(128, dtype=np.float32)
    m_us_neg = (-np.triu(np.ones((128, 128), np.float32), 1)).astype(BFnp)
    m_ui = np.triu(np.ones((128, 128), np.float32), 0).astype(BFnp)
    indt = np.zeros((64, 4096), np.float16)
    for h in range(16):
        for off in (0, 2048):
            indt[h, off + h * 128:off + (h + 1) * 128] = -1.0
            indt[32 + h, off + h * 128:off + (h + 1) * 128] = -1.0
    oh16 = np.zeros((128, 256), np.float32)
    for cb in range(16):
        oh16[:, cb * 16 + cb] = 1.0
    return {"IDENT": ident, "IDENTF": identf, "MASKUSN": m_us_neg, "MASKUI": m_ui,
            "INDT": indt, "OH16": oh16.astype(BFnp),
            "ONESROW": np.ones((1, 128), BFnp), "ONESCOL": np.ones((128, 1), BFnp),
            "ONESROWF": np.ones((1, 128), np.float32),
            "ONESCOLF": np.ones((128, 1), np.float32),
            "ONES2R": np.ones((2, 128), np.float16)}


def _host_core_inputs(core, inp, consts):
    b, hg = core // 2, core % 2
    w_qkv = np.asarray(inp["L0_attn_w_qkv"])
    w_z = np.asarray(inp["L0_attn_w_z"])
    w_b = np.asarray(inp["L0_attn_w_b"])
    w_a = np.asarray(inp["L0_attn_w_a"])
    m = dict(consts)
    xTfull = np.ascontiguousarray(np.asarray(inp["L0_attn_hidden_states"])[b].T)
    m["xT"] = xTfull
    m["xTd"] = np.ascontiguousarray(xTfull[:, hg * TH:(hg + 1) * TH])
    m["wproj"] = np.ascontiguousarray(np.concatenate([
        w_qkv[:, hg * 1024:(hg + 1) * 1024],
        w_qkv[:, 2048 + hg * 1024:2048 + (hg + 1) * 1024],
        w_qkv[:, 4096 + hg * 2048:4096 + (hg + 1) * 2048],
        w_z[:, hg * 2048:(hg + 1) * 2048],
        w_b[:, hg * 16:(hg + 1) * 16],
        w_a[:, hg * 16:(hg + 1) * 16]], axis=1))
    cw = np.asarray(inp["L0_attn_conv_w"])[:, 0, :]
    ch_rows = np.concatenate([
        cw[hg * 1024:(hg + 1) * 1024],
        cw[2048 + hg * 1024:2048 + (hg + 1) * 1024],
        cw[4096 + hg * 2048:4096 + (hg + 1) * 2048]], axis=0)
    m["convw"] = np.ascontiguousarray(
        ch_rows.reshape(32, 128, 4).transpose(1, 0, 2).reshape(128, 128))
    m["wout"] = np.ascontiguousarray(np.asarray(inp["L0_attn_w_out"])[hg * 2048:(hg + 1) * 2048, :])
    wgu_in = np.asarray(inp["L0_ffn_w_gate_up"])
    wg = wgu_in[:INTER].T
    wu = wgu_in[INTER:].T
    gu = np.empty((H, 2 * INTER), np.float32)
    for i in range(INTER // 128):
        gu[:, (2 * i) * 128:(2 * i + 1) * 128] = wg[:, i * 128:(i + 1) * 128]
        gu[:, (2 * i + 1) * 128:(2 * i + 2) * 128] = wu[:, i * 128:(i + 1) * 128]
    m["wgu"] = np.ascontiguousarray(gu)
    m["wdn"] = np.ascontiguousarray(np.asarray(inp["L0_ffn_w_down"]).T)
    m["dtb"] = np.ascontiguousarray(
        np.asarray(inp["L0_attn_dt_bias"]).astype(np.float32)[hg * 16:(hg + 1) * 16].reshape(16, 1))
    m["alog"] = np.ascontiguousarray(
        np.asarray(inp["L0_attn_A_log"]).astype(np.float32)[hg * 16:(hg + 1) * 16].reshape(16, 1))
    m["nrmw"] = np.ascontiguousarray(
        np.asarray(inp["L0_attn_norm_w"]).astype(np.float32).reshape(128, 1))
    return m


class Sem:
    def __init__(self, handle):
        self.h = handle
        self.n = 0

    def inc(self, k=1):
        self.n += k
        return self.n


class KB:
    def __init__(self, debug_taps=(), stop_after=None):
        self.debug_taps = set(debug_taps)
        self.stop_after = stop_after
        self.nc = bass.Bass("TRN2", target_bir_lowering=False, debug=False)
        self.Q = {e: [] for e in ("tensor", "vector", "scalar", "gpsimd", "sync")}
        self.st_ = {}
        self.pfree = {}

    def em(self, e, fn):
        self.Q[e].append(fn)

    def dram(self, name, shape, dtype):
        k = "ExternalOutput" if name in self.debug_taps else "Internal"
        return self.nc.dram_tensor(name, shape, dtype, kind=k).ap()

    def pw(self, eng, name):
        v = self.pfree.get(name)
        if v is not None:
            eng.wait_ge(v[0], v[1])

    def puse(self, name, sem, val):
        self.pfree[name] = (sem, val)

    def dump(self, name, ap, shape, dtype):
        if name not in self.debug_taps:
            return
        D = self.st_
        if name not in D:
            D[name] = self.nc.dram_tensor(name, shape, dtype, kind="ExternalOutput").ap()
        waits = [(s.h, s.n) for s in (self.pe, self.dve, self.act, self.pool) if s.n > 0]
        dst = D[name]

        def f(eng, waits=tuple(waits), dst=dst, ap=ap):
            for h, v in waits:
                eng.wait_ge(h, v)
            eng.dma_start(out=dst, in_=ap).then_inc(self.stm.h, 16)
        self.em("sync", f)
        self.stm.inc(16)

    def barrier(self):
        waits = [(s.h, s.n) for s in
                 (self.pe, self.dve, self.act, self.pool, self.ld, self.cast,
                  self.stm, self.sm, self.ccm) if s.n > 0]
        for e in self.Q:
            def f(eng, waits=tuple(waits)):
                for h, v in waits:
                    eng.wait_ge(h, v)
            self.em(e, f)
        self.pfree = {}

    def build(self):
        nc = self.nc
        with contextlib.ExitStack() as es:
            self._declare(es)
            with contextlib.ExitStack() as es_ps:
                self._declare_ps(es_ps)
                self._setup()
                self._phase_P()
                if self.stop_after == "P":
                    es_ps_hold = True
                else:
                    self._phase_S()
            if self.stop_after not in ("P", "S"):
                self._phase_D()

            stm, sm, cast, ld, ccm = self.stm, self.sm, self.cast, self.ld, self.ccm

            def f(eng, a=stm.n, b=sm.n, c=cast.n, d=ld.n, e=ccm.n):
                eng.wait_ge(stm.h, a)
                eng.wait_ge(sm.h, b)
                eng.wait_ge(cast.h, c)
                eng.wait_ge(ld.h, d)
                if e:
                    eng.wait_ge(ccm.h, e)
            self.em("sync", f)

            blk = es.enter_context(nc.Block())
            Q = self.Q

            @blk.tensor
            def _(eng):
                for fn in Q["tensor"]:
                    fn(eng)

            @blk.vector
            def _(eng):
                for fn in Q["vector"]:
                    fn(eng)

            @blk.scalar
            def _(eng):
                for fn in Q["scalar"]:
                    fn(eng)

            @blk.gpsimd
            def _(eng):
                for fn in Q["gpsimd"]:
                    fn(eng)

            @blk.sync
            def _(eng):
                for fn in Q["sync"]:
                    fn(eng)
        return nc

    def _declare(self, es):
        nc, D = self.nc, self.st_
        for nm, sh, dt in (
            ("xT", [H, S], F32), ("xTd", [H, TH], F32), ("wproj", [H, PCOLS], F32),
            ("convw", [128, 128], F32), ("wout", [H, H], F32), ("wgu", [H, 2 * INTER], F32),
            ("wdn", [INTER, H], F32), ("dtb", [16, 1], F32), ("alog", [16, 1], F32),
            ("nrmw", [128, 1], F32), ("IDENT", [128, 128], BF16), ("IDENTF", [128, 128], F32),
            ("MASKUSN", [128, 128], BF16), ("MASKUI", [128, 128], BF16),
            ("INDT", [64, 4096], FP16), ("OH16", [128, 256], BF16),
            ("ONESROW", [1, 128], BF16), ("ONESCOL", [128, 1], BF16),
            ("ONESROWF", [1, 128], F32), ("ONESCOLF", [128, 1], F32),
            ("ONES2R", [2, 128], FP16),
        ):
            D[nm] = nc.dram_tensor(nm, sh, dt, kind="ExternalInput").ap()
        D["mixedD"] = self.dram("mixedD", [4096, S], BF16)
        D["zsD"] = self.dram("zsD", [H, S], BF16)
        D["onD"] = self.dram("onD", [H, S], BF16)
        D["rsIn"] = self.dram("rsIn", [2 * H, TH], BF16)
        D["rsOut"] = self.dram("rsOut", [H, TH], BF16)
        D["hD"] = self.dram("hD", [INTER, TH], BF16)
        D["x1nD"] = self.dram("x1nD", [H, TH], BF16)
        D["dnD"] = self.dram("dnD", [H, TH], F32)
        D["smallsD"] = self.dram("smallsD", [128, S], F32)
        D["csplitD"] = self.dram("csplitD", [48, S], FP16)
        D["cbsplitD"] = self.dram("cbsplitD", [48, S], FP16)
        D["cD"] = self.dram("cD", [16, S], F32)
        D["l2rD"] = self.dram("l2rD", [16, SLAB], F32)
        D["outD"] = nc.dram_tensor("out", [H, TH], F32, kind="ExternalOutput").ap()

        for nm, sh, dt in (
            ("cons", [128, 1024], BF16), ("consf", [128, 272], F32),
            ("identf", [128, 128], F32), ("ones2", [2, 128], FP16),
            ("onescf", [128, 1], F32),
        ):
            D[nm] = es.enter_context(nc.sbuf_tensor(nm, sh, dt))
        for nm, sh, dt in (
            ("pA0", [128, 512], F32), ("pA1", [128, 512], F32),
            ("pA2", [128, 512], F32), ("pA3", [128, 512], F32),
            ("pB0", [128, 512], F32), ("pB1", [128, 512], F32),
            ("pT0", [128, 1024], BF16), ("pT1", [128, 1024], BF16),
        ):
            D[nm] = es.enter_context(nc.psum_tensor(nm, sh, dt))
        self.pA = [D["pA0"], D["pA1"], D["pA2"], D["pA3"]]
        self.pB = [D["pB0"], D["pB1"]]
        self.pT = [D["pT0"], D["pT1"]]
        for nm in ("pe", "dve", "act", "pool", "ld", "cast", "stm", "sm", "ccm"):
            setattr(self, nm, Sem(es.enter_context(nc.semaphore("s_" + nm))))
        cons, consf = D["cons"], D["consf"]
        self.C_ID = cons[:, 0:128]
        self.C_MUSN = cons[:, 128:256]
        self.C_MUI = cons[:, 256:384]
        self.C_OH = lambda cb: cons[:, 384 + cb * 16:384 + (cb + 1) * 16]
        self.C_OCOL = cons[:, 640:641]
        self.C_OROW = cons[0:1, 642:770]
        self.CF_CONV = lambda cb, j: consf[:, cb * 4 + j:cb * 4 + j + 1]
        self.CF_DTB = consf[0:16, 128:129]
        self.CF_ALOG = consf[0:16, 129:130]
        self.CF_NEGEA = consf[0:16, 130:131]
        self.CF_NRMW = consf[:, 131:132]
        self.CF_OROWF = consf[0:1, 132:260]
        self.CF_DTB32 = consf[32:48, 260:261]
        self.CF_NEGEA96 = consf[96:112, 261:262]
        self.CF_EPS6 = consf[:, 262:263]
        self.CF_RMSE = consf[:, 263:264]
        self.CF_LN128H = consf[:, 264:265]
        self.CF_EPS5 = consf[:, 265:266]
        self.CF_I2048 = consf[:, 266:267]
        self.C_IDF = D["identf"][:, :]
        self.C_OCOLF = D["onescf"][:, :]

    def _declare_ps(self, es_ps):
        nc, D = self.nc, self.st_
        for nm, sh, dt in (
            ("persF", [128, S], F32), ("csplit", [66, S], FP16), ("cbsplit", [66, S], FP16),
        ):
            D[nm] = es_ps.enter_context(nc.sbuf_tensor(nm, sh, dt))

    def _setup(self):
        D = self.st_
        ld, act, dve = self.ld, self.act, self.dve
        cons, consf = D["cons"], D["consf"]

        def f(eng):
            eng.dma_start(out=cons[:, 0:128], in_=D["IDENT"][:, :]).then_inc(ld.h, 16)
            eng.dma_start(out=cons[:, 128:256], in_=D["MASKUSN"][:, :]).then_inc(ld.h, 16)
            eng.dma_start(out=cons[:, 256:384], in_=D["MASKUI"][:, :]).then_inc(ld.h, 16)
            eng.dma_start(out=cons[:, 384:640], in_=D["OH16"][:, :]).then_inc(ld.h, 16)
            eng.dma_start(out=cons[:, 640:641], in_=D["ONESCOL"][:, :]).then_inc(ld.h, 16)
            eng.dma_start(out=cons[0:1, 642:770], in_=D["ONESROW"][:, :]).then_inc(ld.h, 16)
            eng.dma_start(out=consf[:, 0:128], in_=D["convw"][:, :]).then_inc(ld.h, 16)
            eng.dma_start(out=consf[0:16, 128:129], in_=D["dtb"][:, :]).then_inc(ld.h, 16)
            eng.dma_start(out=consf[0:16, 129:130], in_=D["alog"][:, :]).then_inc(ld.h, 16)
            eng.dma_start(out=consf[:, 131:132], in_=D["nrmw"][:, :]).then_inc(ld.h, 16)
            eng.dma_start(out=consf[0:1, 132:260], in_=D["ONESROWF"][:, :]).then_inc(ld.h, 16)
            eng.dma_start(out=D["ones2"][:, :], in_=D["ONES2R"][:, :]).then_inc(ld.h, 16)
            eng.dma_start(out=D["identf"][:, :], in_=D["IDENTF"][:, :]).then_inc(ld.h, 16)
            eng.dma_start(out=D["onescf"][:, :], in_=D["ONESCOLF"][:, :]).then_inc(ld.h, 16)
        self.em("sync", f)
        setup_ld = ld.inc(16 * 14)

        def f(eng, w=setup_ld):
            eng.wait_ge(ld.h, w)
            eng.memset(consf[:, 262:263], 1e-6)
            eng.memset(consf[:, 263:264], 128.0 * 1e-6)
            eng.memset(consf[:, 264:265], LN128H)
            eng.memset(consf[:, 265:266], 1e-5)
            eng.memset(consf[:, 266:267], 1.0 / 2048.0)
            eng.memset(D["csplit"][0:32, :], 0.0)
            eng.memset(D["csplit"][32:64, :], 0.0)
            eng.memset(D["csplit"][64:66, :], 1.0)
            eng.memset(D["cbsplit"][0:32, :], 0.0)
            eng.memset(D["cbsplit"][32:64, :], 0.0)
            eng.memset(D["cbsplit"][64:66, :], -1.0).then_inc(dve.h, 1)
        self.em("vector", f)
        dve.inc(1)

        def f(eng, w=setup_ld):
            eng.wait_ge(ld.h, w)
            eng.activation(out=consf[0:16, 130:131], in_=consf[0:16, 129:130],
                           func=AF.Exp).then_inc(act.h, 1)
        self.em("scalar", f)
        ea = act.inc(1)

        def f(eng, w=ea, NEG=self.CF_NEGEA):
            eng.wait_ge(act.h, w)
            eng.tensor_scalar_mul(NEG, NEG, -1.0)
            eng.tensor_copy(consf[32:48, 260:261], consf[0:16, 128:129])
            eng.tensor_copy(consf[96:112, 261:262], NEG).then_inc(dve.h, 1)
        self.em("vector", f)
        dve.inc(1)

    def _phase_P(self):
        nc, D = self.nc, self.st_
        pe, dve, act, ld, cast, stm, sm = (self.pe, self.dve, self.act, self.ld,
                                           self.cast, self.stm, self.sm)
        pA = self.pA
        QT = 2048
        NQ = S // QT
        FG = 256
        fgroups = [(i * FG, FG) for i in range(24)] + [(6144, 16), (6160, 16)]

        with (
            nc.sbuf_tensor("x_q", [128, 16 * QT], BF16) as x_q,
            nc.sbuf_tensor("wwin", [128, 2 * 16 * FG], BF16) as wwin,
            nc.sbuf_tensor("stg", [128, 4 * 512], BF16) as stg,
            nc.sbuf_tensor("traF", [128, S], F32) as traF,
        ):
            stg_free = [None] * 4
            pA_free = [None] * 4
            wwin_use = [0, 0]
            xq_use = [0]
            rr = [0]

            for q in range(NQ):
                t_base = q * QT
                xv = D["xT"][:, t_base:t_base + QT].rearrange("(k p) t -> p k t", p=128)

                def f(eng, xv=xv, xw=xq_use[0]):
                    eng.wait_ge(pe.h, xw)
                    eng.dma_start(out=x_q[:, :].rearrange("p (k t) -> p k t", t=QT),
                                  in_=xv).then_inc(cast.h, 16)
                self.em("gpsimd", f)
                xq_done = cast.inc(16)

                for gi, (c0, cw) in enumerate(fgroups):
                    par = gi % 2
                    wv = D["wproj"][:, c0:c0 + cw].rearrange("(k p) f -> p k f", p=128)

                    def f(eng, wv=wv, par=par, cw=cw, wprev=wwin_use[par]):
                        eng.wait_ge(pe.h, wprev)
                        eng.dma_start(out=wwin[:, par * (16 * FG):par * (16 * FG) + 16 * cw
                                              ].rearrange("p (k f) -> p k f", f=cw),
                                      in_=wv).then_inc(cast.h, 16)
                    self.em("gpsimd", f)
                    wwin_done = cast.inc(16)

                    for fb in range(max(1, cw // 128)):
                        fbw = min(128, cw)
                        fcol = c0 + fb * 128
                        is_z = 4096 <= fcol < 6144
                        is_b = fcol == 6144
                        is_a = fcol == 6160
                        for tt in range(QT // 512):
                            bank = rr[0] % 4
                            sgi = rr[0] % 4
                            rr[0] += 1

                            def f(eng, bank=bank, par=par, fb=fb, fbw=fbw, tt=tt, cw=cw,
                                  wd=wwin_done, xd=xq_done, pf=pA_free[bank]):
                                eng.wait_ge(cast.h, max(wd, xd))
                                if pf is not None:
                                    eng.wait_ge(pf[0], pf[1])
                                base = par * (16 * FG)
                                ins = None
                                for k in range(16):
                                    ins = eng.matmul(
                                        pA[bank][0:fbw, :],
                                        wwin[:, base + k * cw + fb * 128:
                                             base + k * cw + fb * 128 + fbw],
                                        x_q[:, k * QT + tt * 512:k * QT + (tt + 1) * 512],
                                        start=(k == 0), stop=(k == 15))
                                ins.then_inc(pe.h, 1)
                            self.em("tensor", f)
                            mm_done = pe.inc(1)
                            wwin_use[par] = mm_done
                            xq_use[0] = mm_done

                            def f(eng, bank=bank, sgi=sgi, md=mm_done, sf=stg_free[sgi],
                                  is_z=is_z, is_b=is_b, is_a=is_a, fbw=fbw, tt=tt, t_base=t_base):
                                eng.wait_ge(pe.h, md)
                                if sf is not None and not (is_b or is_a):
                                    eng.wait_ge(sf[0], sf[1])
                                tsl = slice(t_base + tt * 512, t_base + (tt + 1) * 512)
                                sg = stg[:, sgi * 512:(sgi + 1) * 512]
                                if is_b:
                                    eng.copy(traF[0:16, tsl], pA[bank][0:16, :]).then_inc(act.h, 1)
                                elif is_a:
                                    eng.copy(traF[32:48, tsl], pA[bank][0:16, :]).then_inc(act.h, 1)
                                elif is_z:
                                    eng.activation(out=sg[0:fbw, :], in_=pA[bank][0:fbw, :],
                                                   func=AF.Silu).then_inc(act.h, 1)
                                else:
                                    eng.copy(sg[0:fbw, :], pA[bank][0:fbw, :]).then_inc(act.h, 1)
                            self.em("scalar", f)
                            ev = act.inc(1)
                            pA_free[bank] = (act.h, ev)

                            if not (is_b or is_a):
                                dst = (D["zsD"][fcol - 4096:fcol - 4096 + fbw,
                                                t_base + tt * 512:t_base + (tt + 1) * 512]
                                       if is_z else
                                       D["mixedD"][fcol:fcol + fbw,
                                                   t_base + tt * 512:t_base + (tt + 1) * 512])

                                def f(eng, dst=dst, sgi=sgi, ev=ev, fbw=fbw):
                                    eng.wait_ge(act.h, ev)
                                    eng.dma_start(out=dst,
                                                  in_=stg[:, sgi * 512:(sgi + 1) * 512][0:fbw, :]
                                                  ).then_inc(stm.h, 16)
                                self.em("sync", f)
                                stg_free[sgi] = (stm.h, stm.inc(16))

            persF, csplit, cbsplit = D["persF"], D["csplit"], D["cbsplit"]

            def f(eng, w=act.n, DTB32=self.CF_DTB32):
                eng.wait_ge(act.h, w)
                b_ = traF[0:16, :]
                a_ = traF[32:48, :]
                eng.activation(out=persF[32:48, :], in_=b_, func=AF.Sigmoid)
                eng.activation(out=persF[64:80, :], in_=b_, func=AF.Abs)
                eng.activation(out=persF[64:80, :], in_=persF[64:80, :], func=AF.Exp, scale=-1.0)
                eng.activation(out=persF[64:80, :], in_=persF[64:80, :], func=AF.Ln, bias=1.0)
                eng.activation(out=persF[96:112, :], in_=a_, func=AF.Abs, bias=DTB32)
                eng.activation(out=persF[96:112, :], in_=persF[96:112, :], func=AF.Exp, scale=-1.0)
                eng.activation(out=persF[96:112, :], in_=persF[96:112, :], func=AF.Ln, bias=1.0
                               ).then_inc(act.h, 1)
            self.em("scalar", f)
            sp_done = act.inc(1)

            def f(eng, w=sp_done, DTB32=self.CF_DTB32, NEG96=self.CF_NEGEA96):
                eng.wait_ge(act.h, w)
                b_ = traF[0:16, :]
                a_ = traF[32:48, :]
                eng.tensor_scalar(traF[64:80, :], b_, -1.0, 0.0, ALU.mult, ALU.max)
                eng.scalar_tensor_tensor(persF[64:80, :], traF[64:80, :], -1.0, persF[64:80, :],
                                         ALU.mult, ALU.subtract)
                eng.tensor_scalar(traF[96:112, :], a_, DTB32, 0.0, ALU.add, ALU.max)
                eng.tensor_add(persF[96:112, :], traF[96:112, :], persF[96:112, :])
                eng.tensor_scalar_mul(persF[96:112, :], persF[96:112, :], NEG96)
                eng.memset(traF[96:112, 0:CH], 1.0)
                for cix in range(NCHUNK):
                    sl = slice(cix * CH, (cix + 1) * CH)
                    eng.tensor_tensor_scan(persF[0:16, sl], traF[96:112, 0:CH],
                                           persF[96:112, sl], 0.0, ALU.mult, ALU.add)
                eng.tensor_copy(traF[64:80, :], persF[0:16, :])
                eng.tensor_add(traF[64:80, :], traF[64:80, :], persF[64:80, :])
                eng.tensor_copy(csplit[0:16, :], persF[0:16, :])
                eng.tensor_copy(cbsplit[0:16, :], traF[64:80, :])
                eng.tensor_copy(traF[0:16, :], csplit[0:16, :])
                eng.tensor_sub(csplit[32:48, :], persF[0:16, :], traF[0:16, :])
                eng.tensor_copy(persF[64:80, :], cbsplit[0:16, :])
                eng.tensor_sub(cbsplit[32:48, :], traF[64:80, :], persF[64:80, :]
                               ).then_inc(dve.h, 1)
            self.em("vector", f)
            split_done = dve.inc(1)

            def f(eng, w=split_done):
                eng.wait_ge(dve.h, w)
                eng.dma_start(out=D["csplitD"][:, :], in_=D["csplit"][0:48, :]).then_inc(sm.h, 16)
                eng.dma_start(out=D["cbsplitD"][:, :], in_=D["cbsplit"][0:48, :]).then_inc(sm.h, 16)
                eng.dma_start(out=D["cD"][:, :], in_=persF[0:16, :]).then_inc(sm.h, 16)
            self.em("scalar", f)
            cstores = sm.inc(48)

            if "smallsD" in self.debug_taps:
                def f(eng, a=dve.n, b=act.n):
                    eng.wait_ge(dve.h, a)
                    eng.wait_ge(act.h, b)
                    eng.dma_start(out=D["smallsD"][:, :], in_=persF[:, :]).then_inc(stm.h, 16)
                self.em("sync", f)
                stm.inc(16)

            self.barrier()

    def _phase_S(self):
        nc, D = self.nc, self.st_
        pe, dve, act, pool, ld, cast, stm, sm = (self.pe, self.dve, self.act, self.pool,
                                                 self.ld, self.cast, self.stm, self.sm)
        pA, pB, pT = self.pA, self.pB, self.pT
        persF = D["persF"]

        with contextlib.ExitStack() as ess:
            for nm, sh, dt in (
                ("rhsT", [66, 4096], FP16), ("gerow", [1, 1024], F32),
                ("state", [128, NH * DV], F32), ("stbf", [128, NH * DV], BF16),
            ):
                D[nm] = ess.enter_context(nc.sbuf_tensor(nm, sh, dt))
            sc = {}
            for nm, sh, dt in (
                ("mix", [128, 32 * MP], BF16), ("qkvc", [128, 32 * SLAB], BF16),
                ("zs_s", [128, 16 * CH], BF16),
                ("l2r", [16, SLAB], F32), ("rowb", [1, 4096], BF16),
                ("argA", [128, 2048], BF16), ("argQ", [128, 2048], BF16),
                ("AT", [128, 2048], BF16), ("AO", [128, 2048], BF16), ("AQ", [128, 2048], BF16),
                ("Ysb", [128, 16 * 256], BF16),
                ("Pw", [128, 2048], BF16), ("PwT", [128, 2048], BF16),
                ("Pw2", [128, 2048], BF16), ("PwT2", [128, 2048], BF16),
                ("KW", [128, 2048], BF16), ("W2T", [128, 256], BF16), ("Usb", [128, 256], BF16),
                ("ogsb", [128, 2048], BF16), ("og2", [128, 2048], BF16),
                ("stage", [128, 128], F32), ("colsb", [128, 128], F32),
                ("gqb", [128, 2048], BF16), ("gendb", [128, 16], F32),
                ("onstg", [128, 2 * 512], BF16),
                ("cvt", [128, 2 * SLAB], F32),
                ("cflat2", [2, 2048], FP16),
                ("scr16", [16, 16], F32),
            ):
                sc[nm] = ess.enter_context(nc.sbuf_tensor(nm, sh, dt))
            self.sc = sc

            sm = self.sm

            def f(eng):
                eng.dma_start(out=D["rhsT"][0:64, :], in_=D["INDT"][:, :]).then_inc(ld.h, 16)
                srcv = D["cD"][0:16, :].rearrange("h (c t) -> h c t", t=CH)[:, :, CH - 1:CH]
                srcv = srcv.squeeze(2).transpose([1, 0]).unsqueeze(0)
                outv = D["gerow"][0:1, 0:1024].rearrange("one (c h) -> one c h", h=16)
                eng.dma_start(out=outv, in_=srcv,
                              allow_slow_non_contiguous=True).then_inc(ld.h, 16)
            self.em("sync", f)
            sinit_ld = ld.inc(32)

            def f(eng, w=sinit_ld):
                eng.wait_ge(ld.h, w)
                eng.activation(out=D["gerow"][0:1, 0:1024], in_=D["gerow"][0:1, 0:1024],
                               func=AF.Exp).then_inc(act.h, 1)
            self.em("scalar", f)
            ge_exp = act.inc(1)

            def f(eng, w=ge_exp):
                eng.wait_ge(act.h, w)
                eng.memset(D["state"][:, :], 0.0)
                eng.memset(sc["stage"][:, :], 0.0).then_inc(dve.h, 1)
            self.em("vector", f)
            dve.inc(1)

            def f(eng, w=dve.n):
                eng.wait_ge(dve.h, w)
                eng.tensor_copy(D["stbf"][:, :], D["state"][:, :]).then_inc(pool.h, 1)
            self.em("gpsimd", f)
            pool.inc(1)

            prev_slab_pe = [0]
            prev = {"argmm": 0, "stagemm": 0, "onst": [None, None]}

            for sb in range(NSLAB):
                t0 = sb * SLAB
                mixv = sc["mix"][:, :].rearrange("p (b c) -> p b c", c=MP)
                if sb == 0:
                    def f(eng):
                        eng.memset(sc["mix"][:, :].rearrange("p (b c) -> p b c", c=MP)[:, :, 0:8],
                                   0.0).then_inc(pool.h, 1)
                    self.em("gpsimd", f)
                else:
                    def f(eng, dv=dve.n, pl=pool.n):
                        eng.wait_ge(dve.h, dv)
                        v = sc["mix"][:, :].rearrange("p (b c) -> p b c", c=MP)
                        eng.tensor_copy(v[:, :, 0:8], v[:, :, SLAB:SLAB + 8]).then_inc(pool.h, 1)
                    self.em("gpsimd", f)
                halo_done = pool.inc(1)

                mv = D["mixedD"][:, t0:t0 + SLAB].rearrange("(b p) t -> p b t", p=128)

                def f(eng, mv=mv, hd=halo_done, dv=dve.n, pl=pool.n, pp=prev_slab_pe[0]):
                    eng.wait_ge(pool.h, hd)
                    eng.wait_ge(dve.h, dv)
                    eng.wait_ge(pe.h, pp)
                    eng.dma_start(out=sc["mix"][:, :].rearrange("p (b c) -> p b c", c=MP)[:, :, 8:8 + SLAB],
                                  in_=mv).then_inc(ld.h, 16)
                self.em("sync", f)
                mix_rdy = ld.inc(16)

                act_base = act.n
                for cb in range(32):
                    cpar = cb % 2

                    def f(eng, cb=cb, cpar=cpar, mr=mix_rdy, pw=prev_slab_pe[0],
                          CF=self.CF_CONV, wact=(act.n - 1 if cb >= 2 else 0)):
                        if cb == 0:
                            eng.wait_ge(ld.h, mr)
                            eng.wait_ge(pe.h, pw)
                        if cb >= 2:
                            eng.wait_ge(act.h, wact)
                        v = sc["mix"][:, :].rearrange("p (b c) -> p b c", c=MP)
                        dst = sc["cvt"][:, cpar * SLAB:(cpar + 1) * SLAB]
                        eng.tensor_scalar_mul(dst, v[:, cb, 8:8 + SLAB], CF(cb, 3))
                        for j in (2, 1, 0):
                            ins = eng.scalar_tensor_tensor(
                                dst, v[:, cb, 5 + j:5 + j + SLAB], CF(cb, j), dst,
                                ALU.mult, ALU.add)
                        ins.then_inc(dve.h, 1)
                    self.em("vector", f)
                    cdone = dve.inc(1)

                    def f(eng, cb=cb, cpar=cpar, w=cdone):
                        eng.wait_ge(dve.h, w)
                        eng.activation(out=sc["qkvc"][:, cb * SLAB:(cb + 1) * SLAB],
                                       in_=sc["cvt"][:, cpar * SLAB:(cpar + 1) * SLAB],
                                       func=AF.Silu).then_inc(act.h, 1)
                    self.em("scalar", f)
                    act.inc(1)
                silu_done = act.n

                def f(eng, w=silu_done):
                    eng.wait_ge(act.h, w)
                    for cb in range(16):
                        eng.activation(out=sc["og2"][:, (cb % 2) * SLAB:(cb % 2 + 1) * SLAB],
                                       in_=sc["qkvc"][:, cb * SLAB:(cb + 1) * SLAB],
                                       func=AF.Square).then_inc(act.h, 1)
                self.em("scalar", f)
                sq_base = act.n
                act.inc(16)

                def f(eng, base=sq_base, pf=self.pfree.get("pB0"), OH=self.C_OH):
                    if pf is not None:
                        eng.wait_ge(pf[0], pf[1])
                    ins = None
                    for cb in range(16):
                        eng.wait_ge(act.h, base + cb + 1)
                        ins = eng.matmul(pB[0][0:16, 0:SLAB], OH(cb),
                                         sc["og2"][:, (cb % 2) * SLAB:(cb % 2 + 1) * SLAB],
                                         start=(cb == 0), stop=(cb == 15))
                    ins.then_inc(pe.h, 1)
                self.em("tensor", f)
                l2mm = pe.inc(1)

                def f(eng, w=l2mm):
                    eng.wait_ge(pe.h, w)
                    eng.activation(out=sc["l2r"][0:16, :], in_=pB[0][0:16, 0:SLAB],
                                   func=AF.Ln, bias=self.CF_EPS6[0:16, :])
                    eng.activation(out=sc["l2r"][0:16, :], in_=sc["l2r"][0:16, :],
                                   func=AF.Exp, scale=-0.5).then_inc(act.h, 1)
                self.em("scalar", f)
                rstd_done = act.inc(1)
                self.puse("pB0", act.h, rstd_done)

                def f(eng, w=rstd_done):
                    eng.wait_ge(act.h, w)
                    eng.tensor_scalar_mul(sc["l2r"][0:8, :], sc["l2r"][0:8, :], RSQ128
                                          ).then_inc(dve.h, 1)
                self.em("vector", f)
                rstds = dve.inc(1)

                def f(eng, w=rstds):
                    eng.wait_ge(dve.h, w)
                    eng.dma_start(out=D["l2rD"][:, :], in_=sc["l2r"][0:16, :]).then_inc(sm.h, 16)
                self.em("scalar", f)
                l2st = sm.inc(16)

                def f(eng, w=l2st, pw=pe.n):
                    eng.wait_ge(sm.h, w)
                    outv = sc["rowb"][0:1, :].rearrange("one (b t) -> one b t", t=SLAB)
                    eng.dma_start(out=outv, in_=D["l2rD"][0:16, :].unsqueeze(0)
                                  ).then_inc(self.cast.h, 16)
                self.em("gpsimd", f)
                l2fl = self.cast.inc(16)

                for cb in range(16):
                    def f(eng, cb=cb, w=l2fl, pf=self.pfree.get("pB0"), OR=self.C_OROW):
                        eng.wait_ge(self.cast.h, w)
                        if pf is not None:
                            eng.wait_ge(pf[0], pf[1])
                        eng.matmul(pB[0][:, (cb % 2) * 256:(cb % 2) * 256 + SLAB],
                                   OR, sc["rowb"][0:1, cb * SLAB:(cb + 1) * SLAB],
                                   start=True, stop=True).then_inc(pe.h, 1)
                    self.em("tensor", f)
                    bc = pe.inc(1)

                    def f(eng, cb=cb, w=bc):
                        eng.wait_ge(pe.h, w)
                        sl = sc["qkvc"][:, cb * SLAB:(cb + 1) * SLAB]
                        eng.tensor_mul(sl, sl, pB[0][:, (cb % 2) * 256:(cb % 2) * 256 + SLAB]
                                       ).then_inc(dve.h, 1)
                    self.em("vector", f)
                    self.puse("pB0", dve.h, dve.inc(1))
                l2_applied = dve.n

                for tc in range(SLAB // CH):
                    ci = sb * (SLAB // CH) + tc
                    self._chunk(ci, tc, l2_applied, prev)
                prev_slab_pe[0] = pe.n

            self.barrier()
        self._wout_pass()
        self.barrier()

    def _chunk(self, ci, tc, l2_applied, prev):
        D, sc = self.st_, self.sc
        pe, dve, act, pool, sm, ld = self.pe, self.dve, self.act, self.pool, self.sm, self.ld
        pA, pB, pT = self.pA, self.pB, self.pT
        c0 = ci * CH
        csplit, cbsplit, rhsT, persF = D["csplit"], D["cbsplit"], D["rhsT"], D["persF"]

        def qc(cb):
            return sc["qkvc"][:, cb * SLAB + tc * CH: cb * SLAB + tc * CH + CH]

        def zc(h):
            return sc["zs_s"][:, h * CH:(h + 1) * CH]

        def blkf(name, h):
            return sc[name][:, h * 128:(h + 1) * 128]

        zv = D["zsD"][:, c0:c0 + CH].rearrange("(b p) t -> p b t", p=128)

        def f(eng, zv=zv, dv=dve.n):
            eng.wait_ge(dve.h, dv)
            eng.dma_start(out=sc["zs_s"][:, :].rearrange("p (b t) -> p b t", t=CH),
                          in_=zv).then_inc(ld.h, 16)
        self.em("sync", f)
        zs_rdy = ld.inc(16)

        def f(eng, w=prev["argmm"], c0=c0):
            eng.wait_ge(pe.h, w)
            for (row, col, srcD, r0) in ((64, 0, "cbsplitD", 0), (65, 0, "cbsplitD", 32),
                                         (64, 2048, "csplitD", 0), (65, 2048, "csplitD", 32)):
                outv = rhsT[row:row + 1, col:col + 2048].rearrange(
                    "one (h t) -> one h t", t=CH)
                inv = D[srcD][r0:r0 + 16, c0:c0 + CH].unsqueeze(0)
                eng.dma_start(out=outv, in_=inv).then_inc(sm.h, 16)
            for (row, r0) in ((0, 0), (1, 32)):
                outv = sc["cflat2"][row:row + 1, :].rearrange("one (h t) -> one h t", t=CH)
                inv = D["csplitD"][r0:r0 + 16, c0:c0 + CH].unsqueeze(0)
                eng.dma_start(out=outv, in_=inv).then_inc(sm.h, 16)
        self.em("scalar", f)
        flats = sm.inc(96)

        argev = None
        for roff, dst in ((0, "argA"), (2048, "argQ")):
            for j in range(4):
                def f(eng, j=j, roff=roff, w=flats, c0=c0, pf=self.pfree.get("pA%d" % j)):
                    eng.wait_ge(sm.h, w)
                    if pf is not None:
                        eng.wait_ge(pf[0], pf[1])
                    eng.matmul(pA[j][:, :], csplit[0:66, c0:c0 + CH],
                               rhsT[0:66, roff + j * 512:roff + (j + 1) * 512],
                               start=True, stop=True).then_inc(pe.h, 1)
                self.em("tensor", f)
                mm = pe.inc(1)

                def f(eng, j=j, dst=dst, w=mm):
                    eng.wait_ge(pe.h, w)
                    eng.tensor_scalar(sc[dst][:, j * 512:(j + 1) * 512], pA[j][:, :],
                                      0.0, None, ALU.min).then_inc(dve.h, 1)
                self.em("vector", f)
                argev = dve.inc(1)
                self.puse("pA%d" % j, dve.h, argev)

        def f(eng, w=argev):
            eng.wait_ge(dve.h, w)
            eng.activation(out=sc["argA"][:, :], in_=sc["argA"][:, :], func=AF.Exp)
            eng.activation(out=sc["argQ"][:, :], in_=sc["argQ"][:, :], func=AF.Exp
                           ).then_inc(act.h, 1)
        self.em("scalar", f)
        exp_done = act.inc(1)

        def f(eng, c0=c0, w=prev["stagemm"]):
            eng.wait_ge(pe.h, w)
            eng.tensor_copy(sc["stage"][0:16, :], persF[32:48, c0:c0 + CH])
            eng.tensor_copy(sc["scr16"][0:16, 0:1], persF[0:16, c0 + CH - 1:c0 + CH]
                            ).then_inc(dve.h, 1)
        self.em("vector", f)
        stg1 = dve.inc(1)

        def f(eng, c0=c0, w=stg1):
            eng.wait_ge(dve.h, w)
            eng.activation(out=sc["stage"][32:48, :], in_=persF[0:16, c0:c0 + CH], func=AF.Exp)
            eng.activation(out=sc["stage"][64:80, :], in_=persF[0:16, c0:c0 + CH],
                           func=AF.Exp, scale=-1.0, bias=sc["scr16"][0:16, 0:1]
                           ).then_inc(act.h, 1)
        self.em("scalar", f)
        stg2 = act.inc(1)

        def f(eng, c0=c0, w=stg2):
            eng.wait_ge(act.h, w)
            eng.tensor_mul(sc["stage"][32:48, :], sc["stage"][32:48, :],
                           persF[32:48, c0:c0 + CH]).then_inc(dve.h, 1)
        self.em("vector", f)
        stg3 = dve.inc(1)

        def f(eng, w=stg3, pf=self.pfree.get("pB1"), IDF=self.C_IDF):
            eng.wait_ge(dve.h, w)
            if pf is not None:
                eng.wait_ge(pf[0], pf[1])
            eng.transpose(pB[1][:, 0:128], sc["stage"][:, :], IDF).then_inc(pe.h, 1)
        self.em("tensor", f)
        trs = pe.inc(1)
        prev["stagemm"] = trs

        def f(eng, ci=ci, ORF=self.CF_OROWF):
            eng.matmul(pB[1][:, 128:144], ORF, D["gerow"][0:1, ci * 16:(ci + 1) * 16],
                       start=True, stop=True).then_inc(pe.h, 1)
        self.em("tensor", f)
        gemm = pe.inc(1)

        def f(eng, w=gemm):
            eng.wait_ge(pe.h, w)
            eng.tensor_copy(sc["colsb"][:, :], pB[1][:, 0:128])
            eng.tensor_copy(sc["gendb"][:, :], pB[1][:, 128:144]).then_inc(dve.h, 1)
        self.em("vector", f)
        cols_done = dve.inc(1)
        self.puse("pB1", dve.h, cols_done)

        gexp = None
        for j in range(4):
            def f(eng, j=j, w=flats, pf=self.pfree.get("pA%d" % j), O2=D["ones2"]):
                eng.wait_ge(sm.h, w)
                if pf is not None:
                    eng.wait_ge(pf[0], pf[1])
                eng.matmul(pA[j][:, :], O2[0:2, :],
                           sc["cflat2"][0:2, j * 512:(j + 1) * 512],
                           start=True, stop=True).then_inc(pe.h, 1)
            self.em("tensor", f)
            mm = pe.inc(1)

            def f(eng, j=j, w=mm):
                eng.wait_ge(pe.h, w)
                eng.activation(out=sc["og2"][:, j * 512:(j + 1) * 512], in_=pA[j][:, :],
                               func=AF.Exp).then_inc(act.h, 1)
            self.em("scalar", f)
            gexp = act.inc(1)
            self.puse("pA%d" % j, act.h, gexp)
        prev["argmm"] = pe.n

        def f(eng, w=gexp, w2=l2_applied):
            eng.wait_ge(act.h, w)
            eng.wait_ge(dve.h, w2)
            last = None
            for h in range(NH):
                last = eng.tensor_mul(blkf("gqb", h), qc(h // 2), blkf("og2", h))
            last.then_inc(pool.h, 1)
        self.em("gpsimd", f)
        gq_done_pool = pool.inc(1)

        def f(eng, w=dve.n):
            eng.wait_ge(dve.h, w)
            eng.tensor_copy(D["stbf"][:, :], D["state"][:, :]).then_inc(pool.h, 1)
        self.em("gpsimd", f)
        stbf_done = pool.inc(1)

        if ci == 0:
            self.dump("dbg_qkvc", sc["qkvc"][:, :], [128, 32 * SLAB], BF16)
            self.dump("dbg_argA", sc["argA"][:, :], [128, 2048], BF16)
            self.dump("dbg_argQ", sc["argQ"][:, :], [128, 2048], BF16)
            self.dump("dbg_cols", sc["colsb"][:, :], [128, 128], F32)
            self.dump("dbg_gend", sc["gendb"][:, :], [128, 16], F32)
            self.dump("dbg_gqb", sc["gqb"][:, :], [128, 2048], BF16)

        skk_ev = {}
        tev_map = {}
        for h in range(NH):
            par = h % 2
            kb, qb, vb = 8 + h // 2, h // 2, 16 + h

            def f(eng, par=par, kb=kb, qb=qb, w=l2_applied, pf=self.pfree.get("pB%d" % par)):
                eng.wait_ge(dve.h, w)
                if pf is not None:
                    eng.wait_ge(pf[0], pf[1])
                eng.matmul(pB[par][:, 0:128], qc(kb), qc(kb), start=True, stop=True)
                eng.matmul(pB[par][:, 128:256], qc(kb), qc(qb), start=True, stop=True
                           ).then_inc(pe.h, 1)
            self.em("tensor", f)
            skmm = pe.inc(1)

            def f(eng, h=h, par=par, w=skmm, we=exp_done, MU=self.C_MUSN, MQ=self.C_MUI):
                eng.wait_ge(pe.h, w)
                eng.wait_ge(act.h, we)
                eng.tensor_mul(blkf("AT", h), pB[par][:, 0:128],
                               sc["argA"][:, h * 128:(h + 1) * 128])
                eng.tensor_mul(blkf("AT", h), blkf("AT", h), MU)
                eng.tensor_mul(blkf("AQ", h), pB[par][:, 128:256],
                               sc["argQ"][:, h * 128:(h + 1) * 128])
                eng.tensor_mul(blkf("AQ", h), blkf("AQ", h), MQ).then_inc(dve.h, 1)
            self.em("vector", f)
            amul = dve.inc(1)
            self.puse("pB%d" % par, dve.h, amul)
            skk_ev[h] = amul

            def f(eng, h=h, par=par, kb=kb, vb=vb, w=amul,
                  pf=self.pfree.get("pT%d" % par), ID=self.C_ID):
                eng.wait_ge(dve.h, w)
                if pf is not None:
                    eng.wait_ge(pf[0], pf[1])
                eng.transpose(pT[par][:, 0:128], blkf("AT", h), ID)
                eng.transpose(pT[par][:, 128:256], qc(kb), ID)
                eng.transpose(pT[par][:, 256:384], qc(vb), ID).then_inc(pe.h, 1)
            self.em("tensor", f)
            trs = pe.inc(1)

            def f(eng, h=h, par=par, w=trs, wc=cols_done):
                eng.wait_ge(pe.h, w)
                eng.wait_ge(dve.h, wc)
                cb_ = sc["colsb"]
                eng.tensor_copy(blkf("AO", h), pT[par][:, 0:128])
                eng.tensor_scalar(sc["Ysb"][:, h * 256 + 128:h * 256 + 256],
                                  pT[par][:, 128:256], cb_[:, 32 + h:33 + h], -1.0,
                                  ALU.mult, ALU.mult)
                eng.tensor_scalar(blkf("KW", h), pT[par][:, 128:256],
                                  cb_[:, 64 + h:65 + h], None, ALU.mult)
                eng.tensor_scalar(sc["Ysb"][:, h * 256:h * 256 + 128],
                                  pT[par][:, 256:384], cb_[:, h:h + 1], None, ALU.mult
                                  ).then_inc(dve.h, 1)
            self.em("vector", f)
            tev = dve.inc(1)
            tev_map[h] = tev
            self.puse("pT%d" % par, dve.h, tev)

        if ci == 0:
            self.dump("dbg_AT", sc["AT"][:, :], [128, 2048], BF16)
            self.dump("dbg_AQ", sc["AQ"][:, :], [128, 2048], BF16)
            self.dump("dbg_AO", sc["AO"][:, :], [128, 2048], BF16)
            self.dump("dbg_KW", sc["KW"][:, :], [128, 2048], BF16)
            self.dump("dbg_Y0", sc["Ysb"][:, :], [128, 4096], BF16)

        psrc = ("AO", "AT")
        pwv = {h: (dve.h, tev_map[h]) for h in range(NH)}
        yv = {h: (dve.h, tev_map[h]) for h in range(NH)}
        for lvl in range(7):
            pdst = ("Pw", "PwT") if (lvl % 2 == 1) else ("Pw2", "PwT2")
            for h in range(NH):
                par = h % 2
                if lvl >= 1:
                    def f(eng, h=h, par=par, psrc=psrc, lvl=lvl, wv=pwv[h],
                          pf=self.pfree.get("pB%d" % par)):
                        eng.wait_ge(wv[0], wv[1])
                        if pf is not None:
                            eng.wait_ge(pf[0], pf[1])
                        ins = eng.matmul(pB[par][:, 0:128], blkf(psrc[0], h),
                                         blkf(psrc[1], h), start=True, stop=True)
                        if lvl < 6:
                            ins = eng.matmul(pB[par][:, 128:256], blkf(psrc[1], h),
                                             blkf(psrc[0], h), start=True, stop=True)
                        ins.then_inc(pe.h, 1)
                    self.em("tensor", f)
                    bmm = pe.inc(1)

                    def f(eng, h=h, par=par, pdst=pdst, lvl=lvl, w=bmm):
                        eng.wait_ge(pe.h, w)
                        ins = eng.tensor_copy(blkf(pdst[1], h), pB[par][:, 0:128])
                        if lvl < 6:
                            ins = eng.tensor_copy(blkf(pdst[0], h), pB[par][:, 128:256])
                        ins.then_inc(dve.h, 1)
                    self.em("vector", f)
                    bev = dve.inc(1)
                    pwv[h] = (dve.h, bev)
                    self.puse("pB%d" % par, dve.h, bev)
                    pw_t = pdst[1]
                else:
                    pw_t = "AT"

                def f(eng, h=h, par=par, pw_t=pw_t, wp=pwv[h], wy=yv[h],
                      pf=self.pfree.get("pA%d" % par), ID=self.C_ID):
                    eng.wait_ge(wp[0], wp[1])
                    eng.wait_ge(wy[0], wy[1])
                    if pf is not None:
                        eng.wait_ge(pf[0], pf[1])
                    ysl = sc["Ysb"][:, h * 256:(h + 1) * 256]
                    eng.matmul(pA[par][:, 0:256], ID, ysl, start=True, stop=False)
                    eng.matmul(pA[par][:, 0:256], blkf(pw_t, h), ysl,
                               start=False, stop=True).then_inc(pe.h, 1)
                self.em("tensor", f)
                amm = pe.inc(1)

                if h % 2 == 0:
                    def f(eng, h=h, par=par, w=amm):
                        eng.wait_ge(pe.h, w)
                        eng.tensor_copy(sc["Ysb"][:, h * 256:(h + 1) * 256],
                                        pA[par][:, 0:256]).then_inc(dve.h, 1)
                    self.em("vector", f)
                    yev = dve.inc(1)
                    yv[h] = (dve.h, yev)
                    self.puse("pA%d" % par, dve.h, yev)
                else:
                    def f(eng, h=h, par=par, w=amm):
                        eng.wait_ge(pe.h, w)
                        eng.copy(sc["Ysb"][:, h * 256:(h + 1) * 256],
                                 pA[par][:, 0:256]).then_inc(act.h, 1)
                    self.em("scalar", f)
                    yev = act.inc(1)
                    yv[h] = (act.h, yev)
                    self.puse("pA%d" % par, act.h, yev)
            if lvl >= 1:
                psrc = pdst
        yready = (dve.n, act.n)
        if ci == 0:
            self.dump("dbg_Y", sc["Ysb"][:, :], [128, 4096], BF16)

        for h in range(NH):
            par = h % 2

            def f(eng, h=h, par=par, wv=yready, pf=self.pfree.get("pT%d" % par), ID=self.C_ID):
                eng.wait_ge(dve.h, wv[0])
                eng.wait_ge(act.h, wv[1])
                if pf is not None:
                    eng.wait_ge(pf[0], pf[1])
                eng.transpose(pT[par][:, 384:512], sc["Ysb"][:, h * 256 + 128:h * 256 + 256],
                              ID).then_inc(pe.h, 1)
            self.em("tensor", f)
            w2mm = pe.inc(1)

            def f(eng, h=h, par=par, w=w2mm):
                eng.wait_ge(pe.h, w)
                eng.tensor_copy(sc["W2T"][:, par * 128:(par + 1) * 128],
                                pT[par][:, 384:512]).then_inc(dve.h, 1)
            self.em("vector", f)
            w2ev = dve.inc(1)
            self.puse("pT%d" % par, dve.h, w2ev)

            def f(eng, h=h, par=par, w=w2ev, ws=stbf_done, pf=self.pfree.get("pA2"),
                  ID=self.C_ID):
                eng.wait_ge(dve.h, w)
                eng.wait_ge(pool.h, ws)
                if pf is not None:
                    eng.wait_ge(pf[0], pf[1])
                eng.matmul(pA[2][:, par * 128:par * 128 + 128], ID,
                           sc["Ysb"][:, h * 256:h * 256 + 128], start=True, stop=False)
                eng.matmul(pA[2][:, par * 128:par * 128 + 128],
                           sc["W2T"][:, par * 128:(par + 1) * 128],
                           D["stbf"][:, h * 128:(h + 1) * 128], start=False, stop=True
                           ).then_inc(pe.h, 1)
            self.em("tensor", f)
            umm = pe.inc(1)

            def f(eng, h=h, par=par, w=umm):
                eng.wait_ge(pe.h, w)
                eng.tensor_copy(sc["Usb"][:, par * 128:(par + 1) * 128],
                                pA[2][:, par * 128:par * 128 + 128]).then_inc(dve.h, 1)
            self.em("vector", f)
            uev = dve.inc(1)
            self.puse("pA2", dve.h, uev)

            def f(eng, h=h, par=par, w=uev, wg=gq_done_pool, pf=self.pfree.get("pA3")):
                eng.wait_ge(dve.h, w)
                eng.wait_ge(pool.h, wg)
                if pf is not None:
                    eng.wait_ge(pf[0], pf[1])
                eng.matmul(pA[3][:, par * 128:par * 128 + 128],
                           D["stbf"][:, h * 128:(h + 1) * 128], blkf("gqb", h),
                           start=True, stop=False)
                eng.matmul(pA[3][:, par * 128:par * 128 + 128],
                           sc["Usb"][:, par * 128:(par + 1) * 128],
                           blkf("AQ", h), start=False, stop=True).then_inc(pe.h, 1)
            self.em("tensor", f)
            omm = pe.inc(1)

            def f(eng, h=h, par=par, w=omm, wz=zs_rdy):
                eng.wait_ge(pe.h, w)
                eng.wait_ge(ld.h, wz)
                eng.tensor_mul(blkf("ogsb", h), pA[3][:, par * 128:par * 128 + 128],
                               zc(h)).then_inc(dve.h, 1)
            self.em("vector", f)
            ogev = dve.inc(1)
            self.puse("pA3", dve.h, ogev)

            def f(eng, h=h, par=par, w=uev, pf=self.pfree.get("pB%d" % par)):
                eng.wait_ge(dve.h, w)
                if pf is not None:
                    eng.wait_ge(pf[0], pf[1])
                eng.matmul(pB[par][:, 0:128], blkf("KW", h),
                           sc["Usb"][:, par * 128:(par + 1) * 128],
                           start=True, stop=True).then_inc(pe.h, 1)
            self.em("tensor", f)
            smm = pe.inc(1)

            def f(eng, h=h, par=par, w=smm):
                eng.wait_ge(pe.h, w)
                eng.scalar_tensor_tensor(D["state"][:, h * 128:(h + 1) * 128],
                                         D["state"][:, h * 128:(h + 1) * 128],
                                         sc["gendb"][:, h:h + 1], pB[par][:, 0:128],
                                         ALU.mult, ALU.add).then_inc(dve.h, 1)
            self.em("vector", f)
            sev = dve.inc(1)
            self.puse("pB%d" % par, dve.h, sev)

        if ci == 0:
            self.dump("dbg_og", sc["ogsb"][:, :], [128, 2048], BF16)
            self.dump("dbg_state", D["state"][:, :], [128, 2048], F32)

        def f(eng, w=dve.n):
            eng.wait_ge(dve.h, w)
            eng.activation(out=sc["og2"][:, :], in_=sc["ogsb"][:, :], func=AF.Square
                           ).then_inc(act.h, 1)
        self.em("scalar", f)
        og2_done = act.inc(1)

        for bank in range(4):
            def f(eng, bank=bank, w=og2_done, pf=self.pfree.get("pA%d" % bank),
                  OC=self.C_OCOL):
                eng.wait_ge(act.h, w)
                if pf is not None:
                    eng.wait_ge(pf[0], pf[1])
                ins = None
                for hh in range(4):
                    h = bank * 4 + hh
                    ins = eng.matmul(pA[bank][0:1, hh * 128:(hh + 1) * 128], OC,
                                     blkf("og2", h), start=(hh == 0), stop=(hh == 3))
                ins.then_inc(pe.h, 1)
            self.em("tensor", f)
            summ = pe.inc(1)

            def f(eng, bank=bank, w=summ):
                eng.wait_ge(pe.h, w)
                eng.activation(out=sc["rowb"][0:1, bank * 512:(bank + 1) * 512],
                               in_=pA[bank][0:1, :], func=AF.Ln, bias=self.CF_RMSE[0:1, :])
                eng.activation(out=sc["rowb"][0:1, bank * 512:(bank + 1) * 512],
                               in_=sc["rowb"][0:1, bank * 512:(bank + 1) * 512],
                               func=AF.Exp, scale=-0.5, bias=self.CF_LN128H[0:1, :]
                               ).then_inc(act.h, 1)
            self.em("scalar", f)
            self.puse("pA%d" % bank, act.h, act.inc(1))
        rstdd = act.n

        for j in range(4):
            par = j % 2

            def f(eng, j=j, w=rstdd, pf=self.pfree.get("pA%d" % j), OR=self.C_OROW):
                eng.wait_ge(act.h, w)
                if pf is not None:
                    eng.wait_ge(pf[0], pf[1])
                eng.matmul(pA[j][:, 0:512], OR, sc["rowb"][0:1, j * 512:(j + 1) * 512],
                           start=True, stop=True).then_inc(pe.h, 1)
            self.em("tensor", f)
            bcmm = pe.inc(1)

            def f(eng, j=j, par=par, w=bcmm, NW=self.CF_NRMW, ost=prev["onst"][par]):
                eng.wait_ge(pe.h, w)
                if ost is not None:
                    eng.wait_ge(ost[0], ost[1])
                eng.scalar_tensor_tensor(sc["onstg"][:, par * 512:(par + 1) * 512],
                                         sc["ogsb"][:, j * 512:(j + 1) * 512], NW,
                                         pA[j][:, 0:512], ALU.mult, ALU.mult
                                         ).then_inc(dve.h, 1)
            self.em("vector", f)
            onev = dve.inc(1)
            self.puse("pA%d" % j, dve.h, onev)

            dst = D["onD"][j * 512:(j + 1) * 512, c0:c0 + CH].rearrange(
                "(hh p) t -> p hh t", p=128)

            def f(eng, par=par, w=onev, dst=dst):
                eng.wait_ge(dve.h, w)
                eng.dma_start(out=dst,
                              in_=sc["onstg"][:, par * 512:(par + 1) * 512
                                              ].rearrange("p (hh t) -> p hh t", t=CH)
                              ).then_inc(self.stm.h, 16)
            self.em("sync", f)
            prev["onst"][par] = (self.stm.h, self.stm.inc(16))

    def _wout_pass(self):
        nc, D = self.nc, self.st_
        pe, dve, act, cast, ld, stm = self.pe, self.dve, self.act, self.cast, self.ld, self.stm
        pA = self.pA
        with (
            nc.sbuf_tensor("wo_sb", [128, 16 * H], BF16) as wo_sb,
            nc.sbuf_tensor("on_t", [128, 2 * 16 * 512], BF16) as on_t,
            nc.sbuf_tensor("po_st", [128, 4 * 512], BF16) as po_st,
        ):
            wv = D["wout"][:, :].rearrange("(k p) hcol -> p k hcol", p=128)

            def f(eng, wv=wv):
                eng.dma_start(out=wo_sb[:, :].rearrange("p (k c) -> p k c", c=H),
                              in_=wv).then_inc(cast.h, 16)
            self.em("gpsimd", f)
            wo_done = cast.inc(16)

            prev_use = [0, 0]
            st_free = [None] * 4
            rr = [0]
            for tt in range(S // 512):
                tpar = tt % 2
                onv = D["onD"][:, tt * 512:(tt + 1) * 512].rearrange("(k p) t -> p k t", p=128)

                def f(eng, onv=onv, tpar=tpar, w=prev_use[tpar]):
                    eng.wait_ge(pe.h, w)
                    eng.dma_start(out=on_t[:, tpar * (16 * 512):(tpar + 1) * (16 * 512)
                                          ].rearrange("p (k t) -> p k t", t=512),
                                  in_=onv).then_inc(ld.h, 16)
                self.em("sync", f)
                on_done = ld.inc(16)

                half = tt // 8
                colbase = (tt % 8) * 512
                for hb in range(16):
                    bank = rr[0] % 4
                    sgi = rr[0] % 4
                    rr[0] += 1

                    def f(eng, hb=hb, tpar=tpar, bank=bank, w=on_done, ww=wo_done,
                          pf=self.pfree.get("pA%d" % bank)):
                        eng.wait_ge(ld.h, w)
                        eng.wait_ge(cast.h, ww)
                        if pf is not None:
                            eng.wait_ge(pf[0], pf[1])
                        ins = None
                        for k in range(16):
                            ins = eng.matmul(
                                pA[bank][:, :],
                                wo_sb[:, k * H + hb * 128:k * H + (hb + 1) * 128],
                                on_t[:, tpar * (16 * 512) + k * 512:
                                     tpar * (16 * 512) + (k + 1) * 512],
                                start=(k == 0), stop=(k == 15))
                        ins.then_inc(pe.h, 1)
                    self.em("tensor", f)
                    mm = pe.inc(1)
                    prev_use[tpar] = mm

                    def f(eng, bank=bank, sgi=sgi, w=mm, sf=st_free[sgi]):
                        eng.wait_ge(pe.h, w)
                        if sf is not None:
                            eng.wait_ge(sf[0], sf[1])
                        eng.copy(po_st[:, sgi * 512:(sgi + 1) * 512], pA[bank][:, :]
                                 ).then_inc(act.h, 1)
                    self.em("scalar", f)
                    ev = act.inc(1)
                    self.puse("pA%d" % bank, act.h, ev)

                    dst = D["rsIn"][half * H + hb * 128:half * H + (hb + 1) * 128,
                                    colbase:colbase + 512]

                    def f(eng, dst=dst, sgi=sgi, ev=ev):
                        eng.wait_ge(act.h, ev)
                        eng.dma_start(out=dst, in_=po_st[:, sgi * 512:(sgi + 1) * 512]
                                      ).then_inc(stm.h, 16)
                    self.em("sync", f)
                    st_free[sgi] = (stm.h, stm.inc(16))

    def _phase_D(self):
        nc, D = self.nc, self.st_
        pe, dve, act, pool, ld, cast, stm, sm, ccm = (
            self.pe, self.dve, self.act, self.pool, self.ld, self.cast, self.stm,
            self.sm, self.ccm)
        pA, pB, pT = self.pA, self.pB, self.pT

        def f(eng, w=stm.n):
            eng.wait_ge(stm.h, w)
            eng.collective_compute(
                "ReduceScatter", ALU.add, replica_groups=REPL,
                ins=[D["rsIn"][0:2 * H, :]], outs=[D["rsOut"][0:H, :]],
            ).then_inc(ccm.h)
        self.em("gpsimd", f)
        rs_done = ccm.inc(1)

        with contextlib.ExitStack() as esd:
            x1n = esd.enter_context(nc.sbuf_tensor("x1n", [128, 16 * TH], BF16))
            rs_t = esd.enter_context(nc.sbuf_tensor("rs_t", [128, 16 * 512], BF16))
            xf_t = esd.enter_context(nc.sbuf_tensor("xf_t", [128, 16 * 512], BF16))
            x1b = esd.enter_context(nc.sbuf_tensor("x1b", [128, 16 * 512], BF16))
            sq2 = rs_t
            rowst = esd.enter_context(nc.sbuf_tensor("rowst", [1, 4 * 512], F32))

            for tt in range(TH // 512):
                rsv = D["rsOut"][:, tt * 512:(tt + 1) * 512].rearrange("(k p) t -> p k t", p=128)
                xv = D["xTd"][:, tt * 512:(tt + 1) * 512].rearrange("(k p) t -> p k t", p=128)

                def f(eng, rsv=rsv, w=rs_done, wp=pool.n, wq=pe.n):
                    eng.wait_ge(ccm.h, w)
                    eng.wait_ge(pool.h, wp)
                    eng.wait_ge(pe.h, wq)
                    eng.dma_start(out=rs_t[:, :].rearrange("p (k t) -> p k t", t=512),
                                  in_=rsv).then_inc(ld.h, 16)
                self.em("sync", f)

                def f(eng, xv=xv, wp=pool.n):
                    eng.wait_ge(pool.h, wp)
                    eng.dma_start(out=xf_t[:, :].rearrange("p (k t) -> p k t", t=512),
                                  in_=xv).then_inc(cast.h, 16)
                self.em("gpsimd", f)
                cast.inc(16)
                tiles_rdy = ld.inc(16)
                xt_rdy = cast.n

                def f(eng, w=tiles_rdy, wd=dve.n):
                    eng.wait_ge(ld.h, w)
                    eng.wait_ge(dve.h, wd)
                    last = None
                    for k in range(16):
                        last = eng.tensor_add(x1b[:, k * 512:(k + 1) * 512],
                                              xf_t[:, k * 512:(k + 1) * 512],
                                              rs_t[:, k * 512:(k + 1) * 512])
                    last.then_inc(pool.h, 1)
                self.em("gpsimd", f)
                x1b_done = pool.inc(1)

                def f(eng, w=x1b_done, pf0=self.pfree.get("pB0"), OC=self.C_OCOL):
                    eng.wait_ge(pool.h, w)
                    if pf0 is not None:
                        eng.wait_ge(pf0[0], pf0[1])
                    ins = None
                    for k in range(16):
                        ins = eng.matmul(pB[0][0:1, 0:512], OC, x1b[:, k * 512:(k + 1) * 512],
                                         start=(k == 0), stop=(k == 15))
                    ins.then_inc(pe.h, 1)
                self.em("tensor", f)
                summ = pe.inc(1)

                def f(eng, w=x1b_done, wp=pe.n):
                    eng.wait_ge(pool.h, w)
                    eng.wait_ge(pe.h, wp)
                    last = None
                    for k in range(16):
                        last = eng.activation(out=sq2[:, k * 512:(k + 1) * 512],
                                              in_=x1b[:, k * 512:(k + 1) * 512],
                                              func=AF.Square)
                    last.then_inc(act.h, 1)
                self.em("scalar", f)
                sqb = act.inc(1)

                def f(eng, w=sqb, pf1=self.pfree.get("pB1"), OC=self.C_OCOL):
                    eng.wait_ge(act.h, w)
                    if pf1 is not None:
                        eng.wait_ge(pf1[0], pf1[1])
                    ins = None
                    for k in range(16):
                        ins = eng.matmul(pB[1][0:1, 0:512], OC, sq2[:, k * 512:(k + 1) * 512],
                                         start=(k == 0), stop=(k == 15))
                    ins.then_inc(pe.h, 1)
                self.em("tensor", f)
                sq_mm = pe.inc(1)

                def f(eng, a=summ, b=sq_mm):
                    eng.wait_ge(pe.h, max(a, b))
                    eng.activation(out=rowst[0:1, 0:512], in_=pB[0][0:1, 0:512],
                                   func=AF.Copy, scale=self.CF_I2048[0:1, :])
                    eng.activation(out=rowst[0:1, 512:1024], in_=rowst[0:1, 0:512],
                                   func=AF.Square).then_inc(act.h, 1)
                self.em("scalar", f)
                mrow = act.inc(1)

                def f(eng, w=mrow):
                    eng.wait_ge(act.h, w)
                    eng.scalar_tensor_tensor(rowst[0:1, 1024:1536], pB[1][0:1, 0:512],
                                             1.0 / 2048.0, rowst[0:1, 512:1024],
                                             ALU.mult, ALU.subtract).then_inc(dve.h, 1)
                self.em("vector", f)
                varr = dve.inc(1)
                self.puse("pB1", dve.h, varr)

                def f(eng, w=varr):
                    eng.wait_ge(dve.h, w)
                    eng.activation(out=rowst[0:1, 1536:2048], in_=rowst[0:1, 1024:1536],
                                   func=AF.Ln, bias=self.CF_EPS5[0:1, :])
                    eng.activation(out=rowst[0:1, 1536:2048], in_=rowst[0:1, 1536:2048],
                                   func=AF.Exp, scale=-0.5).then_inc(act.h, 1)
                self.em("scalar", f)
                rrow = act.inc(1)

                def f(eng, w=rrow, ORF=self.CF_OROWF):
                    eng.wait_ge(act.h, w)
                    eng.matmul(pB[0][:, 0:512], ORF, rowst[0:1, 0:512], start=True, stop=True)
                    eng.matmul(pB[1][:, 0:512], ORF, rowst[0:1, 1536:2048], start=True,
                               stop=True).then_inc(pe.h, 1)
                self.em("tensor", f)
                bcs = pe.inc(1)

                def f(eng, tt=tt, w=bcs):
                    eng.wait_ge(pe.h, w)
                    last = None
                    for k in range(16):
                        dst = x1n[:, k * TH + tt * 512:k * TH + (tt + 1) * 512]
                        last = eng.tensor_sub(dst, x1b[:, k * 512:(k + 1) * 512],
                                              pB[0][:, 0:512])
                        last = eng.tensor_mul(dst, dst, pB[1][:, 0:512])
                    last.then_inc(dve.h, 1)
                self.em("vector", f)
                x1n_done = dve.inc(1)
                self.puse("pB0", dve.h, x1n_done)
                self.puse("pB1", dve.h, x1n_done)

                xnv = D["x1nD"][:, tt * 512:(tt + 1) * 512].rearrange("(k p) t -> p k t", p=128)

                def f(eng, tt=tt, xnv=xnv, w=x1n_done):
                    eng.wait_ge(dve.h, w)
                    eng.dma_start(out=xnv,
                                  in_=x1n[:, :].rearrange("p (k t2) -> p k t2", t2=TH)
                                  [:, :, tt * 512:(tt + 1) * 512]).then_inc(stm.h, 16)
                self.em("sync", f)
                stm.inc(16)

            with (
                nc.sbuf_tensor("guw", [128, 2 * 16 * 256], BF16) as guw,
                nc.sbuf_tensor("sgst", [128, 512], BF16) as sgst,
                nc.sbuf_tensor("hstg", [128, 2 * 512], BF16) as hstg,
            ):
                guw_use = [0, 0]
                sg_use = [None]
                hst_use = [None, None]
                for pb in range(INTER // 128):
                    par = pb % 2
                    wv = D["wgu"][:, pb * 256:(pb + 1) * 256].rearrange(
                        "(k p) c -> p k c", p=128)

                    def f(eng, wv=wv, par=par, w=guw_use[par]):
                        eng.wait_ge(pe.h, w)
                        eng.dma_start(out=guw[:, par * (16 * 256):(par + 1) * (16 * 256)
                                              ].rearrange("p (k c) -> p k c", c=256),
                                      in_=wv).then_inc(cast.h, 16)
                    self.em("gpsimd", f)
                    gw_done = cast.inc(16)

                    for tt in range(TH // 512):
                        def f(eng, par=par, tt=tt, w=gw_done,
                              pf0=self.pfree.get("pA0"), pf1=self.pfree.get("pA1")):
                            eng.wait_ge(cast.h, w)
                            if pf0 is not None:
                                eng.wait_ge(pf0[0], pf0[1])
                            if pf1 is not None:
                                eng.wait_ge(pf1[0], pf1[1])
                            base = par * (16 * 256)
                            ins = None
                            for k in range(16):
                                eng.matmul(pA[0][:, :],
                                           guw[:, base + k * 256:base + k * 256 + 128],
                                           x1n[:, k * TH + tt * 512:k * TH + (tt + 1) * 512],
                                           start=(k == 0), stop=(k == 15))
                            for k in range(16):
                                ins = eng.matmul(
                                    pA[1][:, :],
                                    guw[:, base + k * 256 + 128:base + k * 256 + 256],
                                    x1n[:, k * TH + tt * 512:k * TH + (tt + 1) * 512],
                                    start=(k == 0), stop=(k == 15))
                            ins.then_inc(pe.h, 1)
                        self.em("tensor", f)
                        gmm = pe.inc(1)
                        guw_use[par] = gmm

                        def f(eng, w=gmm, sg=sg_use[0]):
                            eng.wait_ge(pe.h, w)
                            if sg is not None:
                                eng.wait_ge(sg[0], sg[1])
                            eng.activation(out=sgst[:, :], in_=pA[0][:, :], func=AF.Silu
                                           ).then_inc(act.h, 1)
                        self.em("scalar", f)
                        sil = act.inc(1)
                        self.puse("pA0", act.h, sil)

                        par2 = tt % 2

                        def f(eng, w=sil, par2=par2, hu=hst_use[par2]):
                            eng.wait_ge(act.h, w)
                            if hu is not None:
                                eng.wait_ge(hu[0], hu[1])
                            eng.tensor_mul(hstg[:, par2 * 512:par2 * 512 + 512],
                                           sgst[:, :], pA[1][:, :]).then_inc(dve.h, 1)
                        self.em("vector", f)
                        hmul = dve.inc(1)
                        self.puse("pA1", dve.h, hmul)
                        sg_use[0] = (dve.h, hmul)

                        def f(eng, pb=pb, tt=tt, par2=par2, w=hmul):
                            eng.wait_ge(dve.h, w)
                            eng.dma_start(out=D["hD"][pb * 128:(pb + 1) * 128,
                                                      tt * 512:(tt + 1) * 512],
                                          in_=hstg[:, par2 * 512:par2 * 512 + 512]
                                          ).then_inc(stm.h, 16)
                        self.em("sync", f)
                        hst_use[par2] = (stm.h, stm.inc(16))

        self.barrier()

        with (
            nc.sbuf_tensor("dw", [128, 64 * 512], BF16) as dw,
            nc.sbuf_tensor("hp", [128, 2 * 8 * 512], BF16) as hp,
            nc.sbuf_tensor("dnst", [128, 4 * 512], F32) as dnst,
        ):
            dn_st = [None] * 4
            for qd in range(4):
                wv = D["wdn"][:, qd * 512:(qd + 1) * 512].rearrange("(k p) c -> p k c", p=128)

                def f(eng, wv=wv, w=pe.n):
                    eng.wait_ge(pe.h, w)
                    eng.dma_start(out=dw[:, :].rearrange("p (k c) -> p k c", c=512),
                                  in_=wv).then_inc(cast.h, 16)
                self.em("gpsimd", f)
                dw_done = cast.inc(16)

                for tt in range(TH // 512):
                    hp_done = []
                    for pc in range(8):
                        par = pc % 2
                        hv = D["hD"][pc * 1024:(pc + 1) * 1024, tt * 512:(tt + 1) * 512
                                     ].rearrange("(k p) t -> p k t", p=128)

                        def f(eng, hv=hv, par=par, w=pe.n):
                            eng.wait_ge(pe.h, w)
                            eng.dma_start(out=hp[:, par * (8 * 512):(par + 1) * (8 * 512)
                                                  ].rearrange("p (k t) -> p k t", t=512),
                                          in_=hv).then_inc(ld.h, 16)
                        self.em("sync", f)
                        hp_done.append(ld.inc(16))

                        def f(eng, par=par, pc=pc, qd=qd, w=hp_done[-1], ww=dw_done,
                              pfs=tuple(self.pfree.get("pA%d" % b) for b in range(4))):
                            eng.wait_ge(ld.h, w)
                            eng.wait_ge(cast.h, ww)
                            if pc == 0:
                                for pf in pfs:
                                    if pf is not None:
                                        eng.wait_ge(pf[0], pf[1])
                            base = par * (8 * 512)
                            ins = None
                            for hb in range(4):
                                for kk in range(8):
                                    k = pc * 8 + kk
                                    ins = eng.matmul(
                                        pA[hb][:, :],
                                        dw[:, k * 512 + hb * 128:k * 512 + (hb + 1) * 128],
                                        hp[:, base + kk * 512:base + (kk + 1) * 512],
                                        start=(k == 0), stop=(k == 63))
                            ins.then_inc(pe.h, 1)
                        self.em("tensor", f)
                        pe.inc(1)
                    dn_mm = pe.n

                    for hb in range(4):
                        def f(eng, hb=hb, w=dn_mm, ds=dn_st[hb]):
                            eng.wait_ge(pe.h, w)
                            if ds is not None:
                                eng.wait_ge(ds[0], ds[1])
                            eng.copy(dnst[:, hb * 512:(hb + 1) * 512], pA[hb][:, :]
                                     ).then_inc(act.h, 1)
                        self.em("scalar", f)
                        ev = act.inc(1)
                        self.puse("pA%d" % hb, act.h, ev)

                        def f(eng, hb=hb, qd=qd, tt=tt, w=ev):
                            eng.wait_ge(act.h, w)
                            eng.dma_start(out=D["dnD"][qd * 512 + hb * 128:qd * 512 + (hb + 1) * 128,
                                                       tt * 512:(tt + 1) * 512],
                                          in_=dnst[:, hb * 512:(hb + 1) * 512]).then_inc(stm.h, 16)
                        self.em("sync", f)
                        dn_st[hb] = (stm.h, stm.inc(16))

        self.barrier()

        with (
            nc.sbuf_tensor("dn_t", [128, 16 * 512], F32) as dn_t,
            nc.sbuf_tensor("xn_t", [128, 16 * 512], BF16) as xn_t,
            nc.sbuf_tensor("x2f", [128, 16 * 512], F32) as x2f,
            nc.sbuf_tensor("sqf", [128, 16 * 512], F32) as sqf,
            nc.sbuf_tensor("rows2", [1, 4 * 512], F32) as rows2,
            nc.sbuf_tensor("ostg", [128, 16 * 512], F32) as ostg,
        ):
            for tt in range(TH // 512):
                dnv = D["dnD"][:, tt * 512:(tt + 1) * 512].rearrange("(k p) t -> p k t", p=128)
                xnv = D["x1nD"][:, tt * 512:(tt + 1) * 512].rearrange("(k p) t -> p k t", p=128)

                def f(eng, dnv=dnv, xnv=xnv, wp=pool.n, wpe=pe.n):
                    eng.wait_ge(pool.h, wp)
                    eng.dma_start(out=dn_t[:, :].rearrange("p (k t) -> p k t", t=512),
                                  in_=dnv).then_inc(ld.h, 16)
                    eng.dma_start(out=xn_t[:, :].rearrange("p (k t) -> p k t", t=512),
                                  in_=xnv).then_inc(ld.h, 16)
                self.em("sync", f)
                t_rdy = ld.inc(32)

                def f(eng, w=t_rdy, wd=dve.n):
                    eng.wait_ge(ld.h, w)
                    eng.wait_ge(dve.h, wd)
                    last = None
                    for k in range(16):
                        last = eng.tensor_add(x2f[:, k * 512:(k + 1) * 512],
                                              dn_t[:, k * 512:(k + 1) * 512],
                                              xn_t[:, k * 512:(k + 1) * 512])
                    last.then_inc(pool.h, 1)
                self.em("gpsimd", f)
                x2_done = pool.inc(1)

                def f(eng, w=x2_done, pf0=self.pfree.get("pB0"), OCF=self.C_OCOLF):
                    eng.wait_ge(pool.h, w)
                    if pf0 is not None:
                        eng.wait_ge(pf0[0], pf0[1])
                    ins = None
                    for k in range(16):
                        ins = eng.matmul(pB[0][0:1, 0:512], OCF, x2f[:, k * 512:(k + 1) * 512],
                                         start=(k == 0), stop=(k == 15))
                    ins.then_inc(pe.h, 1)
                self.em("tensor", f)
                summ = pe.inc(1)

                def f(eng, w=x2_done, wp=pe.n):
                    eng.wait_ge(pool.h, w)
                    eng.wait_ge(pe.h, wp)
                    last = None
                    for k in range(16):
                        last = eng.activation(out=sqf[:, k * 512:(k + 1) * 512],
                                              in_=x2f[:, k * 512:(k + 1) * 512],
                                              func=AF.Square)
                    last.then_inc(act.h, 1)
                self.em("scalar", f)
                sqb = act.inc(1)

                def f(eng, w=sqb, pf1=self.pfree.get("pB1"), OCF=self.C_OCOLF):
                    eng.wait_ge(act.h, w)
                    if pf1 is not None:
                        eng.wait_ge(pf1[0], pf1[1])
                    ins = None
                    for k in range(16):
                        ins = eng.matmul(pB[1][0:1, 0:512], OCF, sqf[:, k * 512:(k + 1) * 512],
                                         start=(k == 0), stop=(k == 15))
                    ins.then_inc(pe.h, 1)
                self.em("tensor", f)
                sq_mm = pe.inc(1)

                def f(eng, a=summ, b=sq_mm):
                    eng.wait_ge(pe.h, max(a, b))
                    eng.activation(out=rows2[0:1, 0:512], in_=pB[0][0:1, 0:512],
                                   func=AF.Copy, scale=self.CF_I2048[0:1, :])
                    eng.activation(out=rows2[0:1, 512:1024], in_=rows2[0:1, 0:512],
                                   func=AF.Square).then_inc(act.h, 1)
                self.em("scalar", f)
                mrow = act.inc(1)

                def f(eng, w=mrow):
                    eng.wait_ge(act.h, w)
                    eng.scalar_tensor_tensor(rows2[0:1, 1024:1536], pB[1][0:1, 0:512],
                                             1.0 / 2048.0, rows2[0:1, 512:1024],
                                             ALU.mult, ALU.subtract).then_inc(dve.h, 1)
                self.em("vector", f)
                varr = dve.inc(1)
                self.puse("pB1", dve.h, varr)

                def f(eng, w=varr):
                    eng.wait_ge(dve.h, w)
                    eng.activation(out=rows2[0:1, 1536:2048], in_=rows2[0:1, 1024:1536],
                                   func=AF.Ln, bias=self.CF_EPS5[0:1, :])
                    eng.activation(out=rows2[0:1, 1536:2048], in_=rows2[0:1, 1536:2048],
                                   func=AF.Exp, scale=-0.5).then_inc(act.h, 1)
                self.em("scalar", f)
                rrow = act.inc(1)

                def f(eng, w=rrow, ORF=self.CF_OROWF):
                    eng.wait_ge(act.h, w)
                    eng.matmul(pB[0][:, 0:512], ORF, rows2[0:1, 0:512], start=True, stop=True)
                    eng.matmul(pB[1][:, 0:512], ORF, rows2[0:1, 1536:2048], start=True,
                               stop=True).then_inc(pe.h, 1)
                self.em("tensor", f)
                bcs = pe.inc(1)

                def f(eng, w=bcs, ws=stm.n):
                    eng.wait_ge(pe.h, w)
                    eng.wait_ge(stm.h, ws)
                    last = None
                    for k in range(16):
                        dst = ostg[:, k * 512:(k + 1) * 512]
                        last = eng.tensor_sub(dst, x2f[:, k * 512:(k + 1) * 512], pB[0][:, 0:512])
                        last = eng.tensor_mul(dst, dst, pB[1][:, 0:512])
                    last.then_inc(dve.h, 1)
                self.em("vector", f)
                odone = dve.inc(1)
                self.puse("pB0", dve.h, odone)
                self.puse("pB1", dve.h, odone)

                ov = D["outD"][:, tt * 512:(tt + 1) * 512].rearrange("(k p) t -> p k t", p=128)

                def f(eng, ov=ov, w=odone):
                    eng.wait_ge(dve.h, w)
                    eng.dma_start(out=ov, in_=ostg[:, :].rearrange("p (k t) -> p k t", t=512)
                                  ).then_inc(stm.h, 16)
                self.em("sync", f)
                stm.inc(16)


def build_nc(debug_taps=(), stop_after=None):
    return KB(debug_taps, stop_after).build()


def kernel(**inputs):
    consts = _host_consts()
    in_maps = [_host_core_inputs(c, inputs, consts) for c in range(8)]
    nc = build_nc()
    res = run_bass_kernel_spmd(nc, in_maps, core_ids=list(range(8)))
    out = np.empty((4, S, H), np.float32)
    for c in range(8):
        b, hg = c // 2, c % 2
        out[b, hg * TH:(hg + 1) * TH, :] = np.asarray(res.results[c]["out"]).T
    return out


# version 03
# speedup vs baseline: 1.2621x; 1.2621x over previous
import contextlib
import numpy as np
import ml_dtypes
import concourse.bass as bass
import concourse.mybir as mybir
from concourse.bass_utils import run_bass_kernel_spmd

F32 = mybir.dt.float32
BF16 = mybir.dt.bfloat16
FP16 = mybir.dt.float16
AF = mybir.ActivationFunctionType
ALU = mybir.AluOpType
BFnp = ml_dtypes.bfloat16

S = 8192
H = 2048
NH = 16
DK = DV = 128
CH = 128
NCHUNK = S // CH
SLAB = 256
NSLAB = S // SLAB
PCOLS = 6176
INTER = 8192
TH = S // 2
MP = SLAB + 8

REPL = [[0, 1], [2, 3], [4, 5], [6, 7]]
LN128H = 0.5 * float(np.log(128.0))
RSQ128 = float(128.0 ** -0.5)


def _host_consts():
    ident = np.eye(128, dtype=BFnp)
    identf = np.eye(128, dtype=np.float32)
    m_us_neg = (-np.triu(np.ones((128, 128), np.float32), 1)).astype(BFnp)
    m_ui = np.triu(np.ones((128, 128), np.float32), 0).astype(BFnp)
    indt = np.zeros((64, 4096), np.float16)
    for h in range(16):
        for off in (0, 2048):
            indt[h, off + h * 128:off + (h + 1) * 128] = -1.0
            indt[32 + h, off + h * 128:off + (h + 1) * 128] = -1.0
    oh16 = np.zeros((128, 256), np.float32)
    for cb in range(16):
        oh16[:, cb * 16 + cb] = 1.0
    return {"IDENT": ident, "IDENTF": identf, "MASKUSN": m_us_neg, "MASKUI": m_ui,
            "INDT": indt, "OH16": oh16.astype(BFnp),
            "ONESROW": np.ones((1, 128), BFnp), "ONESCOL": np.ones((128, 1), BFnp),
            "ONESROWF": np.ones((1, 128), np.float32),
            "ONESCOLF": np.ones((128, 1), np.float32),
            "ONES2R": np.ones((2, 128), np.float16)}


def _host_core_inputs(core, inp, consts):
    b, hg = core // 2, core % 2
    w_qkv = np.asarray(inp["L0_attn_w_qkv"])
    w_z = np.asarray(inp["L0_attn_w_z"])
    w_b = np.asarray(inp["L0_attn_w_b"])
    w_a = np.asarray(inp["L0_attn_w_a"])
    m = dict(consts)
    xTfull = np.ascontiguousarray(np.asarray(inp["L0_attn_hidden_states"])[b].T)
    m["xT"] = xTfull
    m["xTd"] = np.ascontiguousarray(xTfull[:, hg * TH:(hg + 1) * TH])
    m["wproj"] = np.ascontiguousarray(np.concatenate([
        w_qkv[:, hg * 1024:(hg + 1) * 1024],
        w_qkv[:, 2048 + hg * 1024:2048 + (hg + 1) * 1024],
        w_qkv[:, 4096 + hg * 2048:4096 + (hg + 1) * 2048],
        w_z[:, hg * 2048:(hg + 1) * 2048],
        w_b[:, hg * 16:(hg + 1) * 16],
        w_a[:, hg * 16:(hg + 1) * 16]], axis=1))
    cw = np.asarray(inp["L0_attn_conv_w"])[:, 0, :]
    ch_rows = np.concatenate([
        cw[hg * 1024:(hg + 1) * 1024],
        cw[2048 + hg * 1024:2048 + (hg + 1) * 1024],
        cw[4096 + hg * 2048:4096 + (hg + 1) * 2048]], axis=0)
    m["convw"] = np.ascontiguousarray(
        ch_rows.reshape(32, 128, 4).transpose(1, 0, 2).reshape(128, 128))
    m["wout"] = np.ascontiguousarray(np.asarray(inp["L0_attn_w_out"])[hg * 2048:(hg + 1) * 2048, :])
    wgu_in = np.asarray(inp["L0_ffn_w_gate_up"])
    wg = wgu_in[:INTER].T
    wu = wgu_in[INTER:].T
    gu = np.empty((H, 2 * INTER), np.float32)
    for i in range(INTER // 128):
        gu[:, (2 * i) * 128:(2 * i + 1) * 128] = wg[:, i * 128:(i + 1) * 128]
        gu[:, (2 * i + 1) * 128:(2 * i + 2) * 128] = wu[:, i * 128:(i + 1) * 128]
    m["wgu"] = np.ascontiguousarray(gu)
    m["wdn"] = np.ascontiguousarray(np.asarray(inp["L0_ffn_w_down"]).T)
    m["dtb"] = np.ascontiguousarray(
        np.asarray(inp["L0_attn_dt_bias"]).astype(np.float32)[hg * 16:(hg + 1) * 16].reshape(16, 1))
    m["alog"] = np.ascontiguousarray(
        np.asarray(inp["L0_attn_A_log"]).astype(np.float32)[hg * 16:(hg + 1) * 16].reshape(16, 1))
    m["nrmw"] = np.ascontiguousarray(
        np.asarray(inp["L0_attn_norm_w"]).astype(np.float32).reshape(128, 1))
    return m


class Sem:
    def __init__(self, handle):
        self.h = handle
        self.n = 0

    def inc(self, k=1):
        self.n += k
        return self.n


class KB:
    def __init__(self, debug_taps=(), stop_after=None):
        self.debug_taps = set(debug_taps)
        self.stop_after = stop_after
        self.nc = bass.Bass("TRN2", target_bir_lowering=False, debug=False)
        self.Q = {e: [] for e in ("tensor", "vector", "scalar", "gpsimd", "sync")}
        self.st_ = {}
        self.pfree = {}

    def em(self, e, fn):
        self.Q[e].append(fn)

    def dram(self, name, shape, dtype):
        k = "ExternalOutput" if name in self.debug_taps else "Internal"
        return self.nc.dram_tensor(name, shape, dtype, kind=k).ap()

    def pw(self, eng, name):
        v = self.pfree.get(name)
        if v is not None:
            eng.wait_ge(v[0], v[1])

    def puse(self, name, sem, val):
        self.pfree[name] = (sem, val)

    def dump(self, name, ap, shape, dtype):
        if name not in self.debug_taps:
            return
        D = self.st_
        if name not in D:
            D[name] = self.nc.dram_tensor(name, shape, dtype, kind="ExternalOutput").ap()
        waits = [(s.h, s.n) for s in (self.pe, self.dve, self.act, self.pool) if s.n > 0]
        dst = D[name]

        def f(eng, waits=tuple(waits), dst=dst, ap=ap):
            for h, v in waits:
                eng.wait_ge(h, v)
            eng.dma_start(out=dst, in_=ap).then_inc(self.stm.h, 16)
        self.em("sync", f)
        self.stm.inc(16)

    def barrier(self):
        waits = [(s.h, s.n) for s in
                 (self.pe, self.dve, self.act, self.pool, self.ld, self.cast,
                  self.stm, self.sm, self.ccm) if s.n > 0]
        for e in self.Q:
            def f(eng, waits=tuple(waits)):
                for h, v in waits:
                    eng.wait_ge(h, v)
            self.em(e, f)
        self.pfree = {}

    def build(self):
        nc = self.nc
        with contextlib.ExitStack() as es:
            self._declare(es)
            with contextlib.ExitStack() as es_ps:
                self._declare_ps(es_ps)
                self._setup()
                self._phase_P()
                if self.stop_after == "P":
                    es_ps_hold = True
                else:
                    self._phase_S()
            if self.stop_after not in ("P", "S"):
                self._phase_D()

            stm, sm, cast, ld, ccm = self.stm, self.sm, self.cast, self.ld, self.ccm

            def f(eng, a=stm.n, b=sm.n, c=cast.n, d=ld.n, e=ccm.n):
                eng.wait_ge(stm.h, a)
                eng.wait_ge(sm.h, b)
                eng.wait_ge(cast.h, c)
                eng.wait_ge(ld.h, d)
                if e:
                    eng.wait_ge(ccm.h, e)
            self.em("sync", f)

            blk = es.enter_context(nc.Block())
            Q = self.Q

            @blk.tensor
            def _(eng):
                for fn in Q["tensor"]:
                    fn(eng)

            @blk.vector
            def _(eng):
                for fn in Q["vector"]:
                    fn(eng)

            @blk.scalar
            def _(eng):
                for fn in Q["scalar"]:
                    fn(eng)

            @blk.gpsimd
            def _(eng):
                for fn in Q["gpsimd"]:
                    fn(eng)

            @blk.sync
            def _(eng):
                for fn in Q["sync"]:
                    fn(eng)
        return nc

    def _declare(self, es):
        nc, D = self.nc, self.st_
        for nm, sh, dt in (
            ("xT", [H, S], F32), ("xTd", [H, TH], F32), ("wproj", [H, PCOLS], F32),
            ("convw", [128, 128], F32), ("wout", [H, H], F32), ("wgu", [H, 2 * INTER], F32),
            ("wdn", [INTER, H], F32), ("dtb", [16, 1], F32), ("alog", [16, 1], F32),
            ("nrmw", [128, 1], F32), ("IDENT", [128, 128], BF16), ("IDENTF", [128, 128], F32),
            ("MASKUSN", [128, 128], BF16), ("MASKUI", [128, 128], BF16),
            ("INDT", [64, 4096], FP16), ("OH16", [128, 256], BF16),
            ("ONESROW", [1, 128], BF16), ("ONESCOL", [128, 1], BF16),
            ("ONESROWF", [1, 128], F32), ("ONESCOLF", [128, 1], F32),
            ("ONES2R", [2, 128], FP16),
        ):
            D[nm] = nc.dram_tensor(nm, sh, dt, kind="ExternalInput").ap()
        D["mixedD"] = self.dram("mixedD", [4096, S], BF16)
        D["zsD"] = self.dram("zsD", [H, S], BF16)
        D["onD"] = self.dram("onD", [H, S], BF16)
        D["rsIn"] = self.dram("rsIn", [2 * H, TH], BF16)
        D["rsOut"] = self.dram("rsOut", [H, TH], BF16)
        D["hD"] = self.dram("hD", [INTER, TH], BF16)
        D["x1nD"] = self.dram("x1nD", [H, TH], BF16)
        D["dnD"] = self.dram("dnD", [H, TH], F32)
        D["smallsD"] = self.dram("smallsD", [128, S], F32)
        D["csplitD"] = self.dram("csplitD", [48, S], FP16)
        D["cbsplitD"] = self.dram("cbsplitD", [48, S], FP16)
        D["cD"] = self.dram("cD", [16, S], F32)
        D["l2rD"] = self.dram("l2rD", [16, SLAB], F32)
        D["outD"] = nc.dram_tensor("out", [H, TH], F32, kind="ExternalOutput").ap()

        for nm, sh, dt in (
            ("cons", [128, 1024], BF16), ("consf", [128, 272], F32),
            ("identf", [128, 128], F32), ("ones2", [2, 128], FP16),
            ("onescf", [128, 1], F32),
        ):
            D[nm] = es.enter_context(nc.sbuf_tensor(nm, sh, dt))
        for nm, sh, dt in (
            ("pA0", [128, 512], F32), ("pA1", [128, 512], F32),
            ("pA2", [128, 512], F32), ("pA3", [128, 512], F32),
            ("pB0", [128, 512], F32), ("pB1", [128, 512], F32),
            ("pT0", [128, 1024], BF16), ("pT1", [128, 1024], BF16),
        ):
            D[nm] = es.enter_context(nc.psum_tensor(nm, sh, dt))
        self.pA = [D["pA0"], D["pA1"], D["pA2"], D["pA3"]]
        self.pB = [D["pB0"], D["pB1"]]
        self.pT = [D["pT0"], D["pT1"]]
        for nm in ("pe", "dve", "act", "pool", "ld", "cast", "stm", "sm", "ccm"):
            setattr(self, nm, Sem(es.enter_context(nc.semaphore("s_" + nm))))
        cons, consf = D["cons"], D["consf"]
        self.C_ID = cons[:, 0:128]
        self.C_MUSN = cons[:, 128:256]
        self.C_MUI = cons[:, 256:384]
        self.C_OH = lambda cb: cons[:, 384 + cb * 16:384 + (cb + 1) * 16]
        self.C_OCOL = cons[:, 640:641]
        self.C_OROW = cons[0:1, 642:770]
        self.CF_CONV = lambda cb, j: consf[:, cb * 4 + j:cb * 4 + j + 1]
        self.CF_DTB = consf[0:16, 128:129]
        self.CF_ALOG = consf[0:16, 129:130]
        self.CF_NEGEA = consf[0:16, 130:131]
        self.CF_NRMW = consf[:, 131:132]
        self.CF_OROWF = consf[0:1, 132:260]
        self.CF_DTB32 = consf[32:48, 260:261]
        self.CF_NEGEA96 = consf[96:112, 261:262]
        self.CF_EPS6 = consf[:, 262:263]
        self.CF_RMSE = consf[:, 263:264]
        self.CF_LN128H = consf[:, 264:265]
        self.CF_EPS5 = consf[:, 265:266]
        self.CF_I2048 = consf[:, 266:267]
        self.C_IDF = D["identf"][:, :]
        self.C_OCOLF = D["onescf"][:, :]

    def _declare_ps(self, es_ps):
        nc, D = self.nc, self.st_
        for nm, sh, dt in (
            ("persF", [128, S], F32), ("csplit", [66, S], FP16), ("cbsplit", [66, S], FP16),
        ):
            D[nm] = es_ps.enter_context(nc.sbuf_tensor(nm, sh, dt))

    def _setup(self):
        D = self.st_
        ld, act, dve = self.ld, self.act, self.dve
        cons, consf = D["cons"], D["consf"]

        def f(eng):
            eng.dma_start(out=cons[:, 0:128], in_=D["IDENT"][:, :]).then_inc(ld.h, 16)
            eng.dma_start(out=cons[:, 128:256], in_=D["MASKUSN"][:, :]).then_inc(ld.h, 16)
            eng.dma_start(out=cons[:, 256:384], in_=D["MASKUI"][:, :]).then_inc(ld.h, 16)
            eng.dma_start(out=cons[:, 384:640], in_=D["OH16"][:, :]).then_inc(ld.h, 16)
            eng.dma_start(out=cons[:, 640:641], in_=D["ONESCOL"][:, :]).then_inc(ld.h, 16)
            eng.dma_start(out=cons[0:1, 642:770], in_=D["ONESROW"][:, :]).then_inc(ld.h, 16)
            eng.dma_start(out=consf[:, 0:128], in_=D["convw"][:, :]).then_inc(ld.h, 16)
            eng.dma_start(out=consf[0:16, 128:129], in_=D["dtb"][:, :]).then_inc(ld.h, 16)
            eng.dma_start(out=consf[0:16, 129:130], in_=D["alog"][:, :]).then_inc(ld.h, 16)
            eng.dma_start(out=consf[:, 131:132], in_=D["nrmw"][:, :]).then_inc(ld.h, 16)
            eng.dma_start(out=consf[0:1, 132:260], in_=D["ONESROWF"][:, :]).then_inc(ld.h, 16)
            eng.dma_start(out=D["ones2"][:, :], in_=D["ONES2R"][:, :]).then_inc(ld.h, 16)
            eng.dma_start(out=D["identf"][:, :], in_=D["IDENTF"][:, :]).then_inc(ld.h, 16)
            eng.dma_start(out=D["onescf"][:, :], in_=D["ONESCOLF"][:, :]).then_inc(ld.h, 16)
        self.em("sync", f)
        setup_ld = ld.inc(16 * 14)

        def f(eng, w=setup_ld):
            eng.wait_ge(ld.h, w)
            eng.memset(consf[:, 262:263], 1e-6)
            eng.memset(consf[:, 263:264], 128.0 * 1e-6)
            eng.memset(consf[:, 264:265], LN128H)
            eng.memset(consf[:, 265:266], 1e-5)
            eng.memset(consf[:, 266:267], 1.0 / 2048.0)
            eng.memset(D["csplit"][0:32, :], 0.0)
            eng.memset(D["csplit"][32:64, :], 0.0)
            eng.memset(D["csplit"][64:66, :], 1.0)
            eng.memset(D["cbsplit"][0:32, :], 0.0)
            eng.memset(D["cbsplit"][32:64, :], 0.0)
            eng.memset(D["cbsplit"][64:66, :], -1.0).then_inc(dve.h, 1)
        self.em("vector", f)
        dve.inc(1)

        def f(eng, w=setup_ld):
            eng.wait_ge(ld.h, w)
            eng.activation(out=consf[0:16, 130:131], in_=consf[0:16, 129:130],
                           func=AF.Exp).then_inc(act.h, 1)
        self.em("scalar", f)
        ea = act.inc(1)

        def f(eng, w=ea, NEG=self.CF_NEGEA):
            eng.wait_ge(act.h, w)
            eng.tensor_scalar_mul(NEG, NEG, -1.0)
            eng.tensor_copy(consf[32:48, 260:261], consf[0:16, 128:129])
            eng.tensor_copy(consf[96:112, 261:262], NEG).then_inc(dve.h, 1)
        self.em("vector", f)
        dve.inc(1)

    def _phase_P(self):
        nc, D = self.nc, self.st_
        pe, dve, act, ld, cast, stm, sm = (self.pe, self.dve, self.act, self.ld,
                                           self.cast, self.stm, self.sm)
        pA = self.pA
        QT = 2048
        NQ = S // QT
        FG = 256
        fgroups = [(i * FG, FG) for i in range(24)] + [(6144, 16), (6160, 16)]

        with (
            nc.sbuf_tensor("x_q", [128, 16 * QT], BF16) as x_q,
            nc.sbuf_tensor("wwin", [128, 2 * 16 * FG], BF16) as wwin,
            nc.sbuf_tensor("stg", [128, 4 * 512], BF16) as stg,
            nc.sbuf_tensor("traF", [128, S], F32) as traF,
        ):
            stg_free = [None] * 4
            pA_free = [None] * 4
            wwin_use = [0, 0]
            xq_use = [0]
            rr = [0]

            for q in range(NQ):
                t_base = q * QT
                xv = D["xT"][:, t_base:t_base + QT].rearrange("(k p) t -> p k t", p=128)

                def f(eng, xv=xv, xw=xq_use[0]):
                    eng.wait_ge(pe.h, xw)
                    eng.dma_start(out=x_q[:, :].rearrange("p (k t) -> p k t", t=QT),
                                  in_=xv).then_inc(cast.h, 16)
                self.em("gpsimd", f)
                xq_done = cast.inc(16)

                for gi, (c0, cw) in enumerate(fgroups):
                    par = gi % 2
                    wv = D["wproj"][:, c0:c0 + cw].rearrange("(k p) f -> p k f", p=128)

                    def f(eng, wv=wv, par=par, cw=cw, wprev=wwin_use[par]):
                        eng.wait_ge(pe.h, wprev)
                        eng.dma_start(out=wwin[:, par * (16 * FG):par * (16 * FG) + 16 * cw
                                              ].rearrange("p (k f) -> p k f", f=cw),
                                      in_=wv).then_inc(cast.h, 16)
                    self.em("gpsimd", f)
                    wwin_done = cast.inc(16)

                    for fb in range(max(1, cw // 128)):
                        fbw = min(128, cw)
                        fcol = c0 + fb * 128
                        is_z = 4096 <= fcol < 6144
                        is_b = fcol == 6144
                        is_a = fcol == 6160
                        for tt in range(QT // 512):
                            bank = rr[0] % 4
                            sgi = rr[0] % 4
                            rr[0] += 1

                            def f(eng, bank=bank, par=par, fb=fb, fbw=fbw, tt=tt, cw=cw,
                                  wd=wwin_done, xd=xq_done, pf=pA_free[bank]):
                                eng.wait_ge(cast.h, max(wd, xd))
                                if pf is not None:
                                    eng.wait_ge(pf[0], pf[1])
                                base = par * (16 * FG)
                                ins = None
                                for k in range(16):
                                    ins = eng.matmul(
                                        pA[bank][0:fbw, :],
                                        wwin[:, base + k * cw + fb * 128:
                                             base + k * cw + fb * 128 + fbw],
                                        x_q[:, k * QT + tt * 512:k * QT + (tt + 1) * 512],
                                        start=(k == 0), stop=(k == 15))
                                ins.then_inc(pe.h, 1)
                            self.em("tensor", f)
                            mm_done = pe.inc(1)
                            wwin_use[par] = mm_done
                            xq_use[0] = mm_done

                            def f(eng, bank=bank, sgi=sgi, md=mm_done, sf=stg_free[sgi],
                                  is_z=is_z, is_b=is_b, is_a=is_a, fbw=fbw, tt=tt, t_base=t_base):
                                eng.wait_ge(pe.h, md)
                                if sf is not None and not (is_b or is_a):
                                    eng.wait_ge(sf[0], sf[1])
                                tsl = slice(t_base + tt * 512, t_base + (tt + 1) * 512)
                                sg = stg[:, sgi * 512:(sgi + 1) * 512]
                                if is_b:
                                    eng.copy(traF[0:16, tsl], pA[bank][0:16, :]).then_inc(act.h, 1)
                                elif is_a:
                                    eng.copy(traF[32:48, tsl], pA[bank][0:16, :]).then_inc(act.h, 1)
                                elif is_z:
                                    eng.activation(out=sg[0:fbw, :], in_=pA[bank][0:fbw, :],
                                                   func=AF.Silu).then_inc(act.h, 1)
                                else:
                                    eng.copy(sg[0:fbw, :], pA[bank][0:fbw, :]).then_inc(act.h, 1)
                            self.em("scalar", f)
                            ev = act.inc(1)
                            pA_free[bank] = (act.h, ev)

                            if not (is_b or is_a):
                                dst = (D["zsD"][fcol - 4096:fcol - 4096 + fbw,
                                                t_base + tt * 512:t_base + (tt + 1) * 512]
                                       if is_z else
                                       D["mixedD"][fcol:fcol + fbw,
                                                   t_base + tt * 512:t_base + (tt + 1) * 512])

                                def f(eng, dst=dst, sgi=sgi, ev=ev, fbw=fbw):
                                    eng.wait_ge(act.h, ev)
                                    eng.dma_start(out=dst,
                                                  in_=stg[:, sgi * 512:(sgi + 1) * 512][0:fbw, :]
                                                  ).then_inc(stm.h, 16)
                                self.em("sync", f)
                                stg_free[sgi] = (stm.h, stm.inc(16))

            persF, csplit, cbsplit = D["persF"], D["csplit"], D["cbsplit"]

            def f(eng, w=act.n, DTB32=self.CF_DTB32):
                eng.wait_ge(act.h, w)
                b_ = traF[0:16, :]
                a_ = traF[32:48, :]
                eng.activation(out=persF[32:48, :], in_=b_, func=AF.Sigmoid)
                eng.activation(out=persF[64:80, :], in_=b_, func=AF.Abs)
                eng.activation(out=persF[64:80, :], in_=persF[64:80, :], func=AF.Exp, scale=-1.0)
                eng.activation(out=persF[64:80, :], in_=persF[64:80, :], func=AF.Ln, bias=1.0)
                eng.activation(out=persF[96:112, :], in_=a_, func=AF.Abs, bias=DTB32)
                eng.activation(out=persF[96:112, :], in_=persF[96:112, :], func=AF.Exp, scale=-1.0)
                eng.activation(out=persF[96:112, :], in_=persF[96:112, :], func=AF.Ln, bias=1.0
                               ).then_inc(act.h, 1)
            self.em("scalar", f)
            sp_done = act.inc(1)

            def f(eng, w=sp_done, DTB32=self.CF_DTB32, NEG96=self.CF_NEGEA96):
                eng.wait_ge(act.h, w)
                b_ = traF[0:16, :]
                a_ = traF[32:48, :]
                eng.tensor_scalar(traF[64:80, :], b_, -1.0, 0.0, ALU.mult, ALU.max)
                eng.scalar_tensor_tensor(persF[64:80, :], traF[64:80, :], -1.0, persF[64:80, :],
                                         ALU.mult, ALU.subtract)
                eng.tensor_scalar(traF[96:112, :], a_, DTB32, 0.0, ALU.add, ALU.max)
                eng.tensor_add(persF[96:112, :], traF[96:112, :], persF[96:112, :])
                eng.tensor_scalar_mul(persF[96:112, :], persF[96:112, :], NEG96)
                eng.memset(traF[96:112, 0:CH], 1.0)
                for cix in range(NCHUNK):
                    sl = slice(cix * CH, (cix + 1) * CH)
                    eng.tensor_tensor_scan(persF[0:16, sl], traF[96:112, 0:CH],
                                           persF[96:112, sl], 0.0, ALU.mult, ALU.add)
                eng.tensor_copy(traF[64:80, :], persF[0:16, :])
                eng.tensor_add(traF[64:80, :], traF[64:80, :], persF[64:80, :])
                eng.tensor_copy(csplit[0:16, :], persF[0:16, :])
                eng.tensor_copy(cbsplit[0:16, :], traF[64:80, :])
                eng.tensor_copy(traF[0:16, :], csplit[0:16, :])
                eng.tensor_sub(csplit[32:48, :], persF[0:16, :], traF[0:16, :])
                eng.tensor_copy(persF[64:80, :], cbsplit[0:16, :])
                eng.tensor_sub(cbsplit[32:48, :], traF[64:80, :], persF[64:80, :]
                               ).then_inc(dve.h, 1)
            self.em("vector", f)
            split_done = dve.inc(1)

            def f(eng, w=split_done):
                eng.wait_ge(dve.h, w)
                eng.dma_start(out=D["csplitD"][:, :], in_=D["csplit"][0:48, :]).then_inc(sm.h, 16)
                eng.dma_start(out=D["cbsplitD"][:, :], in_=D["cbsplit"][0:48, :]).then_inc(sm.h, 16)
                eng.dma_start(out=D["cD"][:, :], in_=persF[0:16, :]).then_inc(sm.h, 16)
            self.em("scalar", f)
            cstores = sm.inc(48)

            if "smallsD" in self.debug_taps:
                def f(eng, a=dve.n, b=act.n):
                    eng.wait_ge(dve.h, a)
                    eng.wait_ge(act.h, b)
                    eng.dma_start(out=D["smallsD"][:, :], in_=persF[:, :]).then_inc(stm.h, 16)
                self.em("sync", f)
                stm.inc(16)

            self.barrier()

    def _phase_S(self):
        nc, D = self.nc, self.st_
        pe, dve, act, pool, ld, cast, stm, sm = (self.pe, self.dve, self.act, self.pool,
                                                 self.ld, self.cast, self.stm, self.sm)
        pA, pB, pT = self.pA, self.pB, self.pT
        persF = D["persF"]

        with contextlib.ExitStack() as ess:
            for nm, sh, dt in (
                ("rhsT", [66, 4096], FP16), ("gerow", [1, 1024], F32),
                ("state", [128, NH * DV], F32), ("stbf", [128, NH * DV], BF16),
            ):
                D[nm] = ess.enter_context(nc.sbuf_tensor(nm, sh, dt))
            sc = {}
            for nm, sh, dt in (
                ("mix", [128, 32 * MP], BF16), ("qkvc", [128, 32 * SLAB], BF16),
                ("zs_s", [128, 16 * CH], BF16),
                ("l2r", [16, SLAB], F32), ("rowb", [1, 4096], BF16),
                ("argA", [128, 2048], BF16), ("argQ", [128, 2048], BF16),
                ("AT", [128, 2048], BF16), ("AO", [128, 2048], BF16), ("AQ", [128, 2048], BF16),
                ("Ysb", [128, 16 * 256], BF16),
                ("Pw", [128, 2048], BF16), ("PwT", [128, 2048], BF16),
                ("Pw2", [128, 2048], BF16), ("PwT2", [128, 2048], BF16),
                ("KW", [128, 2048], BF16), ("W2T", [128, 256], BF16), ("Usb", [128, 256], BF16),
                ("ogsb", [128, 2048], BF16), ("og2", [128, 2048], BF16),
                ("stage", [128, 128], F32), ("colsb", [128, 128], F32),
                ("gqb", [128, 2048], BF16), ("gendb", [128, 16], F32),
                ("onstg", [128, 2 * 512], BF16),
                ("cvt", [128, 2 * SLAB], F32),
                ("cflat2", [2, 2048], FP16),
                ("scr16", [16, 16], F32),
            ):
                sc[nm] = ess.enter_context(nc.sbuf_tensor(nm, sh, dt))
            self.sc = sc

            sm = self.sm

            def f(eng):
                eng.dma_start(out=D["rhsT"][0:64, :], in_=D["INDT"][:, :]).then_inc(ld.h, 16)
                srcv = D["cD"][0:16, :].rearrange("h (c t) -> h c t", t=CH)[:, :, CH - 1:CH]
                srcv = srcv.squeeze(2).transpose([1, 0]).unsqueeze(0)
                outv = D["gerow"][0:1, 0:1024].rearrange("one (c h) -> one c h", h=16)
                eng.dma_start(out=outv, in_=srcv,
                              allow_slow_non_contiguous=True).then_inc(ld.h, 16)
            self.em("sync", f)
            sinit_ld = ld.inc(32)

            def f(eng, w=sinit_ld):
                eng.wait_ge(ld.h, w)
                eng.activation(out=D["gerow"][0:1, 0:1024], in_=D["gerow"][0:1, 0:1024],
                               func=AF.Exp).then_inc(act.h, 1)
            self.em("scalar", f)
            ge_exp = act.inc(1)

            def f(eng, w=ge_exp):
                eng.wait_ge(act.h, w)
                eng.memset(D["state"][:, :], 0.0)
                eng.memset(sc["stage"][:, :], 0.0).then_inc(dve.h, 1)
            self.em("vector", f)
            dve.inc(1)

            def f(eng, w=dve.n):
                eng.wait_ge(dve.h, w)
                eng.tensor_copy(D["stbf"][:, :], D["state"][:, :]).then_inc(pool.h, 1)
            self.em("gpsimd", f)
            pool.inc(1)

            prev_slab_pe = [0]
            prev = {"argmm": 0, "stagemm": 0, "onst": [None, None]}

            for sb in range(NSLAB):
                t0 = sb * SLAB
                mixv = sc["mix"][:, :].rearrange("p (b c) -> p b c", c=MP)
                if sb == 0:
                    def f(eng):
                        eng.memset(sc["mix"][:, :].rearrange("p (b c) -> p b c", c=MP)[:, :, 0:8],
                                   0.0).then_inc(pool.h, 1)
                    self.em("gpsimd", f)
                else:
                    def f(eng, dv=dve.n, pl=pool.n):
                        eng.wait_ge(dve.h, dv)
                        v = sc["mix"][:, :].rearrange("p (b c) -> p b c", c=MP)
                        eng.tensor_copy(v[:, :, 0:8], v[:, :, SLAB:SLAB + 8]).then_inc(pool.h, 1)
                    self.em("gpsimd", f)
                halo_done = pool.inc(1)

                mv = D["mixedD"][:, t0:t0 + SLAB].rearrange("(b p) t -> p b t", p=128)

                def f(eng, mv=mv, hd=halo_done, dv=dve.n, pl=pool.n, pp=prev_slab_pe[0]):
                    eng.wait_ge(pool.h, hd)
                    eng.wait_ge(dve.h, dv)
                    eng.wait_ge(pe.h, pp)
                    eng.dma_start(out=sc["mix"][:, :].rearrange("p (b c) -> p b c", c=MP)[:, :, 8:8 + SLAB],
                                  in_=mv).then_inc(ld.h, 16)
                self.em("sync", f)
                mix_rdy = ld.inc(16)

                act_base = act.n
                for cb in range(32):
                    cpar = cb % 2

                    def f(eng, cb=cb, cpar=cpar, mr=mix_rdy, pw=prev_slab_pe[0],
                          CF=self.CF_CONV, wact=(act.n - 1 if cb >= 2 else 0)):
                        if cb == 0:
                            eng.wait_ge(ld.h, mr)
                            eng.wait_ge(pe.h, pw)
                        if cb >= 2:
                            eng.wait_ge(act.h, wact)
                        v = sc["mix"][:, :].rearrange("p (b c) -> p b c", c=MP)
                        dst = sc["cvt"][:, cpar * SLAB:(cpar + 1) * SLAB]
                        eng.tensor_scalar_mul(dst, v[:, cb, 8:8 + SLAB], CF(cb, 3))
                        for j in (2, 1, 0):
                            ins = eng.scalar_tensor_tensor(
                                dst, v[:, cb, 5 + j:5 + j + SLAB], CF(cb, j), dst,
                                ALU.mult, ALU.add)
                        ins.then_inc(dve.h, 1)
                    self.em("vector", f)
                    cdone = dve.inc(1)

                    def f(eng, cb=cb, cpar=cpar, w=cdone):
                        eng.wait_ge(dve.h, w)
                        eng.activation(out=sc["qkvc"][:, cb * SLAB:(cb + 1) * SLAB],
                                       in_=sc["cvt"][:, cpar * SLAB:(cpar + 1) * SLAB],
                                       func=AF.Silu).then_inc(act.h, 1)
                    self.em("scalar", f)
                    act.inc(1)
                silu_done = act.n

                def f(eng, w=silu_done):
                    eng.wait_ge(act.h, w)
                    for cb in range(16):
                        eng.activation(out=sc["og2"][:, (cb % 2) * SLAB:(cb % 2 + 1) * SLAB],
                                       in_=sc["qkvc"][:, cb * SLAB:(cb + 1) * SLAB],
                                       func=AF.Square).then_inc(act.h, 1)
                self.em("scalar", f)
                sq_base = act.n
                act.inc(16)

                def f(eng, base=sq_base, pf=self.pfree.get("pB0"), OH=self.C_OH):
                    if pf is not None:
                        eng.wait_ge(pf[0], pf[1])
                    ins = None
                    for cb in range(16):
                        eng.wait_ge(act.h, base + cb + 1)
                        ins = eng.matmul(pB[0][0:16, 0:SLAB], OH(cb),
                                         sc["og2"][:, (cb % 2) * SLAB:(cb % 2 + 1) * SLAB],
                                         start=(cb == 0), stop=(cb == 15))
                    ins.then_inc(pe.h, 1)
                self.em("tensor", f)
                l2mm = pe.inc(1)

                def f(eng, w=l2mm):
                    eng.wait_ge(pe.h, w)
                    eng.activation(out=sc["l2r"][0:16, :], in_=pB[0][0:16, 0:SLAB],
                                   func=AF.Ln, bias=self.CF_EPS6[0:16, :])
                    eng.activation(out=sc["l2r"][0:16, :], in_=sc["l2r"][0:16, :],
                                   func=AF.Exp, scale=-0.5).then_inc(act.h, 1)
                self.em("scalar", f)
                rstd_done = act.inc(1)
                self.puse("pB0", act.h, rstd_done)

                def f(eng, w=rstd_done):
                    eng.wait_ge(act.h, w)
                    eng.tensor_scalar_mul(sc["l2r"][0:8, :], sc["l2r"][0:8, :], RSQ128
                                          ).then_inc(dve.h, 1)
                self.em("vector", f)
                rstds = dve.inc(1)

                def f(eng, w=rstds):
                    eng.wait_ge(dve.h, w)
                    eng.dma_start(out=D["l2rD"][:, :], in_=sc["l2r"][0:16, :]).then_inc(sm.h, 16)
                self.em("scalar", f)
                l2st = sm.inc(16)

                def f(eng, w=l2st, pw=pe.n):
                    eng.wait_ge(sm.h, w)
                    outv = sc["rowb"][0:1, :].rearrange("one (b t) -> one b t", t=SLAB)
                    eng.dma_start(out=outv, in_=D["l2rD"][0:16, :].unsqueeze(0)
                                  ).then_inc(self.cast.h, 16)
                self.em("gpsimd", f)
                l2fl = self.cast.inc(16)

                for cb in range(16):
                    def f(eng, cb=cb, w=l2fl, pf=self.pfree.get("pB0"), OR=self.C_OROW):
                        eng.wait_ge(self.cast.h, w)
                        if pf is not None:
                            eng.wait_ge(pf[0], pf[1])
                        eng.matmul(pB[0][:, (cb % 2) * 256:(cb % 2) * 256 + SLAB],
                                   OR, sc["rowb"][0:1, cb * SLAB:(cb + 1) * SLAB],
                                   start=True, stop=True).then_inc(pe.h, 1)
                    self.em("tensor", f)
                    bc = pe.inc(1)

                    def f(eng, cb=cb, w=bc):
                        eng.wait_ge(pe.h, w)
                        sl = sc["qkvc"][:, cb * SLAB:(cb + 1) * SLAB]
                        eng.tensor_mul(sl, sl, pB[0][:, (cb % 2) * 256:(cb % 2) * 256 + SLAB]
                                       ).then_inc(dve.h, 1)
                    self.em("vector", f)
                    self.puse("pB0", dve.h, dve.inc(1))
                l2_applied = dve.n

                for tc in range(SLAB // CH):
                    ci = sb * (SLAB // CH) + tc
                    self._chunk(ci, tc, l2_applied, prev)
                prev_slab_pe[0] = pe.n

            self.barrier()
        self._wout_pass()
        self.barrier()

    def _chunk(self, ci, tc, l2_applied, prev):
        D, sc = self.st_, self.sc
        pe, dve, act, pool, sm, ld = self.pe, self.dve, self.act, self.pool, self.sm, self.ld
        pA, pB, pT = self.pA, self.pB, self.pT
        c0 = ci * CH
        csplit, cbsplit, rhsT, persF = D["csplit"], D["cbsplit"], D["rhsT"], D["persF"]

        def qc(cb):
            return sc["qkvc"][:, cb * SLAB + tc * CH: cb * SLAB + tc * CH + CH]

        def zc(h):
            return sc["zs_s"][:, h * CH:(h + 1) * CH]

        def blkf(name, h):
            return sc[name][:, h * 128:(h + 1) * 128]

        zv = D["zsD"][:, c0:c0 + CH].rearrange("(b p) t -> p b t", p=128)

        def f(eng, zv=zv, dv=dve.n):
            eng.wait_ge(dve.h, dv)
            eng.dma_start(out=sc["zs_s"][:, :].rearrange("p (b t) -> p b t", t=CH),
                          in_=zv).then_inc(ld.h, 16)
        self.em("sync", f)
        zs_rdy = ld.inc(16)

        def f(eng, w=prev["argmm"], c0=c0):
            eng.wait_ge(pe.h, w)
            for (row, col, srcD, r0) in ((64, 0, "cbsplitD", 0), (65, 0, "cbsplitD", 32),
                                         (64, 2048, "csplitD", 0), (65, 2048, "csplitD", 32)):
                outv = rhsT[row:row + 1, col:col + 2048].rearrange(
                    "one (h t) -> one h t", t=CH)
                inv = D[srcD][r0:r0 + 16, c0:c0 + CH].unsqueeze(0)
                eng.dma_start(out=outv, in_=inv).then_inc(sm.h, 16)
            for (row, r0) in ((0, 0), (1, 32)):
                outv = sc["cflat2"][row:row + 1, :].rearrange("one (h t) -> one h t", t=CH)
                inv = D["csplitD"][r0:r0 + 16, c0:c0 + CH].unsqueeze(0)
                eng.dma_start(out=outv, in_=inv).then_inc(sm.h, 16)
        self.em("scalar", f)
        flats = sm.inc(96)

        argev = None
        for roff, dst in ((0, "argA"), (2048, "argQ")):
            for j in range(4):
                def f(eng, j=j, roff=roff, w=flats, c0=c0, pf=self.pfree.get("pA%d" % j)):
                    eng.wait_ge(sm.h, w)
                    if pf is not None:
                        eng.wait_ge(pf[0], pf[1])
                    eng.matmul(pA[j][:, :], csplit[0:66, c0:c0 + CH],
                               rhsT[0:66, roff + j * 512:roff + (j + 1) * 512],
                               start=True, stop=True).then_inc(pe.h, 1)
                self.em("tensor", f)
                mm = pe.inc(1)

                def f(eng, j=j, dst=dst, w=mm):
                    eng.wait_ge(pe.h, w)
                    eng.tensor_scalar(sc[dst][:, j * 512:(j + 1) * 512], pA[j][:, :],
                                      0.0, None, ALU.min).then_inc(dve.h, 1)
                self.em("vector", f)
                argev = dve.inc(1)
                self.puse("pA%d" % j, dve.h, argev)

        def f(eng, w=argev):
            eng.wait_ge(dve.h, w)
            eng.activation(out=sc["argA"][:, :], in_=sc["argA"][:, :], func=AF.Exp)
            eng.activation(out=sc["argQ"][:, :], in_=sc["argQ"][:, :], func=AF.Exp
                           ).then_inc(act.h, 1)
        self.em("scalar", f)
        exp_done = act.inc(1)

        def f(eng, c0=c0, w=prev["stagemm"]):
            eng.wait_ge(pe.h, w)
            eng.tensor_copy(sc["stage"][0:16, :], persF[32:48, c0:c0 + CH])
            eng.tensor_copy(sc["scr16"][0:16, 0:1], persF[0:16, c0 + CH - 1:c0 + CH]
                            ).then_inc(dve.h, 1)
        self.em("vector", f)
        stg1 = dve.inc(1)

        def f(eng, c0=c0, w=stg1):
            eng.wait_ge(dve.h, w)
            eng.activation(out=sc["stage"][32:48, :], in_=persF[0:16, c0:c0 + CH], func=AF.Exp)
            eng.activation(out=sc["stage"][64:80, :], in_=persF[0:16, c0:c0 + CH],
                           func=AF.Exp, scale=-1.0, bias=sc["scr16"][0:16, 0:1]
                           ).then_inc(act.h, 1)
        self.em("scalar", f)
        stg2 = act.inc(1)

        def f(eng, c0=c0, w=stg2):
            eng.wait_ge(act.h, w)
            eng.tensor_mul(sc["stage"][32:48, :], sc["stage"][32:48, :],
                           persF[32:48, c0:c0 + CH]).then_inc(dve.h, 1)
        self.em("vector", f)
        stg3 = dve.inc(1)

        def f(eng, w=stg3, pf=self.pfree.get("pB1"), IDF=self.C_IDF):
            eng.wait_ge(dve.h, w)
            if pf is not None:
                eng.wait_ge(pf[0], pf[1])
            eng.transpose(pB[1][:, 0:128], sc["stage"][:, :], IDF).then_inc(pe.h, 1)
        self.em("tensor", f)
        trs = pe.inc(1)
        prev["stagemm"] = trs

        def f(eng, ci=ci, ORF=self.CF_OROWF):
            eng.matmul(pB[1][:, 128:144], ORF, D["gerow"][0:1, ci * 16:(ci + 1) * 16],
                       start=True, stop=True).then_inc(pe.h, 1)
        self.em("tensor", f)
        gemm = pe.inc(1)

        def f(eng, w=gemm):
            eng.wait_ge(pe.h, w)
            eng.tensor_copy(sc["colsb"][:, :], pB[1][:, 0:128])
            eng.tensor_scalar_mul(sc["colsb"][:, 96:112], sc["colsb"][:, 32:48], -1.0)
            eng.tensor_copy(sc["gendb"][:, :], pB[1][:, 128:144]).then_inc(dve.h, 1)
        self.em("vector", f)
        cols_done = dve.inc(1)
        self.puse("pB1", dve.h, cols_done)

        gexp = None
        for j in range(4):
            def f(eng, j=j, w=flats, pf=self.pfree.get("pA%d" % j), O2=D["ones2"]):
                eng.wait_ge(sm.h, w)
                if pf is not None:
                    eng.wait_ge(pf[0], pf[1])
                eng.matmul(pA[j][:, :], O2[0:2, :],
                           sc["cflat2"][0:2, j * 512:(j + 1) * 512],
                           start=True, stop=True).then_inc(pe.h, 1)
            self.em("tensor", f)
            mm = pe.inc(1)

            def f(eng, j=j, w=mm):
                eng.wait_ge(pe.h, w)
                eng.activation(out=sc["og2"][:, j * 512:(j + 1) * 512], in_=pA[j][:, :],
                               func=AF.Exp).then_inc(act.h, 1)
            self.em("scalar", f)
            gexp = act.inc(1)
            self.puse("pA%d" % j, act.h, gexp)
        prev["argmm"] = pe.n

        def f(eng, w=gexp, w2=l2_applied):
            eng.wait_ge(act.h, w)
            eng.wait_ge(dve.h, w2)
            last = None
            for h in range(NH):
                last = eng.tensor_mul(blkf("gqb", h), qc(h // 2), blkf("og2", h))
            last.then_inc(pool.h, 1)
        self.em("gpsimd", f)
        gq_done_pool = pool.inc(1)

        def f(eng, w=dve.n):
            eng.wait_ge(dve.h, w)
            eng.tensor_copy(D["stbf"][:, :], D["state"][:, :]).then_inc(pool.h, 1)
        self.em("gpsimd", f)
        stbf_done = pool.inc(1)

        if ci == 0:
            self.dump("dbg_qkvc", sc["qkvc"][:, :], [128, 32 * SLAB], BF16)
            self.dump("dbg_argA", sc["argA"][:, :], [128, 2048], BF16)
            self.dump("dbg_argQ", sc["argQ"][:, :], [128, 2048], BF16)
            self.dump("dbg_cols", sc["colsb"][:, :], [128, 128], F32)
            self.dump("dbg_gend", sc["gendb"][:, :], [128, 16], F32)
            self.dump("dbg_gqb", sc["gqb"][:, :], [128, 2048], BF16)

        skk_ev = {}
        tev_map = {}
        skbanks = ["pB0", "pB1", "pA0", "pA1"]
        slot_free = [self.pfree.get(b) for b in skbanks]
        skk_last = [0, 0, 0, 0]
        for h in range(NH):
            slot = h % 4
            bank = [pB[0], pB[1], pA[0], pA[1]][slot]
            kb, qb = 8 + h // 2, h // 2

            def f(eng, bank=bank, kb=kb, qb=qb, w=l2_applied, pf=slot_free[slot]):
                eng.wait_ge(dve.h, w)
                if pf is not None:
                    eng.wait_ge(pf[0], pf[1])
                eng.matmul(bank[:, 0:128], qc(kb), qc(kb), start=True, stop=True)
                eng.matmul(bank[:, 128:256], qc(kb), qc(qb), start=True, stop=True
                           ).then_inc(pe.h, 1)
            self.em("tensor", f)
            skmm = pe.inc(1)

            def f(eng, h=h, bank=bank, w=skmm, we=exp_done,
                  MU=self.C_MUSN, MQ=self.C_MUI):
                eng.wait_ge(pe.h, w)
                eng.wait_ge(act.h, we)
                eng.tensor_mul(blkf("AT", h), bank[:, 0:128],
                               sc["argA"][:, h * 128:(h + 1) * 128])
                eng.tensor_mul(blkf("AT", h), blkf("AT", h), MU)
                eng.tensor_mul(blkf("AQ", h), bank[:, 128:256],
                               sc["argQ"][:, h * 128:(h + 1) * 128])
                eng.tensor_mul(blkf("AQ", h), blkf("AQ", h), MQ).then_inc(dve.h, 1)
            self.em("vector", f)
            amul = dve.inc(1)
            slot_free[slot] = (dve.h, amul)
            skk_last[slot] = amul
            skk_ev[h] = amul
        for i, b in enumerate(skbanks):
            self.puse(b, dve.h, skk_last[i])

        for h in range(NH):
            par = h % 2
            kb, vb = 8 + h // 2, 16 + h

            def f(eng, h=h, par=par, kb=kb, vb=vb, w=skk_ev[h],
                  pf=self.pfree.get("pT%d" % par), ID=self.C_ID):
                eng.wait_ge(dve.h, w)
                if pf is not None:
                    eng.wait_ge(pf[0], pf[1])
                eng.transpose(pT[par][:, 0:128], blkf("AT", h), ID)
                eng.transpose(pT[par][:, 128:256], qc(kb), ID)
                eng.transpose(pT[par][:, 256:384], qc(vb), ID).then_inc(pe.h, 1)
            self.em("tensor", f)
            trs = pe.inc(1)

            if h % 2 == 0:
                def f(eng, h=h, par=par, w=trs, wc=cols_done):
                    eng.wait_ge(pe.h, w)
                    eng.wait_ge(dve.h, wc)
                    cb_ = sc["colsb"]
                    eng.tensor_copy(blkf("AO", h), pT[par][:, 0:128])
                    eng.tensor_scalar(sc["Ysb"][:, h * 256 + 128:h * 256 + 256],
                                      pT[par][:, 128:256], cb_[:, 96 + h:97 + h], None,
                                      ALU.mult)
                    eng.tensor_scalar(blkf("KW", h), pT[par][:, 128:256],
                                      cb_[:, 64 + h:65 + h], None, ALU.mult)
                    eng.tensor_scalar(sc["Ysb"][:, h * 256:h * 256 + 128],
                                      pT[par][:, 256:384], cb_[:, h:h + 1], None, ALU.mult
                                      ).then_inc(dve.h, 1)
                self.em("vector", f)
                tev = dve.inc(1)
                tev_map[h] = (dve.h, tev)
                self.puse("pT%d" % par, dve.h, tev)
            else:
                def f(eng, h=h, par=par, w=trs, wc=cols_done):
                    eng.wait_ge(pe.h, w)
                    eng.wait_ge(dve.h, wc)
                    cb_ = sc["colsb"]
                    eng.copy(blkf("AO", h), pT[par][:, 0:128])
                    eng.mul(sc["Ysb"][:, h * 256 + 128:h * 256 + 256],
                            pT[par][:, 128:256], cb_[:, 96 + h:97 + h])
                    eng.mul(blkf("KW", h), pT[par][:, 128:256], cb_[:, 64 + h:65 + h])
                    eng.mul(sc["Ysb"][:, h * 256:h * 256 + 128],
                            pT[par][:, 256:384], cb_[:, h:h + 1]).then_inc(act.h, 1)
                self.em("scalar", f)
                tev = act.inc(1)
                tev_map[h] = (act.h, tev)
                self.puse("pT%d" % par, act.h, tev)

        psrc = ("AO", "AT")
        pwv = {h: tev_map[h] for h in range(NH)}
        yv = {h: tev_map[h] for h in range(NH)}
        for lvl in range(7):
            pdst = ("Pw", "PwT") if (lvl % 2 == 1) else ("Pw2", "PwT2")
            if lvl >= 1:
                bslot_free = [self.pfree.get("pB0"), self.pfree.get("pB1")]
                for h in range(NH):
                    slot = h % 2
                    bank = pB[slot]

                    def f(eng, h=h, bank=bank, psrc=psrc, lvl=lvl, wv=pwv[h],
                          pf=bslot_free[slot]):
                        eng.wait_ge(wv[0], wv[1])
                        if pf is not None:
                            eng.wait_ge(pf[0], pf[1])
                        ins = eng.matmul(bank[:, 0:128], blkf(psrc[0], h),
                                         blkf(psrc[1], h), start=True, stop=True)
                        if lvl < 6:
                            ins = eng.matmul(bank[:, 128:256], blkf(psrc[1], h),
                                             blkf(psrc[0], h), start=True, stop=True)
                        ins.then_inc(pe.h, 1)
                    self.em("tensor", f)
                    bmm = pe.inc(1)

                    def f(eng, h=h, bank=bank, pdst=pdst, lvl=lvl, w=bmm):
                        eng.wait_ge(pe.h, w)
                        ins = eng.tensor_copy(blkf(pdst[1], h), bank[:, 0:128])
                        if lvl < 6:
                            ins = eng.tensor_copy(blkf(pdst[0], h), bank[:, 128:256])
                        ins.then_inc(dve.h, 1)
                    self.em("vector", f)
                    bev = dve.inc(1)
                    pwv[h] = (dve.h, bev)
                    bslot_free[slot] = (dve.h, bev)
                self.puse("pB0", dve.h, dve.n)
                self.puse("pB1", dve.h, dve.n)
                pw_t = pdst[1]
            else:
                pw_t = "AT"

            aslot_free = [self.pfree.get("pA0"), self.pfree.get("pA1")]
            for h in range(NH):
                slot = h % 2
                bank = pA[slot]

                def f(eng, h=h, bank=bank, pw_t=pw_t, wp=pwv[h], wy=yv[h],
                      pf=aslot_free[slot]):
                    eng.wait_ge(wp[0], wp[1])
                    eng.wait_ge(wy[0], wy[1])
                    if pf is not None:
                        eng.wait_ge(pf[0], pf[1])
                    ysl = sc["Ysb"][:, h * 256:(h + 1) * 256]
                    eng.matmul(bank[:, 0:256], blkf(pw_t, h), ysl,
                               start=True, stop=True).then_inc(pe.h, 1)
                self.em("tensor", f)
                amm = pe.inc(1)

                def f(eng, h=h, bank=bank, w=amm):
                    eng.wait_ge(pe.h, w)
                    ysl = sc["Ysb"][:, h * 256:(h + 1) * 256]
                    eng.tensor_add(ysl, ysl, bank[:, 0:256]).then_inc(dve.h, 1)
                self.em("vector", f)
                yev = (dve.h, dve.inc(1))
                yv[h] = yev
                aslot_free[slot] = yev
            self.puse("pA0", dve.h, dve.n)
            self.puse("pA1", dve.h, dve.n)
            if lvl >= 1:
                psrc = pdst
        yready = (dve.n, act.n)
        if ci == 0:
            self.dump("dbg_Y", sc["Ysb"][:, :], [128, 4096], BF16)

        for h in range(NH):
            par = h % 2

            def f(eng, h=h, par=par, wv=yready, pf=self.pfree.get("pT%d" % par), ID=self.C_ID):
                eng.wait_ge(dve.h, wv[0])
                eng.wait_ge(act.h, wv[1])
                if pf is not None:
                    eng.wait_ge(pf[0], pf[1])
                eng.transpose(pT[par][:, 384:512], sc["Ysb"][:, h * 256 + 128:h * 256 + 256],
                              ID).then_inc(pe.h, 1)
            self.em("tensor", f)
            w2mm = pe.inc(1)

            def f(eng, h=h, par=par, w=w2mm):
                eng.wait_ge(pe.h, w)
                eng.tensor_copy(sc["W2T"][:, par * 128:(par + 1) * 128],
                                pT[par][:, 384:512]).then_inc(dve.h, 1)
            self.em("vector", f)
            w2ev = dve.inc(1)
            self.puse("pT%d" % par, dve.h, w2ev)

            def f(eng, h=h, par=par, w=w2ev, ws=stbf_done, pf=self.pfree.get("pA2"),
                  ID=self.C_ID):
                eng.wait_ge(dve.h, w)
                eng.wait_ge(pool.h, ws)
                if pf is not None:
                    eng.wait_ge(pf[0], pf[1])
                eng.matmul(pA[2][:, par * 128:par * 128 + 128], ID,
                           sc["Ysb"][:, h * 256:h * 256 + 128], start=True, stop=False)
                eng.matmul(pA[2][:, par * 128:par * 128 + 128],
                           sc["W2T"][:, par * 128:(par + 1) * 128],
                           D["stbf"][:, h * 128:(h + 1) * 128], start=False, stop=True
                           ).then_inc(pe.h, 1)
            self.em("tensor", f)
            umm = pe.inc(1)

            def f(eng, h=h, par=par, w=umm):
                eng.wait_ge(pe.h, w)
                eng.tensor_copy(sc["Usb"][:, par * 128:(par + 1) * 128],
                                pA[2][:, par * 128:par * 128 + 128]).then_inc(dve.h, 1)
            self.em("vector", f)
            uev = dve.inc(1)
            self.puse("pA2", dve.h, uev)

            def f(eng, h=h, par=par, w=uev, wg=gq_done_pool, pf=self.pfree.get("pA3")):
                eng.wait_ge(dve.h, w)
                eng.wait_ge(pool.h, wg)
                if pf is not None:
                    eng.wait_ge(pf[0], pf[1])
                eng.matmul(pA[3][:, par * 128:par * 128 + 128],
                           D["stbf"][:, h * 128:(h + 1) * 128], blkf("gqb", h),
                           start=True, stop=False)
                eng.matmul(pA[3][:, par * 128:par * 128 + 128],
                           sc["Usb"][:, par * 128:(par + 1) * 128],
                           blkf("AQ", h), start=False, stop=True).then_inc(pe.h, 1)
            self.em("tensor", f)
            omm = pe.inc(1)

            def f(eng, h=h, par=par, w=omm, wz=zs_rdy):
                eng.wait_ge(pe.h, w)
                eng.wait_ge(ld.h, wz)
                eng.tensor_mul(blkf("ogsb", h), pA[3][:, par * 128:par * 128 + 128],
                               zc(h)).then_inc(dve.h, 1)
            self.em("vector", f)
            ogev = dve.inc(1)
            self.puse("pA3", dve.h, ogev)

            def f(eng, h=h, par=par, w=uev, pf=self.pfree.get("pB%d" % par)):
                eng.wait_ge(dve.h, w)
                if pf is not None:
                    eng.wait_ge(pf[0], pf[1])
                eng.matmul(pB[par][:, 0:128], blkf("KW", h),
                           sc["Usb"][:, par * 128:(par + 1) * 128],
                           start=True, stop=True).then_inc(pe.h, 1)
            self.em("tensor", f)
            smm = pe.inc(1)

            def f(eng, h=h, par=par, w=smm):
                eng.wait_ge(pe.h, w)
                eng.scalar_tensor_tensor(D["state"][:, h * 128:(h + 1) * 128],
                                         D["state"][:, h * 128:(h + 1) * 128],
                                         sc["gendb"][:, h:h + 1], pB[par][:, 0:128],
                                         ALU.mult, ALU.add).then_inc(dve.h, 1)
            self.em("vector", f)
            sev = dve.inc(1)
            self.puse("pB%d" % par, dve.h, sev)

        if ci == 0:
            self.dump("dbg_og", sc["ogsb"][:, :], [128, 2048], BF16)
            self.dump("dbg_state", D["state"][:, :], [128, 2048], F32)

        def f(eng, w=dve.n):
            eng.wait_ge(dve.h, w)
            eng.activation(out=sc["og2"][:, :], in_=sc["ogsb"][:, :], func=AF.Square
                           ).then_inc(act.h, 1)
        self.em("scalar", f)
        og2_done = act.inc(1)

        for bank in range(4):
            def f(eng, bank=bank, w=og2_done, pf=self.pfree.get("pA%d" % bank),
                  OC=self.C_OCOL):
                eng.wait_ge(act.h, w)
                if pf is not None:
                    eng.wait_ge(pf[0], pf[1])
                ins = None
                for hh in range(4):
                    h = bank * 4 + hh
                    ins = eng.matmul(pA[bank][0:1, hh * 128:(hh + 1) * 128], OC,
                                     blkf("og2", h), start=(hh == 0), stop=(hh == 3))
                ins.then_inc(pe.h, 1)
            self.em("tensor", f)
            summ = pe.inc(1)

            def f(eng, bank=bank, w=summ):
                eng.wait_ge(pe.h, w)
                eng.activation(out=sc["rowb"][0:1, bank * 512:(bank + 1) * 512],
                               in_=pA[bank][0:1, :], func=AF.Ln, bias=self.CF_RMSE[0:1, :])
                eng.activation(out=sc["rowb"][0:1, bank * 512:(bank + 1) * 512],
                               in_=sc["rowb"][0:1, bank * 512:(bank + 1) * 512],
                               func=AF.Exp, scale=-0.5, bias=self.CF_LN128H[0:1, :]
                               ).then_inc(act.h, 1)
            self.em("scalar", f)
            self.puse("pA%d" % bank, act.h, act.inc(1))
        rstdd = act.n

        for j in range(4):
            par = j % 2

            def f(eng, j=j, w=rstdd, pf=self.pfree.get("pA%d" % j), OR=self.C_OROW):
                eng.wait_ge(act.h, w)
                if pf is not None:
                    eng.wait_ge(pf[0], pf[1])
                eng.matmul(pA[j][:, 0:512], OR, sc["rowb"][0:1, j * 512:(j + 1) * 512],
                           start=True, stop=True).then_inc(pe.h, 1)
            self.em("tensor", f)
            bcmm = pe.inc(1)

            def f(eng, j=j, par=par, w=bcmm, NW=self.CF_NRMW, ost=prev["onst"][par]):
                eng.wait_ge(pe.h, w)
                if ost is not None:
                    eng.wait_ge(ost[0], ost[1])
                eng.scalar_tensor_tensor(sc["onstg"][:, par * 512:(par + 1) * 512],
                                         sc["ogsb"][:, j * 512:(j + 1) * 512], NW,
                                         pA[j][:, 0:512], ALU.mult, ALU.mult
                                         ).then_inc(dve.h, 1)
            self.em("vector", f)
            onev = dve.inc(1)
            self.puse("pA%d" % j, dve.h, onev)

            dst = D["onD"][j * 512:(j + 1) * 512, c0:c0 + CH].rearrange(
                "(hh p) t -> p hh t", p=128)

            def f(eng, par=par, w=onev, dst=dst):
                eng.wait_ge(dve.h, w)
                eng.dma_start(out=dst,
                              in_=sc["onstg"][:, par * 512:(par + 1) * 512
                                              ].rearrange("p (hh t) -> p hh t", t=CH)
                              ).then_inc(self.stm.h, 16)
            self.em("sync", f)
            prev["onst"][par] = (self.stm.h, self.stm.inc(16))

    def _wout_pass(self):
        nc, D = self.nc, self.st_
        pe, dve, act, cast, ld, stm = self.pe, self.dve, self.act, self.cast, self.ld, self.stm
        pA = self.pA
        with (
            nc.sbuf_tensor("wo_sb", [128, 16 * H], BF16) as wo_sb,
            nc.sbuf_tensor("on_t", [128, 2 * 16 * 512], BF16) as on_t,
            nc.sbuf_tensor("po_st", [128, 4 * 512], BF16) as po_st,
        ):
            wv = D["wout"][:, :].rearrange("(k p) hcol -> p k hcol", p=128)

            def f(eng, wv=wv):
                eng.dma_start(out=wo_sb[:, :].rearrange("p (k c) -> p k c", c=H),
                              in_=wv).then_inc(cast.h, 16)
            self.em("gpsimd", f)
            wo_done = cast.inc(16)

            prev_use = [0, 0]
            st_free = [None] * 4
            rr = [0]
            for tt in range(S // 512):
                tpar = tt % 2
                onv = D["onD"][:, tt * 512:(tt + 1) * 512].rearrange("(k p) t -> p k t", p=128)

                def f(eng, onv=onv, tpar=tpar, w=prev_use[tpar]):
                    eng.wait_ge(pe.h, w)
                    eng.dma_start(out=on_t[:, tpar * (16 * 512):(tpar + 1) * (16 * 512)
                                          ].rearrange("p (k t) -> p k t", t=512),
                                  in_=onv).then_inc(ld.h, 16)
                self.em("sync", f)
                on_done = ld.inc(16)

                half = tt // 8
                colbase = (tt % 8) * 512
                for hb in range(16):
                    bank = rr[0] % 4
                    sgi = rr[0] % 4
                    rr[0] += 1

                    def f(eng, hb=hb, tpar=tpar, bank=bank, w=on_done, ww=wo_done,
                          pf=self.pfree.get("pA%d" % bank)):
                        eng.wait_ge(ld.h, w)
                        eng.wait_ge(cast.h, ww)
                        if pf is not None:
                            eng.wait_ge(pf[0], pf[1])
                        ins = None
                        for k in range(16):
                            ins = eng.matmul(
                                pA[bank][:, :],
                                wo_sb[:, k * H + hb * 128:k * H + (hb + 1) * 128],
                                on_t[:, tpar * (16 * 512) + k * 512:
                                     tpar * (16 * 512) + (k + 1) * 512],
                                start=(k == 0), stop=(k == 15))
                        ins.then_inc(pe.h, 1)
                    self.em("tensor", f)
                    mm = pe.inc(1)
                    prev_use[tpar] = mm

                    def f(eng, bank=bank, sgi=sgi, w=mm, sf=st_free[sgi]):
                        eng.wait_ge(pe.h, w)
                        if sf is not None:
                            eng.wait_ge(sf[0], sf[1])
                        eng.copy(po_st[:, sgi * 512:(sgi + 1) * 512], pA[bank][:, :]
                                 ).then_inc(act.h, 1)
                    self.em("scalar", f)
                    ev = act.inc(1)
                    self.puse("pA%d" % bank, act.h, ev)

                    dst = D["rsIn"][half * H + hb * 128:half * H + (hb + 1) * 128,
                                    colbase:colbase + 512]

                    def f(eng, dst=dst, sgi=sgi, ev=ev):
                        eng.wait_ge(act.h, ev)
                        eng.dma_start(out=dst, in_=po_st[:, sgi * 512:(sgi + 1) * 512]
                                      ).then_inc(stm.h, 16)
                    self.em("sync", f)
                    st_free[sgi] = (stm.h, stm.inc(16))

    def _phase_D(self):
        nc, D = self.nc, self.st_
        pe, dve, act, pool, ld, cast, stm, sm, ccm = (
            self.pe, self.dve, self.act, self.pool, self.ld, self.cast, self.stm,
            self.sm, self.ccm)
        pA, pB, pT = self.pA, self.pB, self.pT

        def f(eng, w=stm.n):
            eng.wait_ge(stm.h, w)
            eng.collective_compute(
                "ReduceScatter", ALU.add, replica_groups=REPL,
                ins=[D["rsIn"][0:2 * H, :]], outs=[D["rsOut"][0:H, :]],
            ).then_inc(ccm.h)
        self.em("gpsimd", f)
        rs_done = ccm.inc(1)

        with contextlib.ExitStack() as esd:
            x1n = esd.enter_context(nc.sbuf_tensor("x1n", [128, 16 * TH], BF16))
            rs_t = esd.enter_context(nc.sbuf_tensor("rs_t", [128, 16 * 512], BF16))
            xf_t = esd.enter_context(nc.sbuf_tensor("xf_t", [128, 16 * 512], BF16))
            x1b = esd.enter_context(nc.sbuf_tensor("x1b", [128, 16 * 512], BF16))
            sq2 = rs_t
            rowst = esd.enter_context(nc.sbuf_tensor("rowst", [1, 4 * 512], F32))

            for tt in range(TH // 512):
                rsv = D["rsOut"][:, tt * 512:(tt + 1) * 512].rearrange("(k p) t -> p k t", p=128)
                xv = D["xTd"][:, tt * 512:(tt + 1) * 512].rearrange("(k p) t -> p k t", p=128)

                def f(eng, rsv=rsv, w=rs_done, wp=pool.n, wq=pe.n):
                    eng.wait_ge(ccm.h, w)
                    eng.wait_ge(pool.h, wp)
                    eng.wait_ge(pe.h, wq)
                    eng.dma_start(out=rs_t[:, :].rearrange("p (k t) -> p k t", t=512),
                                  in_=rsv).then_inc(ld.h, 16)
                self.em("sync", f)

                def f(eng, xv=xv, wp=pool.n):
                    eng.wait_ge(pool.h, wp)
                    eng.dma_start(out=xf_t[:, :].rearrange("p (k t) -> p k t", t=512),
                                  in_=xv).then_inc(cast.h, 16)
                self.em("gpsimd", f)
                cast.inc(16)
                tiles_rdy = ld.inc(16)
                xt_rdy = cast.n

                def f(eng, w=tiles_rdy, wd=dve.n):
                    eng.wait_ge(ld.h, w)
                    eng.wait_ge(dve.h, wd)
                    last = None
                    for k in range(16):
                        last = eng.tensor_add(x1b[:, k * 512:(k + 1) * 512],
                                              xf_t[:, k * 512:(k + 1) * 512],
                                              rs_t[:, k * 512:(k + 1) * 512])
                    last.then_inc(pool.h, 1)
                self.em("gpsimd", f)
                x1b_done = pool.inc(1)

                def f(eng, w=x1b_done, pf0=self.pfree.get("pB0"), OC=self.C_OCOL):
                    eng.wait_ge(pool.h, w)
                    if pf0 is not None:
                        eng.wait_ge(pf0[0], pf0[1])
                    ins = None
                    for k in range(16):
                        ins = eng.matmul(pB[0][0:1, 0:512], OC, x1b[:, k * 512:(k + 1) * 512],
                                         start=(k == 0), stop=(k == 15))
                    ins.then_inc(pe.h, 1)
                self.em("tensor", f)
                summ = pe.inc(1)

                def f(eng, w=x1b_done, wp=pe.n):
                    eng.wait_ge(pool.h, w)
                    eng.wait_ge(pe.h, wp)
                    last = None
                    for k in range(16):
                        last = eng.activation(out=sq2[:, k * 512:(k + 1) * 512],
                                              in_=x1b[:, k * 512:(k + 1) * 512],
                                              func=AF.Square)
                    last.then_inc(act.h, 1)
                self.em("scalar", f)
                sqb = act.inc(1)

                def f(eng, w=sqb, pf1=self.pfree.get("pB1"), OC=self.C_OCOL):
                    eng.wait_ge(act.h, w)
                    if pf1 is not None:
                        eng.wait_ge(pf1[0], pf1[1])
                    ins = None
                    for k in range(16):
                        ins = eng.matmul(pB[1][0:1, 0:512], OC, sq2[:, k * 512:(k + 1) * 512],
                                         start=(k == 0), stop=(k == 15))
                    ins.then_inc(pe.h, 1)
                self.em("tensor", f)
                sq_mm = pe.inc(1)

                def f(eng, a=summ, b=sq_mm):
                    eng.wait_ge(pe.h, max(a, b))
                    eng.activation(out=rowst[0:1, 0:512], in_=pB[0][0:1, 0:512],
                                   func=AF.Copy, scale=self.CF_I2048[0:1, :])
                    eng.activation(out=rowst[0:1, 512:1024], in_=rowst[0:1, 0:512],
                                   func=AF.Square).then_inc(act.h, 1)
                self.em("scalar", f)
                mrow = act.inc(1)

                def f(eng, w=mrow):
                    eng.wait_ge(act.h, w)
                    eng.scalar_tensor_tensor(rowst[0:1, 1024:1536], pB[1][0:1, 0:512],
                                             1.0 / 2048.0, rowst[0:1, 512:1024],
                                             ALU.mult, ALU.subtract).then_inc(dve.h, 1)
                self.em("vector", f)
                varr = dve.inc(1)
                self.puse("pB1", dve.h, varr)

                def f(eng, w=varr):
                    eng.wait_ge(dve.h, w)
                    eng.activation(out=rowst[0:1, 1536:2048], in_=rowst[0:1, 1024:1536],
                                   func=AF.Ln, bias=self.CF_EPS5[0:1, :])
                    eng.activation(out=rowst[0:1, 1536:2048], in_=rowst[0:1, 1536:2048],
                                   func=AF.Exp, scale=-0.5).then_inc(act.h, 1)
                self.em("scalar", f)
                rrow = act.inc(1)

                def f(eng, w=rrow, ORF=self.CF_OROWF):
                    eng.wait_ge(act.h, w)
                    eng.matmul(pB[0][:, 0:512], ORF, rowst[0:1, 0:512], start=True, stop=True)
                    eng.matmul(pB[1][:, 0:512], ORF, rowst[0:1, 1536:2048], start=True,
                               stop=True).then_inc(pe.h, 1)
                self.em("tensor", f)
                bcs = pe.inc(1)

                def f(eng, tt=tt, w=bcs):
                    eng.wait_ge(pe.h, w)
                    last = None
                    for k in range(16):
                        dst = x1n[:, k * TH + tt * 512:k * TH + (tt + 1) * 512]
                        last = eng.tensor_sub(dst, x1b[:, k * 512:(k + 1) * 512],
                                              pB[0][:, 0:512])
                        last = eng.tensor_mul(dst, dst, pB[1][:, 0:512])
                    last.then_inc(dve.h, 1)
                self.em("vector", f)
                x1n_done = dve.inc(1)
                self.puse("pB0", dve.h, x1n_done)
                self.puse("pB1", dve.h, x1n_done)

                xnv = D["x1nD"][:, tt * 512:(tt + 1) * 512].rearrange("(k p) t -> p k t", p=128)

                def f(eng, tt=tt, xnv=xnv, w=x1n_done):
                    eng.wait_ge(dve.h, w)
                    eng.dma_start(out=xnv,
                                  in_=x1n[:, :].rearrange("p (k t2) -> p k t2", t2=TH)
                                  [:, :, tt * 512:(tt + 1) * 512]).then_inc(stm.h, 16)
                self.em("sync", f)
                stm.inc(16)

            with (
                nc.sbuf_tensor("guw", [128, 2 * 16 * 256], BF16) as guw,
                nc.sbuf_tensor("sgst", [128, 512], BF16) as sgst,
                nc.sbuf_tensor("hstg", [128, 2 * 512], BF16) as hstg,
            ):
                guw_use = [0, 0]
                sg_use = [None]
                hst_use = [None, None]
                for pb in range(INTER // 128):
                    par = pb % 2
                    wv = D["wgu"][:, pb * 256:(pb + 1) * 256].rearrange(
                        "(k p) c -> p k c", p=128)

                    def f(eng, wv=wv, par=par, w=guw_use[par]):
                        eng.wait_ge(pe.h, w)
                        eng.dma_start(out=guw[:, par * (16 * 256):(par + 1) * (16 * 256)
                                              ].rearrange("p (k c) -> p k c", c=256),
                                      in_=wv).then_inc(cast.h, 16)
                    self.em("gpsimd", f)
                    gw_done = cast.inc(16)

                    for tt in range(TH // 512):
                        def f(eng, par=par, tt=tt, w=gw_done,
                              pf0=self.pfree.get("pA0"), pf1=self.pfree.get("pA1")):
                            eng.wait_ge(cast.h, w)
                            if pf0 is not None:
                                eng.wait_ge(pf0[0], pf0[1])
                            if pf1 is not None:
                                eng.wait_ge(pf1[0], pf1[1])
                            base = par * (16 * 256)
                            ins = None
                            for k in range(16):
                                eng.matmul(pA[0][:, :],
                                           guw[:, base + k * 256:base + k * 256 + 128],
                                           x1n[:, k * TH + tt * 512:k * TH + (tt + 1) * 512],
                                           start=(k == 0), stop=(k == 15))
                            for k in range(16):
                                ins = eng.matmul(
                                    pA[1][:, :],
                                    guw[:, base + k * 256 + 128:base + k * 256 + 256],
                                    x1n[:, k * TH + tt * 512:k * TH + (tt + 1) * 512],
                                    start=(k == 0), stop=(k == 15))
                            ins.then_inc(pe.h, 1)
                        self.em("tensor", f)
                        gmm = pe.inc(1)
                        guw_use[par] = gmm

                        def f(eng, w=gmm, sg=sg_use[0]):
                            eng.wait_ge(pe.h, w)
                            if sg is not None:
                                eng.wait_ge(sg[0], sg[1])
                            eng.activation(out=sgst[:, :], in_=pA[0][:, :], func=AF.Silu
                                           ).then_inc(act.h, 1)
                        self.em("scalar", f)
                        sil = act.inc(1)
                        self.puse("pA0", act.h, sil)

                        par2 = tt % 2

                        def f(eng, w=sil, par2=par2, hu=hst_use[par2]):
                            eng.wait_ge(act.h, w)
                            if hu is not None:
                                eng.wait_ge(hu[0], hu[1])
                            eng.tensor_mul(hstg[:, par2 * 512:par2 * 512 + 512],
                                           sgst[:, :], pA[1][:, :]).then_inc(dve.h, 1)
                        self.em("vector", f)
                        hmul = dve.inc(1)
                        self.puse("pA1", dve.h, hmul)
                        sg_use[0] = (dve.h, hmul)

                        def f(eng, pb=pb, tt=tt, par2=par2, w=hmul):
                            eng.wait_ge(dve.h, w)
                            eng.dma_start(out=D["hD"][pb * 128:(pb + 1) * 128,
                                                      tt * 512:(tt + 1) * 512],
                                          in_=hstg[:, par2 * 512:par2 * 512 + 512]
                                          ).then_inc(stm.h, 16)
                        self.em("sync", f)
                        hst_use[par2] = (stm.h, stm.inc(16))

        self.barrier()

        with (
            nc.sbuf_tensor("dw", [128, 64 * 512], BF16) as dw,
            nc.sbuf_tensor("hp", [128, 2 * 8 * 512], BF16) as hp,
            nc.sbuf_tensor("dnst", [128, 4 * 512], F32) as dnst,
        ):
            dn_st = [None] * 4
            hp_mm = [0, 0]
            for qd in range(4):
                wv = D["wdn"][:, qd * 512:(qd + 1) * 512].rearrange("(k p) c -> p k c", p=128)

                def f(eng, wv=wv, w=pe.n):
                    eng.wait_ge(pe.h, w)
                    eng.dma_start(out=dw[:, :].rearrange("p (k c) -> p k c", c=512),
                                  in_=wv).then_inc(cast.h, 16)
                self.em("gpsimd", f)
                dw_done = cast.inc(16)

                for tt in range(TH // 512):
                    hp_done = []
                    for pc in range(8):
                        par = pc % 2
                        hv = D["hD"][pc * 1024:(pc + 1) * 1024, tt * 512:(tt + 1) * 512
                                     ].rearrange("(k p) t -> p k t", p=128)
                        slot_mm = hp_mm[par] if True else 0

                        def f(eng, hv=hv, par=par, w=hp_mm[par]):
                            eng.wait_ge(pe.h, w)
                            eng.dma_start(out=hp[:, par * (8 * 512):(par + 1) * (8 * 512)
                                                  ].rearrange("p (k t) -> p k t", t=512),
                                          in_=hv).then_inc(ld.h, 16)
                        self.em("sync", f)
                        hp_done.append(ld.inc(16))

                        def f(eng, par=par, pc=pc, qd=qd, w=hp_done[-1], ww=dw_done,
                              pfs=tuple(self.pfree.get("pA%d" % b) for b in range(4))):
                            eng.wait_ge(ld.h, w)
                            eng.wait_ge(cast.h, ww)
                            if pc == 0:
                                for pf in pfs:
                                    if pf is not None:
                                        eng.wait_ge(pf[0], pf[1])
                            base = par * (8 * 512)
                            ins = None
                            for hb in range(4):
                                for kk in range(8):
                                    k = pc * 8 + kk
                                    ins = eng.matmul(
                                        pA[hb][:, :],
                                        dw[:, k * 512 + hb * 128:k * 512 + (hb + 1) * 128],
                                        hp[:, base + kk * 512:base + (kk + 1) * 512],
                                        start=(k == 0), stop=(k == 63))
                            ins.then_inc(pe.h, 1)
                        self.em("tensor", f)
                        hp_mm[par] = pe.inc(1)
                    dn_mm = pe.n

                    for hb in range(4):
                        def f(eng, hb=hb, w=dn_mm, ds=dn_st[hb]):
                            eng.wait_ge(pe.h, w)
                            if ds is not None:
                                eng.wait_ge(ds[0], ds[1])
                            eng.copy(dnst[:, hb * 512:(hb + 1) * 512], pA[hb][:, :]
                                     ).then_inc(act.h, 1)
                        self.em("scalar", f)
                        ev = act.inc(1)
                        self.puse("pA%d" % hb, act.h, ev)

                        def f(eng, hb=hb, qd=qd, tt=tt, w=ev):
                            eng.wait_ge(act.h, w)
                            eng.dma_start(out=D["dnD"][qd * 512 + hb * 128:qd * 512 + (hb + 1) * 128,
                                                       tt * 512:(tt + 1) * 512],
                                          in_=dnst[:, hb * 512:(hb + 1) * 512]).then_inc(stm.h, 16)
                        self.em("sync", f)
                        dn_st[hb] = (stm.h, stm.inc(16))

        self.barrier()

        with (
            nc.sbuf_tensor("dn_t", [128, 16 * 512], F32) as dn_t,
            nc.sbuf_tensor("xn_t", [128, 16 * 512], BF16) as xn_t,
            nc.sbuf_tensor("x2f", [128, 16 * 512], F32) as x2f,
            nc.sbuf_tensor("sqf", [128, 16 * 512], F32) as sqf,
            nc.sbuf_tensor("rows2", [1, 4 * 512], F32) as rows2,
            nc.sbuf_tensor("ostg", [128, 16 * 512], F32) as ostg,
        ):
            for tt in range(TH // 512):
                dnv = D["dnD"][:, tt * 512:(tt + 1) * 512].rearrange("(k p) t -> p k t", p=128)
                xnv = D["x1nD"][:, tt * 512:(tt + 1) * 512].rearrange("(k p) t -> p k t", p=128)

                def f(eng, dnv=dnv, xnv=xnv, wp=pool.n, wpe=pe.n):
                    eng.wait_ge(pool.h, wp)
                    eng.dma_start(out=dn_t[:, :].rearrange("p (k t) -> p k t", t=512),
                                  in_=dnv).then_inc(ld.h, 16)
                    eng.dma_start(out=xn_t[:, :].rearrange("p (k t) -> p k t", t=512),
                                  in_=xnv).then_inc(ld.h, 16)
                self.em("sync", f)
                t_rdy = ld.inc(32)

                def f(eng, w=t_rdy, wd=dve.n):
                    eng.wait_ge(ld.h, w)
                    eng.wait_ge(dve.h, wd)
                    last = None
                    for k in range(16):
                        last = eng.tensor_add(x2f[:, k * 512:(k + 1) * 512],
                                              dn_t[:, k * 512:(k + 1) * 512],
                                              xn_t[:, k * 512:(k + 1) * 512])
                    last.then_inc(pool.h, 1)
                self.em("gpsimd", f)
                x2_done = pool.inc(1)

                def f(eng, w=x2_done, pf0=self.pfree.get("pB0"), OCF=self.C_OCOLF):
                    eng.wait_ge(pool.h, w)
                    if pf0 is not None:
                        eng.wait_ge(pf0[0], pf0[1])
                    ins = None
                    for k in range(16):
                        ins = eng.matmul(pB[0][0:1, 0:512], OCF, x2f[:, k * 512:(k + 1) * 512],
                                         start=(k == 0), stop=(k == 15))
                    ins.then_inc(pe.h, 1)
                self.em("tensor", f)
                summ = pe.inc(1)

                def f(eng, w=x2_done, wp=pe.n):
                    eng.wait_ge(pool.h, w)
                    eng.wait_ge(pe.h, wp)
                    last = None
                    for k in range(16):
                        last = eng.activation(out=sqf[:, k * 512:(k + 1) * 512],
                                              in_=x2f[:, k * 512:(k + 1) * 512],
                                              func=AF.Square)
                    last.then_inc(act.h, 1)
                self.em("scalar", f)
                sqb = act.inc(1)

                def f(eng, w=sqb, pf1=self.pfree.get("pB1"), OCF=self.C_OCOLF):
                    eng.wait_ge(act.h, w)
                    if pf1 is not None:
                        eng.wait_ge(pf1[0], pf1[1])
                    ins = None
                    for k in range(16):
                        ins = eng.matmul(pB[1][0:1, 0:512], OCF, sqf[:, k * 512:(k + 1) * 512],
                                         start=(k == 0), stop=(k == 15))
                    ins.then_inc(pe.h, 1)
                self.em("tensor", f)
                sq_mm = pe.inc(1)

                def f(eng, a=summ, b=sq_mm):
                    eng.wait_ge(pe.h, max(a, b))
                    eng.activation(out=rows2[0:1, 0:512], in_=pB[0][0:1, 0:512],
                                   func=AF.Copy, scale=self.CF_I2048[0:1, :])
                    eng.activation(out=rows2[0:1, 512:1024], in_=rows2[0:1, 0:512],
                                   func=AF.Square).then_inc(act.h, 1)
                self.em("scalar", f)
                mrow = act.inc(1)

                def f(eng, w=mrow):
                    eng.wait_ge(act.h, w)
                    eng.scalar_tensor_tensor(rows2[0:1, 1024:1536], pB[1][0:1, 0:512],
                                             1.0 / 2048.0, rows2[0:1, 512:1024],
                                             ALU.mult, ALU.subtract).then_inc(dve.h, 1)
                self.em("vector", f)
                varr = dve.inc(1)
                self.puse("pB1", dve.h, varr)

                def f(eng, w=varr):
                    eng.wait_ge(dve.h, w)
                    eng.activation(out=rows2[0:1, 1536:2048], in_=rows2[0:1, 1024:1536],
                                   func=AF.Ln, bias=self.CF_EPS5[0:1, :])
                    eng.activation(out=rows2[0:1, 1536:2048], in_=rows2[0:1, 1536:2048],
                                   func=AF.Exp, scale=-0.5).then_inc(act.h, 1)
                self.em("scalar", f)
                rrow = act.inc(1)

                def f(eng, w=rrow, ORF=self.CF_OROWF):
                    eng.wait_ge(act.h, w)
                    eng.matmul(pB[0][:, 0:512], ORF, rows2[0:1, 0:512], start=True, stop=True)
                    eng.matmul(pB[1][:, 0:512], ORF, rows2[0:1, 1536:2048], start=True,
                               stop=True).then_inc(pe.h, 1)
                self.em("tensor", f)
                bcs = pe.inc(1)

                def f(eng, w=bcs, ws=stm.n):
                    eng.wait_ge(pe.h, w)
                    eng.wait_ge(stm.h, ws)
                    last = None
                    for k in range(16):
                        dst = ostg[:, k * 512:(k + 1) * 512]
                        last = eng.tensor_sub(dst, x2f[:, k * 512:(k + 1) * 512], pB[0][:, 0:512])
                        last = eng.tensor_mul(dst, dst, pB[1][:, 0:512])
                    last.then_inc(dve.h, 1)
                self.em("vector", f)
                odone = dve.inc(1)
                self.puse("pB0", dve.h, odone)
                self.puse("pB1", dve.h, odone)

                ov = D["outD"][:, tt * 512:(tt + 1) * 512].rearrange("(k p) t -> p k t", p=128)

                def f(eng, ov=ov, w=odone):
                    eng.wait_ge(dve.h, w)
                    eng.dma_start(out=ov, in_=ostg[:, :].rearrange("p (k t) -> p k t", t=512)
                                  ).then_inc(stm.h, 16)
                self.em("sync", f)
                stm.inc(16)


def build_nc(debug_taps=(), stop_after=None):
    return KB(debug_taps, stop_after).build()


def kernel(**inputs):
    consts = _host_consts()
    in_maps = [_host_core_inputs(c, inputs, consts) for c in range(8)]
    nc = build_nc()
    res = run_bass_kernel_spmd(nc, in_maps, core_ids=list(range(8)))
    out = np.empty((4, S, H), np.float32)
    for c in range(8):
        b, hg = c // 2, c % 2
        out[b, hg * TH:(hg + 1) * TH, :] = np.asarray(res.results[c]["out"]).T
    return out


# version 06
# speedup vs baseline: 1.3442x; 1.0651x over previous
import contextlib
import numpy as np
import ml_dtypes
import concourse.bass as bass
import concourse.mybir as mybir
from concourse.bass_utils import run_bass_kernel_spmd

F32 = mybir.dt.float32
BF16 = mybir.dt.bfloat16
FP16 = mybir.dt.float16
AF = mybir.ActivationFunctionType
ALU = mybir.AluOpType
BFnp = ml_dtypes.bfloat16

S = 8192
H = 2048
NH = 16
DK = DV = 128
CH = 128
NCHUNK = S // CH
SLAB = 256
NSLAB = S // SLAB
PCOLS = 6176
INTER = 8192
TH = S // 2
MP = SLAB + 8

REPL = [[0, 1], [2, 3], [4, 5], [6, 7]]
LN128H = 0.5 * float(np.log(128.0))
RSQ128 = float(128.0 ** -0.5)


def _host_consts():
    ident = np.eye(128, dtype=BFnp)
    identf = np.eye(128, dtype=np.float32)
    m_us_neg = (-np.triu(np.ones((128, 128), np.float32), 1)).astype(BFnp)
    m_ui = np.triu(np.ones((128, 128), np.float32), 0).astype(BFnp)
    indt = np.zeros((64, 4096), np.float16)
    for h in range(16):
        for off in (0, 2048):
            indt[h, off + h * 128:off + (h + 1) * 128] = -1.0
            indt[32 + h, off + h * 128:off + (h + 1) * 128] = -1.0
    oh16 = np.zeros((128, 256), np.float32)
    for cb in range(16):
        oh16[:, cb * 16 + cb] = 1.0
    return {"IDENT": ident, "IDENTF": identf, "MASKUSN": m_us_neg, "MASKUI": m_ui,
            "INDT": indt, "OH16": oh16.astype(BFnp),
            "ONESROW": np.ones((1, 128), BFnp), "ONESCOL": np.ones((128, 1), BFnp),
            "ONESROWF": np.ones((1, 128), np.float32),
            "ONESCOLF": np.ones((128, 1), np.float32),
            "ONES2R": np.ones((2, 128), np.float16)}


def _host_core_inputs(core, inp, consts):
    b, hg = core // 2, core % 2
    w_qkv = np.asarray(inp["L0_attn_w_qkv"])
    w_z = np.asarray(inp["L0_attn_w_z"])
    w_b = np.asarray(inp["L0_attn_w_b"])
    w_a = np.asarray(inp["L0_attn_w_a"])
    m = dict(consts)
    xTfull = np.ascontiguousarray(np.asarray(inp["L0_attn_hidden_states"])[b].T)
    m["xT"] = xTfull
    m["xTd"] = np.ascontiguousarray(xTfull[:, hg * TH:(hg + 1) * TH])
    m["wproj"] = np.ascontiguousarray(np.concatenate([
        w_qkv[:, hg * 1024:(hg + 1) * 1024],
        w_qkv[:, 2048 + hg * 1024:2048 + (hg + 1) * 1024],
        w_qkv[:, 4096 + hg * 2048:4096 + (hg + 1) * 2048],
        w_z[:, hg * 2048:(hg + 1) * 2048],
        w_b[:, hg * 16:(hg + 1) * 16],
        w_a[:, hg * 16:(hg + 1) * 16]], axis=1))
    cw = np.asarray(inp["L0_attn_conv_w"])[:, 0, :]
    ch_rows = np.concatenate([
        cw[hg * 1024:(hg + 1) * 1024],
        cw[2048 + hg * 1024:2048 + (hg + 1) * 1024],
        cw[4096 + hg * 2048:4096 + (hg + 1) * 2048]], axis=0)
    m["convw"] = np.ascontiguousarray(
        ch_rows.reshape(32, 128, 4).transpose(1, 0, 2).reshape(128, 128))
    m["wout"] = np.ascontiguousarray(np.asarray(inp["L0_attn_w_out"])[hg * 2048:(hg + 1) * 2048, :])
    wgu_in = np.asarray(inp["L0_ffn_w_gate_up"])
    wg = wgu_in[:INTER].T
    wu = wgu_in[INTER:].T
    gu = np.empty((H, 2 * INTER), np.float32)
    for i in range(INTER // 128):
        gu[:, (2 * i) * 128:(2 * i + 1) * 128] = wg[:, i * 128:(i + 1) * 128]
        gu[:, (2 * i + 1) * 128:(2 * i + 2) * 128] = wu[:, i * 128:(i + 1) * 128]
    m["wgu"] = np.ascontiguousarray(gu)
    m["wdn"] = np.ascontiguousarray(np.asarray(inp["L0_ffn_w_down"]).T)
    m["dtb"] = np.ascontiguousarray(
        np.asarray(inp["L0_attn_dt_bias"]).astype(np.float32)[hg * 16:(hg + 1) * 16].reshape(16, 1))
    m["alog"] = np.ascontiguousarray(
        np.asarray(inp["L0_attn_A_log"]).astype(np.float32)[hg * 16:(hg + 1) * 16].reshape(16, 1))
    m["nrmw"] = np.ascontiguousarray(
        np.asarray(inp["L0_attn_norm_w"]).astype(np.float32).reshape(128, 1))
    return m


class Sem:
    def __init__(self, handle):
        self.h = handle
        self.n = 0

    def inc(self, k=1):
        self.n += k
        return self.n


class KB:
    def __init__(self, debug_taps=(), stop_after=None):
        self.debug_taps = set(debug_taps)
        self.stop_after = stop_after
        self.nc = bass.Bass("TRN2", target_bir_lowering=False, debug=False)
        self.Q = {e: [] for e in ("tensor", "vector", "scalar", "gpsimd", "sync")}
        self.st_ = {}
        self.pfree = {}

    def em(self, e, fn):
        self.Q[e].append(fn)

    def dram(self, name, shape, dtype):
        k = "ExternalOutput" if name in self.debug_taps else "Internal"
        return self.nc.dram_tensor(name, shape, dtype, kind=k).ap()

    def pw(self, eng, name):
        v = self.pfree.get(name)
        if v is not None:
            eng.wait_ge(v[0], v[1])

    def puse(self, name, sem, val):
        self.pfree[name] = (sem, val)

    def dump(self, name, ap, shape, dtype):
        if name not in self.debug_taps:
            return
        D = self.st_
        if name not in D:
            D[name] = self.nc.dram_tensor(name, shape, dtype, kind="ExternalOutput").ap()
        waits = [(s.h, s.n) for s in (self.pe, self.dve, self.act, self.pool) if s.n > 0]
        dst = D[name]

        def f(eng, waits=tuple(waits), dst=dst, ap=ap):
            for h, v in waits:
                eng.wait_ge(h, v)
            eng.dma_start(out=dst, in_=ap).then_inc(self.stm.h, 16)
        self.em("sync", f)
        self.stm.inc(16)

    def barrier(self):
        waits = [(s.h, s.n) for s in
                 (self.pe, self.dve, self.act, self.pool, self.ld, self.cast,
                  self.stm, self.sm, self.ccm) if s.n > 0]
        for e in self.Q:
            def f(eng, waits=tuple(waits)):
                for h, v in waits:
                    eng.wait_ge(h, v)
            self.em(e, f)
        self.pfree = {}

    def build(self):
        nc = self.nc
        with contextlib.ExitStack() as es:
            self._declare(es)
            with contextlib.ExitStack() as es_ps:
                self._declare_ps(es_ps)
                self._setup()
                self._phase_P()
                if self.stop_after == "P":
                    es_ps_hold = True
                else:
                    self._phase_S()
            if self.stop_after not in ("P", "S"):
                self._phase_D()

            stm, sm, cast, ld, ccm = self.stm, self.sm, self.cast, self.ld, self.ccm

            def f(eng, a=stm.n, b=sm.n, c=cast.n, d=ld.n, e=ccm.n):
                eng.wait_ge(stm.h, a)
                eng.wait_ge(sm.h, b)
                eng.wait_ge(cast.h, c)
                eng.wait_ge(ld.h, d)
                if e:
                    eng.wait_ge(ccm.h, e)
            self.em("sync", f)

            blk = es.enter_context(nc.Block())
            Q = self.Q

            @blk.tensor
            def _(eng):
                for fn in Q["tensor"]:
                    fn(eng)

            @blk.vector
            def _(eng):
                for fn in Q["vector"]:
                    fn(eng)

            @blk.scalar
            def _(eng):
                for fn in Q["scalar"]:
                    fn(eng)

            @blk.gpsimd
            def _(eng):
                for fn in Q["gpsimd"]:
                    fn(eng)

            @blk.sync
            def _(eng):
                for fn in Q["sync"]:
                    fn(eng)
        return nc

    def _declare(self, es):
        nc, D = self.nc, self.st_
        for nm, sh, dt in (
            ("xT", [H, S], F32), ("xTd", [H, TH], F32), ("wproj", [H, PCOLS], F32),
            ("convw", [128, 128], F32), ("wout", [H, H], F32), ("wgu", [H, 2 * INTER], F32),
            ("wdn", [INTER, H], F32), ("dtb", [16, 1], F32), ("alog", [16, 1], F32),
            ("nrmw", [128, 1], F32), ("IDENT", [128, 128], BF16), ("IDENTF", [128, 128], F32),
            ("MASKUSN", [128, 128], BF16), ("MASKUI", [128, 128], BF16),
            ("INDT", [64, 4096], FP16), ("OH16", [128, 256], BF16),
            ("ONESROW", [1, 128], BF16), ("ONESCOL", [128, 1], BF16),
            ("ONESROWF", [1, 128], F32), ("ONESCOLF", [128, 1], F32),
            ("ONES2R", [2, 128], FP16),
        ):
            D[nm] = nc.dram_tensor(nm, sh, dt, kind="ExternalInput").ap()
        D["mixedD"] = self.dram("mixedD", [4096, S], BF16)
        D["zsD"] = self.dram("zsD", [H, S], BF16)
        D["onD"] = self.dram("onD", [H, S], BF16)
        D["rsIn"] = self.dram("rsIn", [2 * H, TH], BF16)
        D["rsOut"] = self.dram("rsOut", [H, TH], BF16)
        D["hD"] = self.dram("hD", [INTER, TH], BF16)
        D["x1nD"] = self.dram("x1nD", [H, TH], BF16)
        D["dnD"] = self.dram("dnD", [H, TH], F32)
        D["smallsD"] = self.dram("smallsD", [128, S], F32)
        D["csplitD"] = self.dram("csplitD", [48, S], FP16)
        D["cbsplitD"] = self.dram("cbsplitD", [48, S], FP16)
        D["cD"] = self.dram("cD", [16, S], F32)
        D["l2rD"] = self.dram("l2rD", [16, SLAB], F32)
        D["outD"] = nc.dram_tensor("out", [H, TH], F32, kind="ExternalOutput").ap()

        for nm, sh, dt in (
            ("cons", [128, 1024], BF16), ("consf", [128, 272], F32),
            ("identf", [128, 128], F32), ("ones2", [2, 128], FP16),
            ("onescf", [128, 1], F32),
        ):
            D[nm] = es.enter_context(nc.sbuf_tensor(nm, sh, dt))
        for nm, sh, dt in (
            ("pA0", [128, 512], F32), ("pA1", [128, 512], F32),
            ("pA2", [128, 512], F32), ("pA3", [128, 512], F32),
            ("pB0", [128, 512], F32), ("pB1", [128, 512], F32),
            ("pT0", [128, 1024], BF16), ("pT1", [128, 1024], BF16),
        ):
            D[nm] = es.enter_context(nc.psum_tensor(nm, sh, dt))
        self.pA = [D["pA0"], D["pA1"], D["pA2"], D["pA3"]]
        self.pB = [D["pB0"], D["pB1"]]
        self.pT = [D["pT0"], D["pT1"]]
        for nm in ("pe", "dve", "act", "pool", "ld", "cast", "stm", "sm", "ccm"):
            setattr(self, nm, Sem(es.enter_context(nc.semaphore("s_" + nm))))
        cons, consf = D["cons"], D["consf"]
        self.C_ID = cons[:, 0:128]
        self.C_MUSN = cons[:, 128:256]
        self.C_MUI = cons[:, 256:384]
        self.C_OH = lambda cb: cons[:, 384 + cb * 16:384 + (cb + 1) * 16]
        self.C_OCOL = cons[:, 640:641]
        self.C_OROW = cons[0:1, 642:770]
        self.CF_CONV = lambda cb, j: consf[:, cb * 4 + j:cb * 4 + j + 1]
        self.CF_DTB = consf[0:16, 128:129]
        self.CF_ALOG = consf[0:16, 129:130]
        self.CF_NEGEA = consf[0:16, 130:131]
        self.CF_NRMW = consf[:, 131:132]
        self.CF_OROWF = consf[0:1, 132:260]
        self.CF_DTB32 = consf[32:48, 260:261]
        self.CF_NEGEA96 = consf[96:112, 261:262]
        self.CF_EPS6 = consf[:, 262:263]
        self.CF_RMSE = consf[:, 263:264]
        self.CF_LN128H = consf[:, 264:265]
        self.CF_EPS5 = consf[:, 265:266]
        self.CF_I2048 = consf[:, 266:267]
        self.C_IDF = D["identf"][:, :]
        self.C_OCOLF = D["onescf"][:, :]

    def _declare_ps(self, es_ps):
        nc, D = self.nc, self.st_
        for nm, sh, dt in (
            ("persF", [128, S], F32), ("csplit", [66, S], FP16), ("cbsplit", [66, S], FP16),
        ):
            D[nm] = es_ps.enter_context(nc.sbuf_tensor(nm, sh, dt))

    def _setup(self):
        D = self.st_
        ld, act, dve = self.ld, self.act, self.dve
        cons, consf = D["cons"], D["consf"]

        def f(eng):
            eng.dma_start(out=cons[:, 0:128], in_=D["IDENT"][:, :]).then_inc(ld.h, 16)
            eng.dma_start(out=cons[:, 128:256], in_=D["MASKUSN"][:, :]).then_inc(ld.h, 16)
            eng.dma_start(out=cons[:, 256:384], in_=D["MASKUI"][:, :]).then_inc(ld.h, 16)
            eng.dma_start(out=cons[:, 384:640], in_=D["OH16"][:, :]).then_inc(ld.h, 16)
            eng.dma_start(out=cons[:, 640:641], in_=D["ONESCOL"][:, :]).then_inc(ld.h, 16)
            eng.dma_start(out=cons[0:1, 642:770], in_=D["ONESROW"][:, :]).then_inc(ld.h, 16)
            eng.dma_start(out=consf[:, 0:128], in_=D["convw"][:, :]).then_inc(ld.h, 16)
            eng.dma_start(out=consf[0:16, 128:129], in_=D["dtb"][:, :]).then_inc(ld.h, 16)
            eng.dma_start(out=consf[0:16, 129:130], in_=D["alog"][:, :]).then_inc(ld.h, 16)
            eng.dma_start(out=consf[:, 131:132], in_=D["nrmw"][:, :]).then_inc(ld.h, 16)
            eng.dma_start(out=consf[0:1, 132:260], in_=D["ONESROWF"][:, :]).then_inc(ld.h, 16)
            eng.dma_start(out=D["ones2"][:, :], in_=D["ONES2R"][:, :]).then_inc(ld.h, 16)
            eng.dma_start(out=D["identf"][:, :], in_=D["IDENTF"][:, :]).then_inc(ld.h, 16)
            eng.dma_start(out=D["onescf"][:, :], in_=D["ONESCOLF"][:, :]).then_inc(ld.h, 16)
        self.em("sync", f)
        setup_ld = ld.inc(16 * 14)

        def f(eng, w=setup_ld):
            eng.wait_ge(ld.h, w)
            eng.memset(consf[:, 262:263], 1e-6)
            eng.memset(consf[:, 263:264], 128.0 * 1e-6)
            eng.memset(consf[:, 264:265], LN128H)
            eng.memset(consf[:, 265:266], 1e-5)
            eng.memset(consf[:, 266:267], 1.0 / 2048.0)
            eng.memset(D["csplit"][0:32, :], 0.0)
            eng.memset(D["csplit"][32:64, :], 0.0)
            eng.memset(D["csplit"][64:66, :], 1.0)
            eng.memset(D["cbsplit"][0:32, :], 0.0)
            eng.memset(D["cbsplit"][32:64, :], 0.0)
            eng.memset(D["cbsplit"][64:66, :], -1.0).then_inc(dve.h, 1)
        self.em("vector", f)
        dve.inc(1)

        def f(eng, w=setup_ld):
            eng.wait_ge(ld.h, w)
            eng.activation(out=consf[0:16, 130:131], in_=consf[0:16, 129:130],
                           func=AF.Exp).then_inc(act.h, 1)
        self.em("scalar", f)
        ea = act.inc(1)

        def f(eng, w=ea, NEG=self.CF_NEGEA):
            eng.wait_ge(act.h, w)
            eng.tensor_scalar_mul(NEG, NEG, -1.0)
            eng.tensor_copy(consf[32:48, 260:261], consf[0:16, 128:129])
            eng.tensor_copy(consf[96:112, 261:262], NEG).then_inc(dve.h, 1)
        self.em("vector", f)
        dve.inc(1)

    def _phase_P(self):
        nc, D = self.nc, self.st_
        pe, dve, act, ld, cast, stm, sm = (self.pe, self.dve, self.act, self.ld,
                                           self.cast, self.stm, self.sm)
        pA = self.pA
        QT = 2048
        NQ = S // QT
        FG = 256
        fgroups = [(i * FG, FG) for i in range(24)] + [(6144, 16), (6160, 16)]

        with (
            nc.sbuf_tensor("x_q", [128, 16 * QT], BF16) as x_q,
            nc.sbuf_tensor("wwin", [128, 2 * 16 * FG], BF16) as wwin,
            nc.sbuf_tensor("stg", [128, 4 * 512], BF16) as stg,
            nc.sbuf_tensor("traF", [128, S], F32) as traF,
        ):
            stg_free = [None] * 4
            pA_free = [None] * 4
            wwin_use = [0, 0]
            xq_use = [0]
            rr = [0]

            for q in range(NQ):
                t_base = q * QT
                xv = D["xT"][:, t_base:t_base + QT].rearrange("(k p) t -> p k t", p=128)

                def f(eng, xv=xv, xw=xq_use[0]):
                    eng.wait_ge(pe.h, xw)
                    eng.dma_start(out=x_q[:, :].rearrange("p (k t) -> p k t", t=QT),
                                  in_=xv).then_inc(cast.h, 16)
                self.em("gpsimd", f)
                xq_done = cast.inc(16)

                for gi, (c0, cw) in enumerate(fgroups):
                    par = gi % 2
                    wv = D["wproj"][:, c0:c0 + cw].rearrange("(k p) f -> p k f", p=128)

                    def f(eng, wv=wv, par=par, cw=cw, wprev=wwin_use[par]):
                        eng.wait_ge(pe.h, wprev)
                        eng.dma_start(out=wwin[:, par * (16 * FG):par * (16 * FG) + 16 * cw
                                              ].rearrange("p (k f) -> p k f", f=cw),
                                      in_=wv).then_inc(cast.h, 16)
                    self.em("gpsimd", f)
                    wwin_done = cast.inc(16)

                    for fb in range(max(1, cw // 128)):
                        fbw = min(128, cw)
                        fcol = c0 + fb * 128
                        is_z = 4096 <= fcol < 6144
                        is_b = fcol == 6144
                        is_a = fcol == 6160
                        for tt in range(QT // 512):
                            bank = rr[0] % 4
                            sgi = rr[0] % 4
                            rr[0] += 1

                            def f(eng, bank=bank, par=par, fb=fb, fbw=fbw, tt=tt, cw=cw,
                                  wd=wwin_done, xd=xq_done, pf=pA_free[bank]):
                                eng.wait_ge(cast.h, max(wd, xd))
                                if pf is not None:
                                    eng.wait_ge(pf[0], pf[1])
                                base = par * (16 * FG)
                                ins = None
                                for k in range(16):
                                    ins = eng.matmul(
                                        pA[bank][0:fbw, :],
                                        wwin[:, base + k * cw + fb * 128:
                                             base + k * cw + fb * 128 + fbw],
                                        x_q[:, k * QT + tt * 512:k * QT + (tt + 1) * 512],
                                        start=(k == 0), stop=(k == 15))
                                ins.then_inc(pe.h, 1)
                            self.em("tensor", f)
                            mm_done = pe.inc(1)
                            wwin_use[par] = mm_done
                            xq_use[0] = mm_done

                            def f(eng, bank=bank, sgi=sgi, md=mm_done, sf=stg_free[sgi],
                                  is_z=is_z, is_b=is_b, is_a=is_a, fbw=fbw, tt=tt, t_base=t_base):
                                eng.wait_ge(pe.h, md)
                                if sf is not None and not (is_b or is_a):
                                    eng.wait_ge(sf[0], sf[1])
                                tsl = slice(t_base + tt * 512, t_base + (tt + 1) * 512)
                                sg = stg[:, sgi * 512:(sgi + 1) * 512]
                                if is_b:
                                    eng.copy(traF[0:16, tsl], pA[bank][0:16, :]).then_inc(act.h, 1)
                                elif is_a:
                                    eng.copy(traF[32:48, tsl], pA[bank][0:16, :]).then_inc(act.h, 1)
                                elif is_z:
                                    eng.activation(out=sg[0:fbw, :], in_=pA[bank][0:fbw, :],
                                                   func=AF.Silu).then_inc(act.h, 1)
                                else:
                                    eng.copy(sg[0:fbw, :], pA[bank][0:fbw, :]).then_inc(act.h, 1)
                            self.em("scalar", f)
                            ev = act.inc(1)
                            pA_free[bank] = (act.h, ev)

                            if not (is_b or is_a):
                                dst = (D["zsD"][fcol - 4096:fcol - 4096 + fbw,
                                                t_base + tt * 512:t_base + (tt + 1) * 512]
                                       if is_z else
                                       D["mixedD"][fcol:fcol + fbw,
                                                   t_base + tt * 512:t_base + (tt + 1) * 512])

                                def f(eng, dst=dst, sgi=sgi, ev=ev, fbw=fbw):
                                    eng.wait_ge(act.h, ev)
                                    eng.dma_start(out=dst,
                                                  in_=stg[:, sgi * 512:(sgi + 1) * 512][0:fbw, :]
                                                  ).then_inc(stm.h, 16)
                                self.em("sync", f)
                                stg_free[sgi] = (stm.h, stm.inc(16))

            persF, csplit, cbsplit = D["persF"], D["csplit"], D["cbsplit"]

            def f(eng, w=act.n, DTB32=self.CF_DTB32):
                eng.wait_ge(act.h, w)
                b_ = traF[0:16, :]
                a_ = traF[32:48, :]
                eng.activation(out=persF[32:48, :], in_=b_, func=AF.Sigmoid)
                eng.activation(out=persF[64:80, :], in_=b_, func=AF.Abs)
                eng.activation(out=persF[64:80, :], in_=persF[64:80, :], func=AF.Exp, scale=-1.0)
                eng.activation(out=persF[64:80, :], in_=persF[64:80, :], func=AF.Ln, bias=1.0)
                eng.activation(out=persF[96:112, :], in_=a_, func=AF.Abs, bias=DTB32)
                eng.activation(out=persF[96:112, :], in_=persF[96:112, :], func=AF.Exp, scale=-1.0)
                eng.activation(out=persF[96:112, :], in_=persF[96:112, :], func=AF.Ln, bias=1.0
                               ).then_inc(act.h, 1)
            self.em("scalar", f)
            sp_done = act.inc(1)

            def f(eng, w=sp_done, DTB32=self.CF_DTB32, NEG96=self.CF_NEGEA96):
                eng.wait_ge(act.h, w)
                b_ = traF[0:16, :]
                a_ = traF[32:48, :]
                eng.tensor_scalar(traF[64:80, :], b_, -1.0, 0.0, ALU.mult, ALU.max)
                eng.scalar_tensor_tensor(persF[64:80, :], traF[64:80, :], -1.0, persF[64:80, :],
                                         ALU.mult, ALU.subtract)
                eng.tensor_scalar(traF[96:112, :], a_, DTB32, 0.0, ALU.add, ALU.max)
                eng.tensor_add(persF[96:112, :], traF[96:112, :], persF[96:112, :])
                eng.tensor_scalar_mul(persF[96:112, :], persF[96:112, :], NEG96)
                eng.memset(traF[96:112, 0:CH], 1.0)
                for cix in range(NCHUNK):
                    sl = slice(cix * CH, (cix + 1) * CH)
                    eng.tensor_tensor_scan(persF[0:16, sl], traF[96:112, 0:CH],
                                           persF[96:112, sl], 0.0, ALU.mult, ALU.add)
                eng.tensor_copy(traF[64:80, :], persF[0:16, :])
                eng.tensor_add(traF[64:80, :], traF[64:80, :], persF[64:80, :])
                eng.tensor_copy(csplit[0:16, :], persF[0:16, :])
                eng.tensor_copy(cbsplit[0:16, :], traF[64:80, :])
                eng.tensor_copy(traF[0:16, :], csplit[0:16, :])
                eng.tensor_sub(csplit[32:48, :], persF[0:16, :], traF[0:16, :])
                eng.tensor_copy(persF[64:80, :], cbsplit[0:16, :])
                eng.tensor_sub(cbsplit[32:48, :], traF[64:80, :], persF[64:80, :]
                               ).then_inc(dve.h, 1)
            self.em("vector", f)
            split_done = dve.inc(1)

            def f(eng, w=split_done):
                eng.wait_ge(dve.h, w)
                eng.dma_start(out=D["csplitD"][:, :], in_=D["csplit"][0:48, :]).then_inc(sm.h, 16)
                eng.dma_start(out=D["cbsplitD"][:, :], in_=D["cbsplit"][0:48, :]).then_inc(sm.h, 16)
                eng.dma_start(out=D["cD"][:, :], in_=persF[0:16, :]).then_inc(sm.h, 16)
            self.em("scalar", f)
            cstores = sm.inc(48)

            if "smallsD" in self.debug_taps:
                def f(eng, a=dve.n, b=act.n):
                    eng.wait_ge(dve.h, a)
                    eng.wait_ge(act.h, b)
                    eng.dma_start(out=D["smallsD"][:, :], in_=persF[:, :]).then_inc(stm.h, 16)
                self.em("sync", f)
                stm.inc(16)

            self.barrier()

    def _phase_S(self):
        nc, D = self.nc, self.st_
        pe, dve, act, pool, ld, cast, stm, sm = (self.pe, self.dve, self.act, self.pool,
                                                 self.ld, self.cast, self.stm, self.sm)
        pA, pB, pT = self.pA, self.pB, self.pT
        persF = D["persF"]

        with contextlib.ExitStack() as ess:
            for nm, sh, dt in (
                ("rhsT", [66, 4096], FP16), ("gerow", [1, 1024], F32),
                ("state", [128, NH * DV], F32), ("stbf", [128, NH * DV], BF16),
            ):
                D[nm] = ess.enter_context(nc.sbuf_tensor(nm, sh, dt))
            sc = {}
            for nm, sh, dt in (
                ("mix", [128, 32 * MP], BF16), ("qkvc", [128, 32 * SLAB], BF16),
                ("zs_s", [128, 16 * CH], BF16),
                ("l2r", [16, SLAB], F32), ("rowb", [1, 4096], BF16),
                ("argA", [128, 2048], BF16), ("argQ", [128, 2048], BF16),
                ("AT", [128, 2048], BF16), ("AO", [128, 2048], BF16), ("AQ", [128, 2048], BF16),
                ("Ysb", [128, 16 * 256], BF16),
                ("Pw2", [128, 2048], BF16), ("PwT2", [128, 2048], BF16),
                ("KW", [128, 2048], BF16), ("W2T", [128, 512], BF16), ("Usb", [128, 512], BF16),
                ("ogsb", [128, 2048], BF16), ("og2", [128, 2048], BF16),
                ("stage", [128, 128], F32), ("colsb", [128, 128], F32),
                ("gqb", [128, 2048], BF16), ("gendb", [128, 16], F32),
                ("onstg", [128, 2 * 512], BF16),
                ("cvt", [128, 2 * SLAB], F32),
                ("cflat2", [2, 2048], FP16),
                ("scr16", [16, 16], F32),
            ):
                sc[nm] = ess.enter_context(nc.sbuf_tensor(nm, sh, dt))
            self.sc = sc

            sm = self.sm

            def f(eng):
                eng.dma_start(out=D["rhsT"][0:64, :], in_=D["INDT"][:, :]).then_inc(ld.h, 16)
                srcv = D["cD"][0:16, :].rearrange("h (c t) -> h c t", t=CH)[:, :, CH - 1:CH]
                srcv = srcv.squeeze(2).transpose([1, 0]).unsqueeze(0)
                outv = D["gerow"][0:1, 0:1024].rearrange("one (c h) -> one c h", h=16)
                eng.dma_start(out=outv, in_=srcv,
                              allow_slow_non_contiguous=True).then_inc(ld.h, 16)
            self.em("sync", f)
            sinit_ld = ld.inc(32)

            def f(eng, w=sinit_ld):
                eng.wait_ge(ld.h, w)
                eng.activation(out=D["gerow"][0:1, 0:1024], in_=D["gerow"][0:1, 0:1024],
                               func=AF.Exp).then_inc(act.h, 1)
            self.em("scalar", f)
            ge_exp = act.inc(1)

            def f(eng, w=ge_exp):
                eng.wait_ge(act.h, w)
                eng.memset(D["state"][:, :], 0.0)
                eng.memset(sc["stage"][:, :], 0.0).then_inc(dve.h, 1)
            self.em("vector", f)
            dve.inc(1)

            def f(eng, w=dve.n):
                eng.wait_ge(dve.h, w)
                eng.tensor_copy(D["stbf"][:, :], D["state"][:, :]).then_inc(pool.h, 1)
            self.em("gpsimd", f)
            pool.inc(1)

            prev_slab_pe = [0]
            prev = {"argmm": 0, "stagemm": 0, "onst": [None, None]}

            for sb in range(NSLAB):
                t0 = sb * SLAB
                mixv = sc["mix"][:, :].rearrange("p (b c) -> p b c", c=MP)
                if sb == 0:
                    def f(eng):
                        eng.memset(sc["mix"][:, :].rearrange("p (b c) -> p b c", c=MP)[:, :, 0:8],
                                   0.0).then_inc(pool.h, 1)
                    self.em("gpsimd", f)
                else:
                    def f(eng, dv=dve.n, pl=pool.n):
                        eng.wait_ge(dve.h, dv)
                        v = sc["mix"][:, :].rearrange("p (b c) -> p b c", c=MP)
                        eng.tensor_copy(v[:, :, 0:8], v[:, :, SLAB:SLAB + 8]).then_inc(pool.h, 1)
                    self.em("gpsimd", f)
                halo_done = pool.inc(1)

                mv = D["mixedD"][:, t0:t0 + SLAB].rearrange("(b p) t -> p b t", p=128)

                def f(eng, mv=mv, hd=halo_done, dv=dve.n, pl=pool.n, pp=prev_slab_pe[0]):
                    eng.wait_ge(pool.h, hd)
                    eng.wait_ge(dve.h, dv)
                    eng.wait_ge(pe.h, pp)
                    eng.dma_start(out=sc["mix"][:, :].rearrange("p (b c) -> p b c", c=MP)[:, :, 8:8 + SLAB],
                                  in_=mv).then_inc(ld.h, 16)
                self.em("sync", f)
                mix_rdy = ld.inc(16)

                act_base = act.n
                for cb in range(32):
                    cpar = cb % 2

                    def f(eng, cb=cb, cpar=cpar, mr=mix_rdy, pw=prev_slab_pe[0],
                          CF=self.CF_CONV, wact=(act.n - 1 if cb >= 2 else 0)):
                        if cb == 0:
                            eng.wait_ge(ld.h, mr)
                            eng.wait_ge(pe.h, pw)
                        if cb >= 2:
                            eng.wait_ge(act.h, wact)
                        v = sc["mix"][:, :].rearrange("p (b c) -> p b c", c=MP)
                        dst = sc["cvt"][:, cpar * SLAB:(cpar + 1) * SLAB]
                        eng.tensor_scalar_mul(dst, v[:, cb, 8:8 + SLAB], CF(cb, 3))
                        for j in (2, 1, 0):
                            ins = eng.scalar_tensor_tensor(
                                dst, v[:, cb, 5 + j:5 + j + SLAB], CF(cb, j), dst,
                                ALU.mult, ALU.add)
                        ins.then_inc(dve.h, 1)
                    self.em("vector", f)
                    cdone = dve.inc(1)

                    def f(eng, cb=cb, cpar=cpar, w=cdone):
                        eng.wait_ge(dve.h, w)
                        eng.activation(out=sc["qkvc"][:, cb * SLAB:(cb + 1) * SLAB],
                                       in_=sc["cvt"][:, cpar * SLAB:(cpar + 1) * SLAB],
                                       func=AF.Silu).then_inc(act.h, 1)
                    self.em("scalar", f)
                    act.inc(1)
                silu_done = act.n

                def f(eng, w=silu_done):
                    eng.wait_ge(act.h, w)
                    for cb in range(16):
                        eng.activation(out=sc["og2"][:, (cb % 2) * SLAB:(cb % 2 + 1) * SLAB],
                                       in_=sc["qkvc"][:, cb * SLAB:(cb + 1) * SLAB],
                                       func=AF.Square).then_inc(act.h, 1)
                self.em("scalar", f)
                sq_base = act.n
                act.inc(16)

                def f(eng, base=sq_base, pf=self.pfree.get("pB0"), OH=self.C_OH):
                    if pf is not None:
                        eng.wait_ge(pf[0], pf[1])
                    ins = None
                    for cb in range(16):
                        eng.wait_ge(act.h, base + cb + 1)
                        ins = eng.matmul(pB[0][0:16, 0:SLAB], OH(cb),
                                         sc["og2"][:, (cb % 2) * SLAB:(cb % 2 + 1) * SLAB],
                                         start=(cb == 0), stop=(cb == 15))
                    ins.then_inc(pe.h, 1)
                self.em("tensor", f)
                l2mm = pe.inc(1)

                def f(eng, w=l2mm):
                    eng.wait_ge(pe.h, w)
                    eng.activation(out=sc["l2r"][0:16, :], in_=pB[0][0:16, 0:SLAB],
                                   func=AF.Ln, bias=self.CF_EPS6[0:16, :])
                    eng.activation(out=sc["l2r"][0:16, :], in_=sc["l2r"][0:16, :],
                                   func=AF.Exp, scale=-0.5).then_inc(act.h, 1)
                self.em("scalar", f)
                rstd_done = act.inc(1)
                self.puse("pB0", act.h, rstd_done)

                def f(eng, w=rstd_done):
                    eng.wait_ge(act.h, w)
                    eng.tensor_scalar_mul(sc["l2r"][0:8, :], sc["l2r"][0:8, :], RSQ128
                                          ).then_inc(dve.h, 1)
                self.em("vector", f)
                rstds = dve.inc(1)

                def f(eng, w=rstds):
                    eng.wait_ge(dve.h, w)
                    eng.dma_start(out=D["l2rD"][:, :], in_=sc["l2r"][0:16, :]).then_inc(sm.h, 16)
                self.em("scalar", f)
                l2st = sm.inc(16)

                def f(eng, w=l2st, pw=pe.n):
                    eng.wait_ge(sm.h, w)
                    outv = sc["rowb"][0:1, :].rearrange("one (b t) -> one b t", t=SLAB)
                    eng.dma_start(out=outv, in_=D["l2rD"][0:16, :].unsqueeze(0)
                                  ).then_inc(self.cast.h, 16)
                self.em("gpsimd", f)
                l2fl = self.cast.inc(16)

                for cb in range(16):
                    def f(eng, cb=cb, w=l2fl, pf=self.pfree.get("pB0"), OR=self.C_OROW):
                        eng.wait_ge(self.cast.h, w)
                        if pf is not None:
                            eng.wait_ge(pf[0], pf[1])
                        eng.matmul(pB[0][:, (cb % 2) * 256:(cb % 2) * 256 + SLAB],
                                   OR, sc["rowb"][0:1, cb * SLAB:(cb + 1) * SLAB],
                                   start=True, stop=True).then_inc(pe.h, 1)
                    self.em("tensor", f)
                    bc = pe.inc(1)

                    def f(eng, cb=cb, w=bc):
                        eng.wait_ge(pe.h, w)
                        sl = sc["qkvc"][:, cb * SLAB:(cb + 1) * SLAB]
                        eng.tensor_mul(sl, sl, pB[0][:, (cb % 2) * 256:(cb % 2) * 256 + SLAB]
                                       ).then_inc(dve.h, 1)
                    self.em("vector", f)
                    self.puse("pB0", dve.h, dve.inc(1))
                l2_applied = dve.n

                for tc in range(SLAB // CH):
                    ci = sb * (SLAB // CH) + tc
                    self._chunk(ci, tc, l2_applied, prev)
                prev_slab_pe[0] = pe.n

            self.barrier()
        self._wout_pass()
        self.barrier()

    def _chunk(self, ci, tc, l2_applied, prev):
        D, sc = self.st_, self.sc
        pe, dve, act, pool, sm, ld = self.pe, self.dve, self.act, self.pool, self.sm, self.ld
        pA, pB, pT = self.pA, self.pB, self.pT
        c0 = ci * CH
        csplit, cbsplit, rhsT, persF = D["csplit"], D["cbsplit"], D["rhsT"], D["persF"]

        def qc(cb):
            return sc["qkvc"][:, cb * SLAB + tc * CH: cb * SLAB + tc * CH + CH]

        def zc(h):
            return sc["zs_s"][:, h * CH:(h + 1) * CH]

        def blkf(name, h):
            return sc[name][:, h * 128:(h + 1) * 128]

        zv = D["zsD"][:, c0:c0 + CH].rearrange("(b p) t -> p b t", p=128)

        def f(eng, zv=zv, dv=dve.n):
            eng.wait_ge(dve.h, dv)
            eng.dma_start(out=sc["zs_s"][:, :].rearrange("p (b t) -> p b t", t=CH),
                          in_=zv).then_inc(ld.h, 16)
        self.em("sync", f)
        zs_rdy = ld.inc(16)

        def f(eng, w=prev["argmm"], c0=c0):
            eng.wait_ge(pe.h, w)
            for (row, col, srcD, r0) in ((64, 0, "cbsplitD", 0), (65, 0, "cbsplitD", 32),
                                         (64, 2048, "csplitD", 0), (65, 2048, "csplitD", 32)):
                outv = rhsT[row:row + 1, col:col + 2048].rearrange(
                    "one (h t) -> one h t", t=CH)
                inv = D[srcD][r0:r0 + 16, c0:c0 + CH].unsqueeze(0)
                eng.dma_start(out=outv, in_=inv).then_inc(sm.h, 16)
            for (row, r0) in ((0, 0), (1, 32)):
                outv = sc["cflat2"][row:row + 1, :].rearrange("one (h t) -> one h t", t=CH)
                inv = D["csplitD"][r0:r0 + 16, c0:c0 + CH].unsqueeze(0)
                eng.dma_start(out=outv, in_=inv).then_inc(sm.h, 16)
        self.em("scalar", f)
        flats = sm.inc(96)

        argev = None
        for roff, dst in ((0, "argA"), (2048, "argQ")):
            for j in range(4):
                def f(eng, j=j, roff=roff, w=flats, c0=c0, pf=self.pfree.get("pA%d" % j)):
                    eng.wait_ge(sm.h, w)
                    if pf is not None:
                        eng.wait_ge(pf[0], pf[1])
                    eng.matmul(pA[j][:, :], csplit[0:66, c0:c0 + CH],
                               rhsT[0:66, roff + j * 512:roff + (j + 1) * 512],
                               start=True, stop=True).then_inc(pe.h, 1)
                self.em("tensor", f)
                mm = pe.inc(1)

                def f(eng, j=j, dst=dst, w=mm):
                    eng.wait_ge(pe.h, w)
                    eng.tensor_scalar(sc[dst][:, j * 512:(j + 1) * 512], pA[j][:, :],
                                      0.0, None, ALU.min).then_inc(dve.h, 1)
                self.em("vector", f)
                argev = dve.inc(1)
                self.puse("pA%d" % j, dve.h, argev)

        def f(eng, w=argev):
            eng.wait_ge(dve.h, w)
            eng.activation(out=sc["argA"][:, :], in_=sc["argA"][:, :], func=AF.Exp)
            eng.activation(out=sc["argQ"][:, :], in_=sc["argQ"][:, :], func=AF.Exp
                           ).then_inc(act.h, 1)
        self.em("scalar", f)
        exp_done = act.inc(1)

        def f(eng, c0=c0, w=prev["stagemm"]):
            eng.wait_ge(pe.h, w)
            eng.tensor_copy(sc["stage"][0:16, :], persF[32:48, c0:c0 + CH])
            eng.tensor_copy(sc["scr16"][0:16, 0:1], persF[0:16, c0 + CH - 1:c0 + CH]
                            ).then_inc(dve.h, 1)
        self.em("vector", f)
        stg1 = dve.inc(1)

        def f(eng, c0=c0, w=stg1):
            eng.wait_ge(dve.h, w)
            eng.activation(out=sc["stage"][32:48, :], in_=persF[0:16, c0:c0 + CH], func=AF.Exp)
            eng.activation(out=sc["stage"][64:80, :], in_=persF[0:16, c0:c0 + CH],
                           func=AF.Exp, scale=-1.0, bias=sc["scr16"][0:16, 0:1]
                           ).then_inc(act.h, 1)
        self.em("scalar", f)
        stg2 = act.inc(1)

        def f(eng, c0=c0, w=stg2):
            eng.wait_ge(act.h, w)
            eng.tensor_mul(sc["stage"][32:48, :], sc["stage"][32:48, :],
                           persF[32:48, c0:c0 + CH]).then_inc(dve.h, 1)
        self.em("vector", f)
        stg3 = dve.inc(1)

        def f(eng, w=stg3, pf=self.pfree.get("pB1"), IDF=self.C_IDF):
            eng.wait_ge(dve.h, w)
            if pf is not None:
                eng.wait_ge(pf[0], pf[1])
            eng.transpose(pB[1][:, 0:128], sc["stage"][:, :], IDF).then_inc(pe.h, 1)
        self.em("tensor", f)
        trs = pe.inc(1)
        prev["stagemm"] = trs

        def f(eng, ci=ci, ORF=self.CF_OROWF):
            eng.matmul(pB[1][:, 128:144], ORF, D["gerow"][0:1, ci * 16:(ci + 1) * 16],
                       start=True, stop=True).then_inc(pe.h, 1)
        self.em("tensor", f)
        gemm = pe.inc(1)

        def f(eng, w=gemm):
            eng.wait_ge(pe.h, w)
            eng.tensor_copy(sc["colsb"][:, :], pB[1][:, 0:128])
            eng.tensor_scalar_mul(sc["colsb"][:, 96:112], sc["colsb"][:, 32:48], -1.0)
            eng.tensor_copy(sc["gendb"][:, :], pB[1][:, 128:144]).then_inc(dve.h, 1)
        self.em("vector", f)
        cols_done = dve.inc(1)
        self.puse("pB1", dve.h, cols_done)

        gexp = None
        for j in range(4):
            def f(eng, j=j, w=flats, pf=self.pfree.get("pA%d" % j), O2=D["ones2"]):
                eng.wait_ge(sm.h, w)
                if pf is not None:
                    eng.wait_ge(pf[0], pf[1])
                eng.matmul(pA[j][:, :], O2[0:2, :],
                           sc["cflat2"][0:2, j * 512:(j + 1) * 512],
                           start=True, stop=True).then_inc(pe.h, 1)
            self.em("tensor", f)
            mm = pe.inc(1)

            def f(eng, j=j, w=mm):
                eng.wait_ge(pe.h, w)
                eng.activation(out=sc["og2"][:, j * 512:(j + 1) * 512], in_=pA[j][:, :],
                               func=AF.Exp).then_inc(act.h, 1)
            self.em("scalar", f)
            gexp = act.inc(1)
            self.puse("pA%d" % j, act.h, gexp)
        prev["argmm"] = pe.n

        def f(eng, w=gexp, w2=l2_applied):
            eng.wait_ge(act.h, w)
            eng.wait_ge(dve.h, w2)
            last = None
            for h in range(NH):
                last = eng.tensor_mul(blkf("gqb", h), qc(h // 2), blkf("og2", h))
            last.then_inc(pool.h, 1)
        self.em("gpsimd", f)
        gq_done_pool = pool.inc(1)

        def f(eng, w=dve.n):
            eng.wait_ge(dve.h, w)
            eng.tensor_copy(D["stbf"][:, :], D["state"][:, :]).then_inc(pool.h, 1)
        self.em("gpsimd", f)
        stbf_done = pool.inc(1)

        if ci == 0:
            self.dump("dbg_qkvc", sc["qkvc"][:, :], [128, 32 * SLAB], BF16)
            self.dump("dbg_argA", sc["argA"][:, :], [128, 2048], BF16)
            self.dump("dbg_argQ", sc["argQ"][:, :], [128, 2048], BF16)
            self.dump("dbg_cols", sc["colsb"][:, :], [128, 128], F32)
            self.dump("dbg_gend", sc["gendb"][:, :], [128, 16], F32)
            self.dump("dbg_gqb", sc["gqb"][:, :], [128, 2048], BF16)

        skk_ev = {}
        tev_map = {}
        skbanks = ["pB0", "pB1", "pA0", "pA1"]
        slot_free = [self.pfree.get(b) for b in skbanks]
        skk_last = [0, 0, 0, 0]
        for h in range(NH):
            slot = h % 4
            bank = [pB[0], pB[1], pA[0], pA[1]][slot]
            kb, qb = 8 + h // 2, h // 2

            def f(eng, bank=bank, kb=kb, qb=qb, w=l2_applied, pf=slot_free[slot]):
                eng.wait_ge(dve.h, w)
                if pf is not None:
                    eng.wait_ge(pf[0], pf[1])
                eng.matmul(bank[:, 0:128], qc(kb), qc(kb), start=True, stop=True)
                eng.matmul(bank[:, 128:256], qc(kb), qc(qb), start=True, stop=True
                           ).then_inc(pe.h, 1)
            self.em("tensor", f)
            skmm = pe.inc(1)

            def f(eng, h=h, bank=bank, w=skmm, we=exp_done,
                  MU=self.C_MUSN, MQ=self.C_MUI):
                eng.wait_ge(pe.h, w)
                eng.wait_ge(act.h, we)
                eng.tensor_mul(blkf("AT", h), bank[:, 0:128],
                               sc["argA"][:, h * 128:(h + 1) * 128])
                eng.tensor_mul(blkf("AT", h), blkf("AT", h), MU)
                eng.tensor_mul(blkf("AQ", h), bank[:, 128:256],
                               sc["argQ"][:, h * 128:(h + 1) * 128])
                eng.tensor_mul(blkf("AQ", h), blkf("AQ", h), MQ).then_inc(dve.h, 1)
            self.em("vector", f)
            amul = dve.inc(1)
            slot_free[slot] = (dve.h, amul)
            skk_last[slot] = amul
            skk_ev[h] = amul
        for i, b in enumerate(skbanks):
            self.puse(b, dve.h, skk_last[i])

        for h in range(NH):
            par = h % 2
            kb, vb = 8 + h // 2, 16 + h

            def f(eng, h=h, par=par, kb=kb, vb=vb, w=skk_ev[h],
                  pf=self.pfree.get("pT%d" % par), ID=self.C_ID):
                eng.wait_ge(dve.h, w)
                if pf is not None:
                    eng.wait_ge(pf[0], pf[1])
                eng.transpose(pT[par][:, 0:128], blkf("AT", h), ID)
                eng.transpose(pT[par][:, 128:256], qc(kb), ID)
                eng.transpose(pT[par][:, 256:384], qc(vb), ID).then_inc(pe.h, 1)
            self.em("tensor", f)
            trs = pe.inc(1)

            if h % 2 == 0:
                def f(eng, h=h, par=par, w=trs, wc=cols_done):
                    eng.wait_ge(pe.h, w)
                    eng.wait_ge(dve.h, wc)
                    cb_ = sc["colsb"]
                    eng.tensor_copy(blkf("AO", h), pT[par][:, 0:128])
                    eng.tensor_scalar(sc["Ysb"][:, h * 256 + 128:h * 256 + 256],
                                      pT[par][:, 128:256], cb_[:, 96 + h:97 + h], None,
                                      ALU.mult)
                    eng.tensor_scalar(blkf("KW", h), pT[par][:, 128:256],
                                      cb_[:, 64 + h:65 + h], None, ALU.mult)
                    eng.tensor_scalar(sc["Ysb"][:, h * 256:h * 256 + 128],
                                      pT[par][:, 256:384], cb_[:, h:h + 1], None, ALU.mult
                                      ).then_inc(dve.h, 1)
                self.em("vector", f)
                tev = dve.inc(1)
                tev_map[h] = (dve.h, tev)
                self.puse("pT%d" % par, dve.h, tev)
            else:
                def f(eng, h=h, par=par, w=trs, wc=cols_done):
                    eng.wait_ge(pe.h, w)
                    eng.wait_ge(dve.h, wc)
                    cb_ = sc["colsb"]
                    eng.copy(blkf("AO", h), pT[par][:, 0:128])
                    eng.mul(sc["Ysb"][:, h * 256 + 128:h * 256 + 256],
                            pT[par][:, 128:256], cb_[:, 96 + h:97 + h])
                    eng.mul(blkf("KW", h), pT[par][:, 128:256], cb_[:, 64 + h:65 + h])
                    eng.mul(sc["Ysb"][:, h * 256:h * 256 + 128],
                            pT[par][:, 256:384], cb_[:, h:h + 1]).then_inc(act.h, 1)
                self.em("scalar", f)
                tev = act.inc(1)
                tev_map[h] = (act.h, tev)
                self.puse("pT%d" % par, act.h, tev)

        psrc = ("AO", "AT")
        pwv = {h: tev_map[h] for h in range(NH)}
        yv = {h: tev_map[h] for h in range(NH)}
        for lvl in range(7):
            pdst = ("Pw2", "PwT2") if (lvl % 2 == 1) else ("AO", "AT")
            if lvl >= 1:
                bgroup_free = [self.pfree.get("pB0"), self.pfree.get("pB1")]
                for g in range(4):
                    hs = [4 * g + i for i in range(4)]

                    def f(eng, hs=tuple(hs), psrc=psrc, lvl=lvl,
                          wvs=tuple(pwv[h] for h in hs),
                          pfs=tuple(p for p in bgroup_free if p is not None)):
                        for wv in wvs:
                            eng.wait_ge(wv[0], wv[1])
                        for pf in pfs:
                            eng.wait_ge(pf[0], pf[1])
                        ins = None
                        for i, h in enumerate(hs):
                            ins = eng.matmul(pB[0][:, i * 128:(i + 1) * 128],
                                             blkf(psrc[0], h), blkf(psrc[1], h),
                                             start=True, stop=True)
                            if lvl < 6:
                                ins = eng.matmul(pB[1][:, i * 128:(i + 1) * 128],
                                                 blkf(psrc[1], h), blkf(psrc[0], h),
                                                 start=True, stop=True)
                        ins.then_inc(pe.h, 1)
                    self.em("tensor", f)
                    bmm = pe.inc(1)

                    def f(eng, g=g, pdst=pdst, lvl=lvl, w=bmm):
                        eng.wait_ge(pe.h, w)
                        ins = eng.copy(sc[pdst[1]][:, g * 512:(g + 1) * 512],
                                       pB[0][:, 0:512])
                        if lvl < 6:
                            ins = eng.copy(sc[pdst[0]][:, g * 512:(g + 1) * 512],
                                           pB[1][:, 0:512])
                        ins.then_inc(act.h, 1)
                    self.em("scalar", f)
                    bev = act.inc(1)
                    for h in hs:
                        pwv[h] = (act.h, bev)
                    bgroup_free = [(act.h, bev)]
                self.puse("pB0", act.h, act.n)
                self.puse("pB1", act.h, act.n)
                pw_t = pdst[1]
            else:
                pw_t = "AT"

            aslot_free = [self.pfree.get("pA0"), self.pfree.get("pA1")]
            for p in range(NH // 2):
                h0, h1 = 2 * p, 2 * p + 1
                slot = p % 2
                bank = pA[slot]

                def f(eng, h0=h0, h1=h1, bank=bank, pw_t=pw_t,
                      wp0=pwv[h0], wp1=pwv[h1], wy0=yv[h0], wy1=yv[h1],
                      pf=aslot_free[slot]):
                    eng.wait_ge(wp0[0], wp0[1])
                    eng.wait_ge(wp1[0], wp1[1])
                    eng.wait_ge(wy0[0], wy0[1])
                    eng.wait_ge(wy1[0], wy1[1])
                    if pf is not None:
                        eng.wait_ge(pf[0], pf[1])
                    eng.matmul(bank[:, 0:256], blkf(pw_t, h0),
                               sc["Ysb"][:, h0 * 256:(h0 + 1) * 256],
                               start=True, stop=True)
                    eng.matmul(bank[:, 256:512], blkf(pw_t, h1),
                               sc["Ysb"][:, h1 * 256:(h1 + 1) * 256],
                               start=True, stop=True).then_inc(pe.h, 1)
                self.em("tensor", f)
                amm = pe.inc(1)

                def f(eng, h0=h0, bank=bank, w=amm):
                    eng.wait_ge(pe.h, w)
                    ysl = sc["Ysb"][:, h0 * 256:h0 * 256 + 512]
                    eng.tensor_add(ysl, ysl, bank[:, 0:512]).then_inc(dve.h, 1)
                self.em("vector", f)
                yev = (dve.h, dve.inc(1))
                yv[h0] = yev
                yv[h1] = yev
                aslot_free[slot] = yev
            self.puse("pA0", dve.h, dve.n)
            self.puse("pA1", dve.h, dve.n)
            if lvl >= 1:
                psrc = pdst
        yready = (dve.n, act.n)
        if ci == 0:
            self.dump("dbg_Y", sc["Ysb"][:, :], [128, 4096], BF16)

        for p in range(NH // 2):
            h0, h1 = 2 * p, 2 * p + 1
            par = p % 2

            def f(eng, h0=h0, h1=h1, par=par, wv=yready,
                  pf=self.pfree.get("pT%d" % par), ID=self.C_ID):
                eng.wait_ge(dve.h, wv[0])
                eng.wait_ge(act.h, wv[1])
                if pf is not None:
                    eng.wait_ge(pf[0], pf[1])
                eng.transpose(pT[par][:, 384:512],
                              sc["Ysb"][:, h0 * 256 + 128:h0 * 256 + 256], ID)
                eng.transpose(pT[par][:, 512:640],
                              sc["Ysb"][:, h1 * 256 + 128:h1 * 256 + 256], ID
                              ).then_inc(pe.h, 1)
            self.em("tensor", f)
            w2mm = pe.inc(1)

            def f(eng, par=par, w=w2mm):
                eng.wait_ge(pe.h, w)
                eng.tensor_copy(sc["W2T"][:, par * 256:(par + 1) * 256],
                                pT[par][:, 384:640]).then_inc(dve.h, 1)
            self.em("vector", f)
            w2ev = dve.inc(1)
            self.puse("pT%d" % par, dve.h, w2ev)

            def f(eng, h0=h0, h1=h1, par=par, w=w2ev, ws=stbf_done,
                  pf=self.pfree.get("pA2"), ID=self.C_ID):
                eng.wait_ge(dve.h, w)
                eng.wait_ge(pool.h, ws)
                if pf is not None:
                    eng.wait_ge(pf[0], pf[1])
                for i, h in ((0, h0), (1, h1)):
                    c0_ = par * 256 + i * 128
                    eng.matmul(pA[2][:, c0_:c0_ + 128], ID,
                               sc["Ysb"][:, h * 256:h * 256 + 128], start=True, stop=False)
                    ins = eng.matmul(pA[2][:, c0_:c0_ + 128],
                                     sc["W2T"][:, par * 256 + i * 128:par * 256 + (i + 1) * 128],
                                     D["stbf"][:, h * 128:(h + 1) * 128], start=False,
                                     stop=True)
                ins.then_inc(pe.h, 1)
            self.em("tensor", f)
            umm = pe.inc(1)

            def f(eng, par=par, w=umm):
                eng.wait_ge(pe.h, w)
                eng.tensor_copy(sc["Usb"][:, par * 256:(par + 1) * 256],
                                pA[2][:, par * 256:(par + 1) * 256]).then_inc(dve.h, 1)
            self.em("vector", f)
            uev = dve.inc(1)
            self.puse("pA2", dve.h, uev)

            def f(eng, h0=h0, h1=h1, par=par, w=uev, wg=gq_done_pool,
                  pf=self.pfree.get("pA3")):
                eng.wait_ge(dve.h, w)
                eng.wait_ge(pool.h, wg)
                if pf is not None:
                    eng.wait_ge(pf[0], pf[1])
                for i, h in ((0, h0), (1, h1)):
                    c0_ = par * 256 + i * 128
                    eng.matmul(pA[3][:, c0_:c0_ + 128],
                               D["stbf"][:, h * 128:(h + 1) * 128], blkf("gqb", h),
                               start=True, stop=False)
                    ins = eng.matmul(pA[3][:, c0_:c0_ + 128],
                                     sc["Usb"][:, par * 256 + i * 128:par * 256 + (i + 1) * 128],
                                     blkf("AQ", h), start=False, stop=True)
                ins.then_inc(pe.h, 1)
            self.em("tensor", f)
            omm = pe.inc(1)

            def f(eng, h0=h0, par=par, w=omm, wz=zs_rdy):
                eng.wait_ge(pe.h, w)
                eng.wait_ge(ld.h, wz)
                eng.tensor_mul(sc["ogsb"][:, h0 * 128:h0 * 128 + 256],
                               pA[3][:, par * 256:(par + 1) * 256],
                               sc["zs_s"][:, h0 * CH:h0 * CH + 2 * CH]).then_inc(dve.h, 1)
            self.em("vector", f)
            ogev = dve.inc(1)
            self.puse("pA3", dve.h, ogev)

            def f(eng, h0=h0, h1=h1, par=par, w=uev, pf=self.pfree.get("pB%d" % par)):
                eng.wait_ge(dve.h, w)
                if pf is not None:
                    eng.wait_ge(pf[0], pf[1])
                eng.matmul(pB[par][:, 0:128], blkf("KW", h0),
                           sc["Usb"][:, par * 256:par * 256 + 128],
                           start=True, stop=True)
                eng.matmul(pB[par][:, 128:256], blkf("KW", h1),
                           sc["Usb"][:, par * 256 + 128:par * 256 + 256],
                           start=True, stop=True).then_inc(pe.h, 1)
            self.em("tensor", f)
            smm = pe.inc(1)

            def f(eng, h0=h0, h1=h1, par=par, w=smm):
                eng.wait_ge(pe.h, w)
                eng.scalar_tensor_tensor(D["state"][:, h0 * 128:(h0 + 1) * 128],
                                         D["state"][:, h0 * 128:(h0 + 1) * 128],
                                         sc["gendb"][:, h0:h0 + 1], pB[par][:, 0:128],
                                         ALU.mult, ALU.add)
                eng.scalar_tensor_tensor(D["state"][:, h1 * 128:(h1 + 1) * 128],
                                         D["state"][:, h1 * 128:(h1 + 1) * 128],
                                         sc["gendb"][:, h1:h1 + 1], pB[par][:, 128:256],
                                         ALU.mult, ALU.add).then_inc(dve.h, 1)
            self.em("vector", f)
            sev = dve.inc(1)
            self.puse("pB%d" % par, dve.h, sev)

        if ci == 0:
            self.dump("dbg_og", sc["ogsb"][:, :], [128, 2048], BF16)
            self.dump("dbg_state", D["state"][:, :], [128, 2048], F32)

        def f(eng, w=dve.n):
            eng.wait_ge(dve.h, w)
            eng.activation(out=sc["og2"][:, :], in_=sc["ogsb"][:, :], func=AF.Square
                           ).then_inc(act.h, 1)
        self.em("scalar", f)
        og2_done = act.inc(1)

        for bank in range(4):
            def f(eng, bank=bank, w=og2_done, pf=self.pfree.get("pA%d" % bank),
                  OC=self.C_OCOL):
                eng.wait_ge(act.h, w)
                if pf is not None:
                    eng.wait_ge(pf[0], pf[1])
                ins = None
                for hh in range(4):
                    h = bank * 4 + hh
                    ins = eng.matmul(pA[bank][0:1, hh * 128:(hh + 1) * 128], OC,
                                     blkf("og2", h), start=(hh == 0), stop=(hh == 3))
                ins.then_inc(pe.h, 1)
            self.em("tensor", f)
            summ = pe.inc(1)

            def f(eng, bank=bank, w=summ):
                eng.wait_ge(pe.h, w)
                eng.activation(out=sc["rowb"][0:1, bank * 512:(bank + 1) * 512],
                               in_=pA[bank][0:1, :], func=AF.Ln, bias=self.CF_RMSE[0:1, :])
                eng.activation(out=sc["rowb"][0:1, bank * 512:(bank + 1) * 512],
                               in_=sc["rowb"][0:1, bank * 512:(bank + 1) * 512],
                               func=AF.Exp, scale=-0.5, bias=self.CF_LN128H[0:1, :]
                               ).then_inc(act.h, 1)
            self.em("scalar", f)
            self.puse("pA%d" % bank, act.h, act.inc(1))
        rstdd = act.n

        for j in range(4):
            par = j % 2

            def f(eng, j=j, w=rstdd, pf=self.pfree.get("pA%d" % j), OR=self.C_OROW):
                eng.wait_ge(act.h, w)
                if pf is not None:
                    eng.wait_ge(pf[0], pf[1])
                eng.matmul(pA[j][:, 0:512], OR, sc["rowb"][0:1, j * 512:(j + 1) * 512],
                           start=True, stop=True).then_inc(pe.h, 1)
            self.em("tensor", f)
            bcmm = pe.inc(1)

            def f(eng, j=j, par=par, w=bcmm, NW=self.CF_NRMW, ost=prev["onst"][par]):
                eng.wait_ge(pe.h, w)
                if ost is not None:
                    eng.wait_ge(ost[0], ost[1])
                eng.scalar_tensor_tensor(sc["onstg"][:, par * 512:(par + 1) * 512],
                                         sc["ogsb"][:, j * 512:(j + 1) * 512], NW,
                                         pA[j][:, 0:512], ALU.mult, ALU.mult
                                         ).then_inc(dve.h, 1)
            self.em("vector", f)
            onev = dve.inc(1)
            self.puse("pA%d" % j, dve.h, onev)

            dst = D["onD"][j * 512:(j + 1) * 512, c0:c0 + CH].rearrange(
                "(hh p) t -> p hh t", p=128)

            def f(eng, par=par, w=onev, dst=dst):
                eng.wait_ge(dve.h, w)
                eng.dma_start(out=dst,
                              in_=sc["onstg"][:, par * 512:(par + 1) * 512
                                              ].rearrange("p (hh t) -> p hh t", t=CH)
                              ).then_inc(self.stm.h, 16)
            self.em("sync", f)
            prev["onst"][par] = (self.stm.h, self.stm.inc(16))

    def _wout_pass(self):
        nc, D = self.nc, self.st_
        pe, dve, act, cast, ld, stm = self.pe, self.dve, self.act, self.cast, self.ld, self.stm
        pA = self.pA
        with (
            nc.sbuf_tensor("wo_sb", [128, 16 * H], BF16) as wo_sb,
            nc.sbuf_tensor("on_t", [128, 2 * 16 * 512], BF16) as on_t,
            nc.sbuf_tensor("po_st", [128, 4 * 512], BF16) as po_st,
        ):
            wv = D["wout"][:, :].rearrange("(k p) hcol -> p k hcol", p=128)

            def f(eng, wv=wv):
                eng.dma_start(out=wo_sb[:, :].rearrange("p (k c) -> p k c", c=H),
                              in_=wv).then_inc(cast.h, 16)
            self.em("gpsimd", f)
            wo_done = cast.inc(16)

            prev_use = [0, 0]
            st_free = [None] * 4
            rr = [0]
            for tt in range(S // 512):
                tpar = tt % 2
                onv = D["onD"][:, tt * 512:(tt + 1) * 512].rearrange("(k p) t -> p k t", p=128)

                def f(eng, onv=onv, tpar=tpar, w=prev_use[tpar]):
                    eng.wait_ge(pe.h, w)
                    eng.dma_start(out=on_t[:, tpar * (16 * 512):(tpar + 1) * (16 * 512)
                                          ].rearrange("p (k t) -> p k t", t=512),
                                  in_=onv).then_inc(ld.h, 16)
                self.em("sync", f)
                on_done = ld.inc(16)

                half = tt // 8
                colbase = (tt % 8) * 512
                for hb in range(16):
                    bank = rr[0] % 4
                    sgi = rr[0] % 4
                    rr[0] += 1

                    def f(eng, hb=hb, tpar=tpar, bank=bank, w=on_done, ww=wo_done,
                          pf=self.pfree.get("pA%d" % bank)):
                        eng.wait_ge(ld.h, w)
                        eng.wait_ge(cast.h, ww)
                        if pf is not None:
                            eng.wait_ge(pf[0], pf[1])
                        ins = None
                        for k in range(16):
                            ins = eng.matmul(
                                pA[bank][:, :],
                                wo_sb[:, k * H + hb * 128:k * H + (hb + 1) * 128],
                                on_t[:, tpar * (16 * 512) + k * 512:
                                     tpar * (16 * 512) + (k + 1) * 512],
                                start=(k == 0), stop=(k == 15))
                        ins.then_inc(pe.h, 1)
                    self.em("tensor", f)
                    mm = pe.inc(1)
                    prev_use[tpar] = mm

                    def f(eng, bank=bank, sgi=sgi, w=mm, sf=st_free[sgi]):
                        eng.wait_ge(pe.h, w)
                        if sf is not None:
                            eng.wait_ge(sf[0], sf[1])
                        eng.copy(po_st[:, sgi * 512:(sgi + 1) * 512], pA[bank][:, :]
                                 ).then_inc(act.h, 1)
                    self.em("scalar", f)
                    ev = act.inc(1)
                    self.puse("pA%d" % bank, act.h, ev)

                    dst = D["rsIn"][half * H + hb * 128:half * H + (hb + 1) * 128,
                                    colbase:colbase + 512]

                    def f(eng, dst=dst, sgi=sgi, ev=ev):
                        eng.wait_ge(act.h, ev)
                        eng.dma_start(out=dst, in_=po_st[:, sgi * 512:(sgi + 1) * 512]
                                      ).then_inc(stm.h, 16)
                    self.em("sync", f)
                    st_free[sgi] = (stm.h, stm.inc(16))

    def _phase_D(self):
        nc, D = self.nc, self.st_
        pe, dve, act, pool, ld, cast, stm, sm, ccm = (
            self.pe, self.dve, self.act, self.pool, self.ld, self.cast, self.stm,
            self.sm, self.ccm)
        pA, pB, pT = self.pA, self.pB, self.pT

        def f(eng, w=stm.n):
            eng.wait_ge(stm.h, w)
            eng.collective_compute(
                "ReduceScatter", ALU.add, replica_groups=REPL,
                ins=[D["rsIn"][0:2 * H, :]], outs=[D["rsOut"][0:H, :]],
            ).then_inc(ccm.h)
        self.em("gpsimd", f)
        rs_done = ccm.inc(1)

        with contextlib.ExitStack() as esd:
            x1n = esd.enter_context(nc.sbuf_tensor("x1n", [128, 16 * TH], BF16))
            rs_t = esd.enter_context(nc.sbuf_tensor("rs_t", [128, 16 * 512], BF16))
            xf_t = esd.enter_context(nc.sbuf_tensor("xf_t", [128, 16 * 512], BF16))
            x1b = esd.enter_context(nc.sbuf_tensor("x1b", [128, 16 * 512], BF16))
            sq2 = rs_t
            rowst = esd.enter_context(nc.sbuf_tensor("rowst", [1, 4 * 512], F32))

            for tt in range(TH // 512):
                rsv = D["rsOut"][:, tt * 512:(tt + 1) * 512].rearrange("(k p) t -> p k t", p=128)
                xv = D["xTd"][:, tt * 512:(tt + 1) * 512].rearrange("(k p) t -> p k t", p=128)

                def f(eng, rsv=rsv, w=rs_done, wp=pool.n, wq=pe.n):
                    eng.wait_ge(ccm.h, w)
                    eng.wait_ge(pool.h, wp)
                    eng.wait_ge(pe.h, wq)
                    eng.dma_start(out=rs_t[:, :].rearrange("p (k t) -> p k t", t=512),
                                  in_=rsv).then_inc(ld.h, 16)
                self.em("sync", f)

                def f(eng, xv=xv, wp=pool.n):
                    eng.wait_ge(pool.h, wp)
                    eng.dma_start(out=xf_t[:, :].rearrange("p (k t) -> p k t", t=512),
                                  in_=xv).then_inc(cast.h, 16)
                self.em("gpsimd", f)
                cast.inc(16)
                tiles_rdy = ld.inc(16)
                xt_rdy = cast.n

                def f(eng, w=tiles_rdy, wd=dve.n):
                    eng.wait_ge(ld.h, w)
                    eng.wait_ge(dve.h, wd)
                    last = None
                    for k in range(16):
                        last = eng.tensor_add(x1b[:, k * 512:(k + 1) * 512],
                                              xf_t[:, k * 512:(k + 1) * 512],
                                              rs_t[:, k * 512:(k + 1) * 512])
                    last.then_inc(pool.h, 1)
                self.em("gpsimd", f)
                x1b_done = pool.inc(1)

                def f(eng, w=x1b_done, pf0=self.pfree.get("pB0"), OC=self.C_OCOL):
                    eng.wait_ge(pool.h, w)
                    if pf0 is not None:
                        eng.wait_ge(pf0[0], pf0[1])
                    ins = None
                    for k in range(16):
                        ins = eng.matmul(pB[0][0:1, 0:512], OC, x1b[:, k * 512:(k + 1) * 512],
                                         start=(k == 0), stop=(k == 15))
                    ins.then_inc(pe.h, 1)
                self.em("tensor", f)
                summ = pe.inc(1)

                def f(eng, w=x1b_done, wp=pe.n):
                    eng.wait_ge(pool.h, w)
                    eng.wait_ge(pe.h, wp)
                    last = None
                    for k in range(16):
                        last = eng.activation(out=sq2[:, k * 512:(k + 1) * 512],
                                              in_=x1b[:, k * 512:(k + 1) * 512],
                                              func=AF.Square)
                    last.then_inc(act.h, 1)
                self.em("scalar", f)
                sqb = act.inc(1)

                def f(eng, w=sqb, pf1=self.pfree.get("pB1"), OC=self.C_OCOL):
                    eng.wait_ge(act.h, w)
                    if pf1 is not None:
                        eng.wait_ge(pf1[0], pf1[1])
                    ins = None
                    for k in range(16):
                        ins = eng.matmul(pB[1][0:1, 0:512], OC, sq2[:, k * 512:(k + 1) * 512],
                                         start=(k == 0), stop=(k == 15))
                    ins.then_inc(pe.h, 1)
                self.em("tensor", f)
                sq_mm = pe.inc(1)

                def f(eng, a=summ, b=sq_mm):
                    eng.wait_ge(pe.h, max(a, b))
                    eng.activation(out=rowst[0:1, 0:512], in_=pB[0][0:1, 0:512],
                                   func=AF.Copy, scale=self.CF_I2048[0:1, :])
                    eng.activation(out=rowst[0:1, 512:1024], in_=rowst[0:1, 0:512],
                                   func=AF.Square).then_inc(act.h, 1)
                self.em("scalar", f)
                mrow = act.inc(1)

                def f(eng, w=mrow):
                    eng.wait_ge(act.h, w)
                    eng.scalar_tensor_tensor(rowst[0:1, 1024:1536], pB[1][0:1, 0:512],
                                             1.0 / 2048.0, rowst[0:1, 512:1024],
                                             ALU.mult, ALU.subtract).then_inc(dve.h, 1)
                self.em("vector", f)
                varr = dve.inc(1)
                self.puse("pB1", dve.h, varr)

                def f(eng, w=varr):
                    eng.wait_ge(dve.h, w)
                    eng.activation(out=rowst[0:1, 1536:2048], in_=rowst[0:1, 1024:1536],
                                   func=AF.Ln, bias=self.CF_EPS5[0:1, :])
                    eng.activation(out=rowst[0:1, 1536:2048], in_=rowst[0:1, 1536:2048],
                                   func=AF.Exp, scale=-0.5).then_inc(act.h, 1)
                self.em("scalar", f)
                rrow = act.inc(1)

                def f(eng, w=rrow, ORF=self.CF_OROWF):
                    eng.wait_ge(act.h, w)
                    eng.matmul(pB[0][:, 0:512], ORF, rowst[0:1, 0:512], start=True, stop=True)
                    eng.matmul(pB[1][:, 0:512], ORF, rowst[0:1, 1536:2048], start=True,
                               stop=True).then_inc(pe.h, 1)
                self.em("tensor", f)
                bcs = pe.inc(1)

                def f(eng, tt=tt, w=bcs):
                    eng.wait_ge(pe.h, w)
                    last = None
                    for k in range(16):
                        dst = x1n[:, k * TH + tt * 512:k * TH + (tt + 1) * 512]
                        last = eng.tensor_sub(dst, x1b[:, k * 512:(k + 1) * 512],
                                              pB[0][:, 0:512])
                        last = eng.tensor_mul(dst, dst, pB[1][:, 0:512])
                    last.then_inc(dve.h, 1)
                self.em("vector", f)
                x1n_done = dve.inc(1)
                self.puse("pB0", dve.h, x1n_done)
                self.puse("pB1", dve.h, x1n_done)

                xnv = D["x1nD"][:, tt * 512:(tt + 1) * 512].rearrange("(k p) t -> p k t", p=128)

                def f(eng, tt=tt, xnv=xnv, w=x1n_done):
                    eng.wait_ge(dve.h, w)
                    eng.dma_start(out=xnv,
                                  in_=x1n[:, :].rearrange("p (k t2) -> p k t2", t2=TH)
                                  [:, :, tt * 512:(tt + 1) * 512]).then_inc(stm.h, 16)
                self.em("sync", f)
                stm.inc(16)

            with (
                nc.sbuf_tensor("guw", [128, 2 * 16 * 256], BF16) as guw,
                nc.sbuf_tensor("sgst", [128, 512], BF16) as sgst,
                nc.sbuf_tensor("hstg", [128, 2 * 512], BF16) as hstg,
            ):
                guw_use = [0, 0]
                sg_use = [None]
                hst_use = [None, None]
                for pb in range(INTER // 128):
                    par = pb % 2
                    wv = D["wgu"][:, pb * 256:(pb + 1) * 256].rearrange(
                        "(k p) c -> p k c", p=128)

                    def f(eng, wv=wv, par=par, w=guw_use[par]):
                        eng.wait_ge(pe.h, w)
                        eng.dma_start(out=guw[:, par * (16 * 256):(par + 1) * (16 * 256)
                                              ].rearrange("p (k c) -> p k c", c=256),
                                      in_=wv).then_inc(cast.h, 16)
                    self.em("gpsimd", f)
                    gw_done = cast.inc(16)

                    for tt in range(TH // 512):
                        def f(eng, par=par, tt=tt, w=gw_done,
                              pf0=self.pfree.get("pA0"), pf1=self.pfree.get("pA1")):
                            eng.wait_ge(cast.h, w)
                            if pf0 is not None:
                                eng.wait_ge(pf0[0], pf0[1])
                            if pf1 is not None:
                                eng.wait_ge(pf1[0], pf1[1])
                            base = par * (16 * 256)
                            ins = None
                            for k in range(16):
                                eng.matmul(pA[0][:, :],
                                           guw[:, base + k * 256:base + k * 256 + 128],
                                           x1n[:, k * TH + tt * 512:k * TH + (tt + 1) * 512],
                                           start=(k == 0), stop=(k == 15))
                            for k in range(16):
                                ins = eng.matmul(
                                    pA[1][:, :],
                                    guw[:, base + k * 256 + 128:base + k * 256 + 256],
                                    x1n[:, k * TH + tt * 512:k * TH + (tt + 1) * 512],
                                    start=(k == 0), stop=(k == 15))
                            ins.then_inc(pe.h, 1)
                        self.em("tensor", f)
                        gmm = pe.inc(1)
                        guw_use[par] = gmm

                        def f(eng, w=gmm, sg=sg_use[0]):
                            eng.wait_ge(pe.h, w)
                            if sg is not None:
                                eng.wait_ge(sg[0], sg[1])
                            eng.activation(out=sgst[:, :], in_=pA[0][:, :], func=AF.Silu
                                           ).then_inc(act.h, 1)
                        self.em("scalar", f)
                        sil = act.inc(1)
                        self.puse("pA0", act.h, sil)

                        par2 = tt % 2

                        def f(eng, w=sil, par2=par2, hu=hst_use[par2]):
                            eng.wait_ge(act.h, w)
                            if hu is not None:
                                eng.wait_ge(hu[0], hu[1])
                            eng.tensor_mul(hstg[:, par2 * 512:par2 * 512 + 512],
                                           sgst[:, :], pA[1][:, :]).then_inc(dve.h, 1)
                        self.em("vector", f)
                        hmul = dve.inc(1)
                        self.puse("pA1", dve.h, hmul)
                        sg_use[0] = (dve.h, hmul)

                        def f(eng, pb=pb, tt=tt, par2=par2, w=hmul):
                            eng.wait_ge(dve.h, w)
                            eng.dma_start(out=D["hD"][pb * 128:(pb + 1) * 128,
                                                      tt * 512:(tt + 1) * 512],
                                          in_=hstg[:, par2 * 512:par2 * 512 + 512]
                                          ).then_inc(stm.h, 16)
                        self.em("sync", f)
                        hst_use[par2] = (stm.h, stm.inc(16))

        self.barrier()

        with (
            nc.sbuf_tensor("dw", [128, 64 * 512], BF16) as dw,
            nc.sbuf_tensor("hp", [128, 2 * 8 * 512], BF16) as hp,
            nc.sbuf_tensor("dnst", [128, 4 * 512], F32) as dnst,
        ):
            dn_st = [None] * 4
            hp_mm = [0, 0]
            for qd in range(4):
                wv = D["wdn"][:, qd * 512:(qd + 1) * 512].rearrange("(k p) c -> p k c", p=128)

                def f(eng, wv=wv, w=pe.n):
                    eng.wait_ge(pe.h, w)
                    eng.dma_start(out=dw[:, :].rearrange("p (k c) -> p k c", c=512),
                                  in_=wv).then_inc(cast.h, 16)
                self.em("gpsimd", f)
                dw_done = cast.inc(16)

                for tt in range(TH // 512):
                    hp_done = []
                    for pc in range(8):
                        par = pc % 2
                        hv = D["hD"][pc * 1024:(pc + 1) * 1024, tt * 512:(tt + 1) * 512
                                     ].rearrange("(k p) t -> p k t", p=128)
                        slot_mm = hp_mm[par] if True else 0

                        def f(eng, hv=hv, par=par, w=hp_mm[par]):
                            eng.wait_ge(pe.h, w)
                            eng.dma_start(out=hp[:, par * (8 * 512):(par + 1) * (8 * 512)
                                                  ].rearrange("p (k t) -> p k t", t=512),
                                          in_=hv).then_inc(ld.h, 16)
                        self.em("sync", f)
                        hp_done.append(ld.inc(16))

                        def f(eng, par=par, pc=pc, qd=qd, w=hp_done[-1], ww=dw_done,
                              pfs=tuple(self.pfree.get("pA%d" % b) for b in range(4))):
                            eng.wait_ge(ld.h, w)
                            eng.wait_ge(cast.h, ww)
                            if pc == 0:
                                for pf in pfs:
                                    if pf is not None:
                                        eng.wait_ge(pf[0], pf[1])
                            base = par * (8 * 512)
                            ins = None
                            for hb in range(4):
                                for kk in range(8):
                                    k = pc * 8 + kk
                                    ins = eng.matmul(
                                        pA[hb][:, :],
                                        dw[:, k * 512 + hb * 128:k * 512 + (hb + 1) * 128],
                                        hp[:, base + kk * 512:base + (kk + 1) * 512],
                                        start=(k == 0), stop=(k == 63))
                            ins.then_inc(pe.h, 1)
                        self.em("tensor", f)
                        hp_mm[par] = pe.inc(1)
                    dn_mm = pe.n

                    for hb in range(4):
                        def f(eng, hb=hb, w=dn_mm, ds=dn_st[hb]):
                            eng.wait_ge(pe.h, w)
                            if ds is not None:
                                eng.wait_ge(ds[0], ds[1])
                            eng.copy(dnst[:, hb * 512:(hb + 1) * 512], pA[hb][:, :]
                                     ).then_inc(act.h, 1)
                        self.em("scalar", f)
                        ev = act.inc(1)
                        self.puse("pA%d" % hb, act.h, ev)

                        def f(eng, hb=hb, qd=qd, tt=tt, w=ev):
                            eng.wait_ge(act.h, w)
                            eng.dma_start(out=D["dnD"][qd * 512 + hb * 128:qd * 512 + (hb + 1) * 128,
                                                       tt * 512:(tt + 1) * 512],
                                          in_=dnst[:, hb * 512:(hb + 1) * 512]).then_inc(stm.h, 16)
                        self.em("sync", f)
                        dn_st[hb] = (stm.h, stm.inc(16))

        self.barrier()

        with (
            nc.sbuf_tensor("dn_t", [128, 16 * 512], F32) as dn_t,
            nc.sbuf_tensor("xn_t", [128, 16 * 512], BF16) as xn_t,
            nc.sbuf_tensor("x2f", [128, 16 * 512], F32) as x2f,
            nc.sbuf_tensor("sqf", [128, 16 * 512], F32) as sqf,
            nc.sbuf_tensor("rows2", [1, 4 * 512], F32) as rows2,
            nc.sbuf_tensor("ostg", [128, 16 * 512], F32) as ostg,
        ):
            for tt in range(TH // 512):
                dnv = D["dnD"][:, tt * 512:(tt + 1) * 512].rearrange("(k p) t -> p k t", p=128)
                xnv = D["x1nD"][:, tt * 512:(tt + 1) * 512].rearrange("(k p) t -> p k t", p=128)

                def f(eng, dnv=dnv, xnv=xnv, wp=pool.n, wpe=pe.n):
                    eng.wait_ge(pool.h, wp)
                    eng.dma_start(out=dn_t[:, :].rearrange("p (k t) -> p k t", t=512),
                                  in_=dnv).then_inc(ld.h, 16)
                    eng.dma_start(out=xn_t[:, :].rearrange("p (k t) -> p k t", t=512),
                                  in_=xnv).then_inc(ld.h, 16)
                self.em("sync", f)
                t_rdy = ld.inc(32)

                def f(eng, w=t_rdy, wd=dve.n):
                    eng.wait_ge(ld.h, w)
                    eng.wait_ge(dve.h, wd)
                    last = None
                    for k in range(16):
                        last = eng.tensor_add(x2f[:, k * 512:(k + 1) * 512],
                                              dn_t[:, k * 512:(k + 1) * 512],
                                              xn_t[:, k * 512:(k + 1) * 512])
                    last.then_inc(pool.h, 1)
                self.em("gpsimd", f)
                x2_done = pool.inc(1)

                def f(eng, w=x2_done, pf0=self.pfree.get("pB0"), OCF=self.C_OCOLF):
                    eng.wait_ge(pool.h, w)
                    if pf0 is not None:
                        eng.wait_ge(pf0[0], pf0[1])
                    ins = None
                    for k in range(16):
                        ins = eng.matmul(pB[0][0:1, 0:512], OCF, x2f[:, k * 512:(k + 1) * 512],
                                         start=(k == 0), stop=(k == 15))
                    ins.then_inc(pe.h, 1)
                self.em("tensor", f)
                summ = pe.inc(1)

                def f(eng, w=x2_done, wp=pe.n):
                    eng.wait_ge(pool.h, w)
                    eng.wait_ge(pe.h, wp)
                    last = None
                    for k in range(16):
                        last = eng.activation(out=sqf[:, k * 512:(k + 1) * 512],
                                              in_=x2f[:, k * 512:(k + 1) * 512],
                                              func=AF.Square)
                    last.then_inc(act.h, 1)
                self.em("scalar", f)
                sqb = act.inc(1)

                def f(eng, w=sqb, pf1=self.pfree.get("pB1"), OCF=self.C_OCOLF):
                    eng.wait_ge(act.h, w)
                    if pf1 is not None:
                        eng.wait_ge(pf1[0], pf1[1])
                    ins = None
                    for k in range(16):
                        ins = eng.matmul(pB[1][0:1, 0:512], OCF, sqf[:, k * 512:(k + 1) * 512],
                                         start=(k == 0), stop=(k == 15))
                    ins.then_inc(pe.h, 1)
                self.em("tensor", f)
                sq_mm = pe.inc(1)

                def f(eng, a=summ, b=sq_mm):
                    eng.wait_ge(pe.h, max(a, b))
                    eng.activation(out=rows2[0:1, 0:512], in_=pB[0][0:1, 0:512],
                                   func=AF.Copy, scale=self.CF_I2048[0:1, :])
                    eng.activation(out=rows2[0:1, 512:1024], in_=rows2[0:1, 0:512],
                                   func=AF.Square).then_inc(act.h, 1)
                self.em("scalar", f)
                mrow = act.inc(1)

                def f(eng, w=mrow):
                    eng.wait_ge(act.h, w)
                    eng.scalar_tensor_tensor(rows2[0:1, 1024:1536], pB[1][0:1, 0:512],
                                             1.0 / 2048.0, rows2[0:1, 512:1024],
                                             ALU.mult, ALU.subtract).then_inc(dve.h, 1)
                self.em("vector", f)
                varr = dve.inc(1)
                self.puse("pB1", dve.h, varr)

                def f(eng, w=varr):
                    eng.wait_ge(dve.h, w)
                    eng.activation(out=rows2[0:1, 1536:2048], in_=rows2[0:1, 1024:1536],
                                   func=AF.Ln, bias=self.CF_EPS5[0:1, :])
                    eng.activation(out=rows2[0:1, 1536:2048], in_=rows2[0:1, 1536:2048],
                                   func=AF.Exp, scale=-0.5).then_inc(act.h, 1)
                self.em("scalar", f)
                rrow = act.inc(1)

                def f(eng, w=rrow, ORF=self.CF_OROWF):
                    eng.wait_ge(act.h, w)
                    eng.matmul(pB[0][:, 0:512], ORF, rows2[0:1, 0:512], start=True, stop=True)
                    eng.matmul(pB[1][:, 0:512], ORF, rows2[0:1, 1536:2048], start=True,
                               stop=True).then_inc(pe.h, 1)
                self.em("tensor", f)
                bcs = pe.inc(1)

                def f(eng, w=bcs, ws=stm.n):
                    eng.wait_ge(pe.h, w)
                    eng.wait_ge(stm.h, ws)
                    last = None
                    for k in range(16):
                        dst = ostg[:, k * 512:(k + 1) * 512]
                        last = eng.tensor_sub(dst, x2f[:, k * 512:(k + 1) * 512], pB[0][:, 0:512])
                        last = eng.tensor_mul(dst, dst, pB[1][:, 0:512])
                    last.then_inc(dve.h, 1)
                self.em("vector", f)
                odone = dve.inc(1)
                self.puse("pB0", dve.h, odone)
                self.puse("pB1", dve.h, odone)

                ov = D["outD"][:, tt * 512:(tt + 1) * 512].rearrange("(k p) t -> p k t", p=128)

                def f(eng, ov=ov, w=odone):
                    eng.wait_ge(dve.h, w)
                    eng.dma_start(out=ov, in_=ostg[:, :].rearrange("p (k t) -> p k t", t=512)
                                  ).then_inc(stm.h, 16)
                self.em("sync", f)
                stm.inc(16)


def build_nc(debug_taps=(), stop_after=None):
    return KB(debug_taps, stop_after).build()


def kernel(**inputs):
    consts = _host_consts()
    in_maps = [_host_core_inputs(c, inputs, consts) for c in range(8)]
    nc = build_nc()
    res = run_bass_kernel_spmd(nc, in_maps, core_ids=list(range(8)))
    out = np.empty((4, S, H), np.float32)
    for c in range(8):
        b, hg = c // 2, c % 2
        out[b, hg * TH:(hg + 1) * TH, :] = np.asarray(res.results[c]["out"]).T
    return out


# version 09
# speedup vs baseline: 1.3488x; 1.0034x over previous
import contextlib
import numpy as np
import ml_dtypes
import concourse.bass as bass
import concourse.mybir as mybir
from concourse.bass_utils import run_bass_kernel_spmd

F32 = mybir.dt.float32
BF16 = mybir.dt.bfloat16
FP16 = mybir.dt.float16
AF = mybir.ActivationFunctionType
ALU = mybir.AluOpType
BFnp = ml_dtypes.bfloat16

S = 8192
H = 2048
NH = 16
DK = DV = 128
CH = 128
NCHUNK = S // CH
SLAB = 256
NSLAB = S // SLAB
PCOLS = 6176
INTER = 8192
TH = S // 2
MP = SLAB + 8

REPL = [[0, 1], [2, 3], [4, 5], [6, 7]]
LN128H = 0.5 * float(np.log(128.0))
RSQ128 = float(128.0 ** -0.5)


def _host_consts():
    ident = np.eye(128, dtype=BFnp)
    identf = np.eye(128, dtype=np.float32)
    m_us_neg = (-np.triu(np.ones((128, 128), np.float32), 1)).astype(BFnp)
    m_ui = np.triu(np.ones((128, 128), np.float32), 0).astype(BFnp)
    indt = np.zeros((64, 4096), np.float16)
    for h in range(16):
        for off in (0, 2048):
            indt[h, off + h * 128:off + (h + 1) * 128] = -1.0
            indt[32 + h, off + h * 128:off + (h + 1) * 128] = -1.0
    oh16 = np.zeros((128, 256), np.float32)
    for cb in range(16):
        oh16[:, cb * 16 + cb] = 1.0
    return {"IDENT": ident, "IDENTF": identf, "MASKUSN": m_us_neg, "MASKUI": m_ui,
            "INDT": indt, "OH16": oh16.astype(BFnp),
            "ONESROW": np.ones((1, 128), BFnp), "ONESCOL": np.ones((128, 1), BFnp),
            "ONESROWF": np.ones((1, 128), np.float32),
            "ONESCOLF": np.ones((128, 1), np.float32),
            "ONES2R": np.ones((2, 128), np.float16)}


def _host_core_inputs(core, inp, consts):
    b, hg = core // 2, core % 2
    w_qkv = np.asarray(inp["L0_attn_w_qkv"])
    w_z = np.asarray(inp["L0_attn_w_z"])
    w_b = np.asarray(inp["L0_attn_w_b"])
    w_a = np.asarray(inp["L0_attn_w_a"])
    m = dict(consts)
    xTfull = np.ascontiguousarray(np.asarray(inp["L0_attn_hidden_states"])[b].T)
    m["xT"] = xTfull
    m["xTd"] = np.ascontiguousarray(xTfull[:, hg * TH:(hg + 1) * TH])
    m["wproj"] = np.ascontiguousarray(np.concatenate([
        w_qkv[:, hg * 1024:(hg + 1) * 1024],
        w_qkv[:, 2048 + hg * 1024:2048 + (hg + 1) * 1024],
        w_qkv[:, 4096 + hg * 2048:4096 + (hg + 1) * 2048],
        w_z[:, hg * 2048:(hg + 1) * 2048],
        w_b[:, hg * 16:(hg + 1) * 16],
        w_a[:, hg * 16:(hg + 1) * 16]], axis=1))
    cw = np.asarray(inp["L0_attn_conv_w"])[:, 0, :]
    ch_rows = np.concatenate([
        cw[hg * 1024:(hg + 1) * 1024],
        cw[2048 + hg * 1024:2048 + (hg + 1) * 1024],
        cw[4096 + hg * 2048:4096 + (hg + 1) * 2048]], axis=0)
    m["convw"] = np.ascontiguousarray(
        ch_rows.reshape(32, 128, 4).transpose(1, 0, 2).reshape(128, 128))
    m["wout"] = np.ascontiguousarray(np.asarray(inp["L0_attn_w_out"])[hg * 2048:(hg + 1) * 2048, :])
    wgu_in = np.asarray(inp["L0_ffn_w_gate_up"])
    wg = wgu_in[:INTER].T
    wu = wgu_in[INTER:].T
    gu = np.empty((H, 2 * INTER), np.float32)
    for i in range(INTER // 128):
        gu[:, (2 * i) * 128:(2 * i + 1) * 128] = wg[:, i * 128:(i + 1) * 128]
        gu[:, (2 * i + 1) * 128:(2 * i + 2) * 128] = wu[:, i * 128:(i + 1) * 128]
    m["wgu"] = np.ascontiguousarray(gu)
    m["wdn"] = np.ascontiguousarray(np.asarray(inp["L0_ffn_w_down"]).T)
    m["dtb"] = np.ascontiguousarray(
        np.asarray(inp["L0_attn_dt_bias"]).astype(np.float32)[hg * 16:(hg + 1) * 16].reshape(16, 1))
    m["alog"] = np.ascontiguousarray(
        np.asarray(inp["L0_attn_A_log"]).astype(np.float32)[hg * 16:(hg + 1) * 16].reshape(16, 1))
    m["nrmw"] = np.ascontiguousarray(
        np.asarray(inp["L0_attn_norm_w"]).astype(np.float32).reshape(128, 1))
    return m


class Sem:
    def __init__(self, handle):
        self.h = handle
        self.n = 0

    def inc(self, k=1):
        self.n += k
        return self.n


class KB:
    def __init__(self, debug_taps=(), stop_after=None):
        self.debug_taps = set(debug_taps)
        self.stop_after = stop_after
        self.nc = bass.Bass("TRN2", target_bir_lowering=False, debug=False)
        self.Q = {e: [] for e in ("tensor", "vector", "scalar", "gpsimd", "sync")}
        self.st_ = {}
        self.pfree = {}
        self.flats_done = {}

    def em(self, e, fn):
        self.Q[e].append(fn)

    def dram(self, name, shape, dtype):
        k = "ExternalOutput" if name in self.debug_taps else "Internal"
        return self.nc.dram_tensor(name, shape, dtype, kind=k).ap()

    def pw(self, eng, name):
        v = self.pfree.get(name)
        if v is not None:
            eng.wait_ge(v[0], v[1])

    def puse(self, name, sem, val):
        self.pfree[name] = (sem, val)

    def dump(self, name, ap, shape, dtype):
        if name not in self.debug_taps:
            return
        D = self.st_
        if name not in D:
            D[name] = self.nc.dram_tensor(name, shape, dtype, kind="ExternalOutput").ap()
        waits = [(s.h, s.n) for s in (self.pe, self.dve, self.act, self.pool) if s.n > 0]
        dst = D[name]

        def f(eng, waits=tuple(waits), dst=dst, ap=ap):
            for h, v in waits:
                eng.wait_ge(h, v)
            eng.dma_start(out=dst, in_=ap).then_inc(self.stm.h, 16)
        self.em("sync", f)
        self.stm.inc(16)

    def barrier(self):
        waits = [(s.h, s.n) for s in
                 (self.pe, self.dve, self.act, self.pool, self.ld, self.cast,
                  self.stm, self.sm, self.ccm) if s.n > 0]
        for e in self.Q:
            def f(eng, waits=tuple(waits)):
                for h, v in waits:
                    eng.wait_ge(h, v)
            self.em(e, f)
        self.pfree = {}

    def build(self):
        nc = self.nc
        with contextlib.ExitStack() as es:
            self._declare(es)
            with contextlib.ExitStack() as es_ps:
                self._declare_ps(es_ps)
                self._setup()
                self._phase_P()
                if self.stop_after == "P":
                    es_ps_hold = True
                else:
                    self._phase_S()
            if self.stop_after not in ("P", "S"):
                self._phase_D()

            stm, sm, cast, ld, ccm = self.stm, self.sm, self.cast, self.ld, self.ccm

            def f(eng, a=stm.n, b=sm.n, c=cast.n, d=ld.n, e=ccm.n):
                eng.wait_ge(stm.h, a)
                eng.wait_ge(sm.h, b)
                eng.wait_ge(cast.h, c)
                eng.wait_ge(ld.h, d)
                if e:
                    eng.wait_ge(ccm.h, e)
            self.em("sync", f)

            blk = es.enter_context(nc.Block())
            Q = self.Q

            @blk.tensor
            def _(eng):
                for fn in Q["tensor"]:
                    fn(eng)

            @blk.vector
            def _(eng):
                for fn in Q["vector"]:
                    fn(eng)

            @blk.scalar
            def _(eng):
                for fn in Q["scalar"]:
                    fn(eng)

            @blk.gpsimd
            def _(eng):
                for fn in Q["gpsimd"]:
                    fn(eng)

            @blk.sync
            def _(eng):
                for fn in Q["sync"]:
                    fn(eng)
        return nc

    def _declare(self, es):
        nc, D = self.nc, self.st_
        for nm, sh, dt in (
            ("xT", [H, S], F32), ("xTd", [H, TH], F32), ("wproj", [H, PCOLS], F32),
            ("convw", [128, 128], F32), ("wout", [H, H], F32), ("wgu", [H, 2 * INTER], F32),
            ("wdn", [INTER, H], F32), ("dtb", [16, 1], F32), ("alog", [16, 1], F32),
            ("nrmw", [128, 1], F32), ("IDENT", [128, 128], BF16), ("IDENTF", [128, 128], F32),
            ("MASKUSN", [128, 128], BF16), ("MASKUI", [128, 128], BF16),
            ("INDT", [64, 4096], FP16), ("OH16", [128, 256], BF16),
            ("ONESROW", [1, 128], BF16), ("ONESCOL", [128, 1], BF16),
            ("ONESROWF", [1, 128], F32), ("ONESCOLF", [128, 1], F32),
            ("ONES2R", [2, 128], FP16),
        ):
            D[nm] = nc.dram_tensor(nm, sh, dt, kind="ExternalInput").ap()
        D["mixedD"] = self.dram("mixedD", [4096, S], BF16)
        D["zsD"] = self.dram("zsD", [H, S], BF16)
        D["onD"] = self.dram("onD", [H, S], BF16)
        D["rsIn"] = self.dram("rsIn", [2 * H, TH], BF16)
        D["rsOut"] = self.dram("rsOut", [H, TH], BF16)
        D["hD"] = self.dram("hD", [INTER, TH], BF16)
        D["x1nD"] = self.dram("x1nD", [H, TH], BF16)
        D["dnD"] = self.dram("dnD", [H, TH], F32)
        D["smallsD"] = self.dram("smallsD", [128, S], F32)
        D["csplitD"] = self.dram("csplitD", [48, S], FP16)
        D["cbsplitD"] = self.dram("cbsplitD", [48, S], FP16)
        D["cD"] = self.dram("cD", [16, S], F32)
        D["l2rD"] = self.dram("l2rD", [16, SLAB], F32)
        D["outD"] = nc.dram_tensor("out", [H, TH], F32, kind="ExternalOutput").ap()

        for nm, sh, dt in (
            ("cons", [128, 1024], BF16), ("consf", [128, 272], F32),
            ("identf", [128, 128], F32), ("ones2", [2, 128], FP16),
            ("onescf", [128, 1], F32),
        ):
            D[nm] = es.enter_context(nc.sbuf_tensor(nm, sh, dt))
        for nm, sh, dt in (
            ("pA0", [128, 512], F32), ("pA1", [128, 512], F32),
            ("pA2", [128, 512], F32), ("pA3", [128, 512], F32),
            ("pB0", [128, 512], F32), ("pB1", [128, 512], F32),
            ("pT0", [128, 1024], BF16), ("pT1", [128, 1024], BF16),
        ):
            D[nm] = es.enter_context(nc.psum_tensor(nm, sh, dt))
        self.pA = [D["pA0"], D["pA1"], D["pA2"], D["pA3"]]
        self.pB = [D["pB0"], D["pB1"]]
        self.pT = [D["pT0"], D["pT1"]]
        for nm in ("pe", "dve", "act", "pool", "ld", "cast", "stm", "sm", "ccm"):
            setattr(self, nm, Sem(es.enter_context(nc.semaphore("s_" + nm))))
        cons, consf = D["cons"], D["consf"]
        self.C_ID = cons[:, 0:128]
        self.C_MUSN = cons[:, 128:256]
        self.C_MUI = cons[:, 256:384]
        self.C_OH = lambda cb: cons[:, 384 + cb * 16:384 + (cb + 1) * 16]
        self.C_OCOL = cons[:, 640:641]
        self.C_OROW = cons[0:1, 642:770]
        self.CF_CONV = lambda cb, j: consf[:, cb * 4 + j:cb * 4 + j + 1]
        self.CF_DTB = consf[0:16, 128:129]
        self.CF_ALOG = consf[0:16, 129:130]
        self.CF_NEGEA = consf[0:16, 130:131]
        self.CF_NRMW = consf[:, 131:132]
        self.CF_OROWF = consf[0:1, 132:260]
        self.CF_DTB32 = consf[32:48, 260:261]
        self.CF_NEGEA96 = consf[96:112, 261:262]
        self.CF_EPS6 = consf[:, 262:263]
        self.CF_RMSE = consf[:, 263:264]
        self.CF_LN128H = consf[:, 264:265]
        self.CF_EPS5 = consf[:, 265:266]
        self.CF_I2048 = consf[:, 266:267]
        self.C_IDF = D["identf"][:, :]
        self.C_OCOLF = D["onescf"][:, :]

    def _declare_ps(self, es_ps):
        nc, D = self.nc, self.st_
        for nm, sh, dt in (
            ("persF", [128, S], F32), ("csplit", [66, S], FP16), ("cbsplit", [66, S], FP16),
        ):
            D[nm] = es_ps.enter_context(nc.sbuf_tensor(nm, sh, dt))

    def _setup(self):
        D = self.st_
        ld, act, dve = self.ld, self.act, self.dve
        cons, consf = D["cons"], D["consf"]

        def f(eng):
            eng.dma_start(out=cons[:, 0:128], in_=D["IDENT"][:, :]).then_inc(ld.h, 16)
            eng.dma_start(out=cons[:, 128:256], in_=D["MASKUSN"][:, :]).then_inc(ld.h, 16)
            eng.dma_start(out=cons[:, 256:384], in_=D["MASKUI"][:, :]).then_inc(ld.h, 16)
            eng.dma_start(out=cons[:, 384:640], in_=D["OH16"][:, :]).then_inc(ld.h, 16)
            eng.dma_start(out=cons[:, 640:641], in_=D["ONESCOL"][:, :]).then_inc(ld.h, 16)
            eng.dma_start(out=cons[0:1, 642:770], in_=D["ONESROW"][:, :]).then_inc(ld.h, 16)
            eng.dma_start(out=consf[:, 0:128], in_=D["convw"][:, :]).then_inc(ld.h, 16)
            eng.dma_start(out=consf[0:16, 128:129], in_=D["dtb"][:, :]).then_inc(ld.h, 16)
            eng.dma_start(out=consf[0:16, 129:130], in_=D["alog"][:, :]).then_inc(ld.h, 16)
            eng.dma_start(out=consf[:, 131:132], in_=D["nrmw"][:, :]).then_inc(ld.h, 16)
            eng.dma_start(out=consf[0:1, 132:260], in_=D["ONESROWF"][:, :]).then_inc(ld.h, 16)
            eng.dma_start(out=D["ones2"][:, :], in_=D["ONES2R"][:, :]).then_inc(ld.h, 16)
            eng.dma_start(out=D["identf"][:, :], in_=D["IDENTF"][:, :]).then_inc(ld.h, 16)
            eng.dma_start(out=D["onescf"][:, :], in_=D["ONESCOLF"][:, :]).then_inc(ld.h, 16)
        self.em("sync", f)
        setup_ld = ld.inc(16 * 14)

        def f(eng, w=setup_ld):
            eng.wait_ge(ld.h, w)
            eng.memset(consf[:, 262:263], 1e-6)
            eng.memset(consf[:, 263:264], 128.0 * 1e-6)
            eng.memset(consf[:, 264:265], LN128H)
            eng.memset(consf[:, 265:266], 1e-5)
            eng.memset(consf[:, 266:267], 1.0 / 2048.0)
            eng.memset(D["csplit"][0:32, :], 0.0)
            eng.memset(D["csplit"][32:64, :], 0.0)
            eng.memset(D["csplit"][64:66, :], 1.0)
            eng.memset(D["cbsplit"][0:32, :], 0.0)
            eng.memset(D["cbsplit"][32:64, :], 0.0)
            eng.memset(D["cbsplit"][64:66, :], -1.0).then_inc(dve.h, 1)
        self.em("vector", f)
        dve.inc(1)

        def f(eng, w=setup_ld):
            eng.wait_ge(ld.h, w)
            eng.activation(out=consf[0:16, 130:131], in_=consf[0:16, 129:130],
                           func=AF.Exp).then_inc(act.h, 1)
        self.em("scalar", f)
        ea = act.inc(1)

        def f(eng, w=ea, NEG=self.CF_NEGEA):
            eng.wait_ge(act.h, w)
            eng.tensor_scalar_mul(NEG, NEG, -1.0)
            eng.tensor_copy(consf[32:48, 260:261], consf[0:16, 128:129])
            eng.tensor_copy(consf[96:112, 261:262], NEG).then_inc(dve.h, 1)
        self.em("vector", f)
        dve.inc(1)

    def _phase_P(self):
        nc, D = self.nc, self.st_
        pe, dve, act, ld, cast, stm, sm = (self.pe, self.dve, self.act, self.ld,
                                           self.cast, self.stm, self.sm)
        pA = self.pA
        QT = 2048
        NQ = S // QT
        FG = 256
        fgroups = [(i * FG, FG) for i in range(24)] + [(6144, 16), (6160, 16)]

        with (
            nc.sbuf_tensor("x_q", [128, 16 * QT], BF16) as x_q,
            nc.sbuf_tensor("wwin", [128, 2 * 16 * FG], BF16) as wwin,
            nc.sbuf_tensor("stg", [128, 4 * 512], BF16) as stg,
            nc.sbuf_tensor("traF", [128, S], F32) as traF,
        ):
            stg_free = [None] * 4
            pA_free = [None] * 4
            wwin_use = [0, 0]
            xq_use = [0]
            rr = [0]

            for q in range(NQ):
                t_base = q * QT
                xv = D["xT"][:, t_base:t_base + QT].rearrange("(k p) t -> p k t", p=128)

                def f(eng, xv=xv, xw=xq_use[0]):
                    eng.wait_ge(pe.h, xw)
                    eng.dma_start(out=x_q[:, :].rearrange("p (k t) -> p k t", t=QT),
                                  in_=xv).then_inc(cast.h, 16)
                self.em("gpsimd", f)
                xq_done = cast.inc(16)

                for gi, (c0, cw) in enumerate(fgroups):
                    par = gi % 2
                    wv = D["wproj"][:, c0:c0 + cw].rearrange("(k p) f -> p k f", p=128)

                    def f(eng, wv=wv, par=par, cw=cw, wprev=wwin_use[par]):
                        eng.wait_ge(pe.h, wprev)
                        eng.dma_start(out=wwin[:, par * (16 * FG):par * (16 * FG) + 16 * cw
                                              ].rearrange("p (k f) -> p k f", f=cw),
                                      in_=wv).then_inc(cast.h, 16)
                    self.em("gpsimd", f)
                    wwin_done = cast.inc(16)

                    for fb in range(max(1, cw // 128)):
                        fbw = min(128, cw)
                        fcol = c0 + fb * 128
                        is_z = 4096 <= fcol < 6144
                        is_b = fcol == 6144
                        is_a = fcol == 6160
                        for tt in range(QT // 512):
                            bank = rr[0] % 4
                            sgi = rr[0] % 4
                            rr[0] += 1

                            def f(eng, bank=bank, par=par, fb=fb, fbw=fbw, tt=tt, cw=cw,
                                  wd=wwin_done, xd=xq_done, pf=pA_free[bank]):
                                eng.wait_ge(cast.h, max(wd, xd))
                                if pf is not None:
                                    eng.wait_ge(pf[0], pf[1])
                                base = par * (16 * FG)
                                ins = None
                                for k in range(16):
                                    ins = eng.matmul(
                                        pA[bank][0:fbw, :],
                                        wwin[:, base + k * cw + fb * 128:
                                             base + k * cw + fb * 128 + fbw],
                                        x_q[:, k * QT + tt * 512:k * QT + (tt + 1) * 512],
                                        start=(k == 0), stop=(k == 15))
                                ins.then_inc(pe.h, 1)
                            self.em("tensor", f)
                            mm_done = pe.inc(1)
                            wwin_use[par] = mm_done
                            xq_use[0] = mm_done

                            def f(eng, bank=bank, sgi=sgi, md=mm_done, sf=stg_free[sgi],
                                  is_z=is_z, is_b=is_b, is_a=is_a, fbw=fbw, tt=tt, t_base=t_base):
                                eng.wait_ge(pe.h, md)
                                if sf is not None and not (is_b or is_a):
                                    eng.wait_ge(sf[0], sf[1])
                                tsl = slice(t_base + tt * 512, t_base + (tt + 1) * 512)
                                sg = stg[:, sgi * 512:(sgi + 1) * 512]
                                if is_b:
                                    eng.copy(traF[0:16, tsl], pA[bank][0:16, :]).then_inc(act.h, 1)
                                elif is_a:
                                    eng.copy(traF[32:48, tsl], pA[bank][0:16, :]).then_inc(act.h, 1)
                                elif is_z:
                                    eng.activation(out=sg[0:fbw, :], in_=pA[bank][0:fbw, :],
                                                   func=AF.Silu).then_inc(act.h, 1)
                                else:
                                    eng.copy(sg[0:fbw, :], pA[bank][0:fbw, :]).then_inc(act.h, 1)
                            self.em("scalar", f)
                            ev = act.inc(1)
                            pA_free[bank] = (act.h, ev)

                            if not (is_b or is_a):
                                dst = (D["zsD"][fcol - 4096:fcol - 4096 + fbw,
                                                t_base + tt * 512:t_base + (tt + 1) * 512]
                                       if is_z else
                                       D["mixedD"][fcol:fcol + fbw,
                                                   t_base + tt * 512:t_base + (tt + 1) * 512])

                                def f(eng, dst=dst, sgi=sgi, ev=ev, fbw=fbw):
                                    eng.wait_ge(act.h, ev)
                                    eng.dma_start(out=dst,
                                                  in_=stg[:, sgi * 512:(sgi + 1) * 512][0:fbw, :]
                                                  ).then_inc(stm.h, 16)
                                self.em("sync", f)
                                stg_free[sgi] = (stm.h, stm.inc(16))

            persF, csplit, cbsplit = D["persF"], D["csplit"], D["cbsplit"]

            def f(eng, w=act.n, DTB32=self.CF_DTB32):
                eng.wait_ge(act.h, w)
                b_ = traF[0:16, :]
                a_ = traF[32:48, :]
                eng.activation(out=persF[32:48, :], in_=b_, func=AF.Sigmoid)
                eng.activation(out=persF[64:80, :], in_=b_, func=AF.Abs)
                eng.activation(out=persF[64:80, :], in_=persF[64:80, :], func=AF.Exp, scale=-1.0)
                eng.activation(out=persF[64:80, :], in_=persF[64:80, :], func=AF.Ln, bias=1.0)
                eng.activation(out=persF[96:112, :], in_=a_, func=AF.Abs, bias=DTB32)
                eng.activation(out=persF[96:112, :], in_=persF[96:112, :], func=AF.Exp, scale=-1.0)
                eng.activation(out=persF[96:112, :], in_=persF[96:112, :], func=AF.Ln, bias=1.0
                               ).then_inc(act.h, 1)
            self.em("scalar", f)
            sp_done = act.inc(1)

            def f(eng, w=sp_done, DTB32=self.CF_DTB32, NEG96=self.CF_NEGEA96):
                eng.wait_ge(act.h, w)
                b_ = traF[0:16, :]
                a_ = traF[32:48, :]
                eng.tensor_scalar(traF[64:80, :], b_, -1.0, 0.0, ALU.mult, ALU.max)
                eng.scalar_tensor_tensor(persF[64:80, :], traF[64:80, :], -1.0, persF[64:80, :],
                                         ALU.mult, ALU.subtract)
                eng.tensor_scalar(traF[96:112, :], a_, DTB32, 0.0, ALU.add, ALU.max)
                eng.tensor_add(persF[96:112, :], traF[96:112, :], persF[96:112, :])
                eng.tensor_scalar_mul(persF[96:112, :], persF[96:112, :], NEG96)
                eng.memset(traF[96:112, 0:CH], 1.0)
                for cix in range(NCHUNK):
                    sl = slice(cix * CH, (cix + 1) * CH)
                    eng.tensor_tensor_scan(persF[0:16, sl], traF[96:112, 0:CH],
                                           persF[96:112, sl], 0.0, ALU.mult, ALU.add)
                eng.tensor_copy(traF[64:80, :], persF[0:16, :])
                eng.tensor_add(traF[64:80, :], traF[64:80, :], persF[64:80, :])
                eng.tensor_copy(csplit[0:16, :], persF[0:16, :])
                eng.tensor_copy(cbsplit[0:16, :], traF[64:80, :])
                eng.tensor_copy(traF[0:16, :], csplit[0:16, :])
                eng.tensor_sub(csplit[32:48, :], persF[0:16, :], traF[0:16, :])
                eng.tensor_copy(persF[64:80, :], cbsplit[0:16, :])
                eng.tensor_sub(cbsplit[32:48, :], traF[64:80, :], persF[64:80, :]
                               ).then_inc(dve.h, 1)
            self.em("vector", f)
            split_done = dve.inc(1)

            def f(eng, w=split_done):
                eng.wait_ge(dve.h, w)
                eng.dma_start(out=D["csplitD"][:, :], in_=D["csplit"][0:48, :]).then_inc(sm.h, 16)
                eng.dma_start(out=D["cbsplitD"][:, :], in_=D["cbsplit"][0:48, :]).then_inc(sm.h, 16)
                eng.dma_start(out=D["cD"][:, :], in_=persF[0:16, :]).then_inc(sm.h, 16)
            self.em("scalar", f)
            cstores = sm.inc(48)

            if "smallsD" in self.debug_taps:
                def f(eng, a=dve.n, b=act.n):
                    eng.wait_ge(dve.h, a)
                    eng.wait_ge(act.h, b)
                    eng.dma_start(out=D["smallsD"][:, :], in_=persF[:, :]).then_inc(stm.h, 16)
                self.em("sync", f)
                stm.inc(16)

            self.barrier()

    def _phase_S(self):
        nc, D = self.nc, self.st_
        pe, dve, act, pool, ld, cast, stm, sm = (self.pe, self.dve, self.act, self.pool,
                                                 self.ld, self.cast, self.stm, self.sm)
        pA, pB, pT = self.pA, self.pB, self.pT
        persF = D["persF"]

        with contextlib.ExitStack() as ess:
            for nm, sh, dt in (
                ("rhsT", [66, 4096], FP16), ("gerow", [1, 1024], F32),
                ("state", [128, NH * DV], F32), ("stbf", [128, NH * DV], BF16),
            ):
                D[nm] = ess.enter_context(nc.sbuf_tensor(nm, sh, dt))
            sc = {}
            for nm, sh, dt in (
                ("mix", [128, 32 * MP], BF16), ("qkvc", [128, 32 * SLAB], BF16),
                ("zs_s", [128, 16 * CH], BF16),
                ("l2r", [16, SLAB], F32), ("rowb", [1, 4096], BF16),
                ("argA", [128, 2048], BF16), ("argQ", [128, 2048], BF16),
                ("AT", [128, 2048], BF16), ("AO", [128, 2048], BF16), ("AQ", [128, 2048], BF16),
                ("Ysb", [128, 16 * 256], BF16),
                ("Pw2", [128, 2048], BF16), ("PwT2", [128, 2048], BF16),
                ("KW", [128, 2048], BF16), ("W2T", [128, 512], BF16), ("Usb", [128, 512], BF16),
                ("ogsb", [128, 2048], BF16), ("og2", [128, 2048], BF16),
                ("stage", [128, 128], F32), ("colsb", [128, 128], F32),
                ("gqb", [128, 2048], BF16), ("gendb", [128, 16], F32),
                ("onstg", [128, 2 * 512], BF16),
                ("cvt", [128, 2 * SLAB], F32),
                ("cflat2", [2, 2048], FP16),
                ("scr16", [16, 16], F32),
            ):
                sc[nm] = ess.enter_context(nc.sbuf_tensor(nm, sh, dt))
            self.sc = sc

            sm = self.sm

            def f(eng):
                eng.dma_start(out=D["rhsT"][0:64, :], in_=D["INDT"][:, :]).then_inc(ld.h, 16)
                srcv = D["cD"][0:16, :].rearrange("h (c t) -> h c t", t=CH)[:, :, CH - 1:CH]
                srcv = srcv.squeeze(2).transpose([1, 0]).unsqueeze(0)
                outv = D["gerow"][0:1, 0:1024].rearrange("one (c h) -> one c h", h=16)
                eng.dma_start(out=outv, in_=srcv,
                              allow_slow_non_contiguous=True).then_inc(ld.h, 16)
            self.em("sync", f)
            sinit_ld = ld.inc(32)

            def f(eng, w=sinit_ld):
                eng.wait_ge(ld.h, w)
                eng.activation(out=D["gerow"][0:1, 0:1024], in_=D["gerow"][0:1, 0:1024],
                               func=AF.Exp).then_inc(act.h, 1)
            self.em("scalar", f)
            ge_exp = act.inc(1)

            def f(eng, w=ge_exp):
                eng.wait_ge(act.h, w)
                eng.memset(D["state"][:, :], 0.0)
                eng.memset(sc["stage"][:, :], 0.0).then_inc(dve.h, 1)
            self.em("vector", f)
            dve.inc(1)

            def f(eng, w=dve.n):
                eng.wait_ge(dve.h, w)
                eng.tensor_copy(D["stbf"][:, :], D["state"][:, :]).then_inc(pool.h, 1)
            self.em("gpsimd", f)
            pool.inc(1)

            prev_slab_pe = [0]
            prev = {"argmm": 0, "stagemm": 0, "onst": [None, None]}

            for sb in range(NSLAB):
                t0 = sb * SLAB
                mixv = sc["mix"][:, :].rearrange("p (b c) -> p b c", c=MP)
                if sb == 0:
                    def f(eng):
                        eng.memset(sc["mix"][:, :].rearrange("p (b c) -> p b c", c=MP)[:, :, 0:8],
                                   0.0).then_inc(pool.h, 1)
                    self.em("gpsimd", f)
                else:
                    def f(eng, dv=dve.n, pl=pool.n):
                        eng.wait_ge(dve.h, dv)
                        v = sc["mix"][:, :].rearrange("p (b c) -> p b c", c=MP)
                        eng.tensor_copy(v[:, :, 0:8], v[:, :, SLAB:SLAB + 8]).then_inc(pool.h, 1)
                    self.em("gpsimd", f)
                halo_done = pool.inc(1)

                mv = D["mixedD"][:, t0:t0 + SLAB].rearrange("(b p) t -> p b t", p=128)

                def f(eng, mv=mv, hd=halo_done, dv=dve.n, pl=pool.n, pp=prev_slab_pe[0]):
                    eng.wait_ge(pool.h, hd)
                    eng.wait_ge(dve.h, dv)
                    eng.wait_ge(pe.h, pp)
                    eng.dma_start(out=sc["mix"][:, :].rearrange("p (b c) -> p b c", c=MP)[:, :, 8:8 + SLAB],
                                  in_=mv).then_inc(ld.h, 16)
                self.em("sync", f)
                mix_rdy = ld.inc(16)

                act_base = act.n
                for cb in range(32):
                    cpar = cb % 2

                    def f(eng, cb=cb, cpar=cpar, mr=mix_rdy, pw=prev_slab_pe[0],
                          CF=self.CF_CONV, wact=(act.n - 1 if cb >= 2 else 0)):
                        if cb == 0:
                            eng.wait_ge(ld.h, mr)
                            eng.wait_ge(pe.h, pw)
                        if cb >= 2:
                            eng.wait_ge(act.h, wact)
                        v = sc["mix"][:, :].rearrange("p (b c) -> p b c", c=MP)
                        dst = sc["cvt"][:, cpar * SLAB:(cpar + 1) * SLAB]
                        eng.tensor_scalar_mul(dst, v[:, cb, 8:8 + SLAB], CF(cb, 3))
                        for j in (2, 1, 0):
                            ins = eng.scalar_tensor_tensor(
                                dst, v[:, cb, 5 + j:5 + j + SLAB], CF(cb, j), dst,
                                ALU.mult, ALU.add)
                        ins.then_inc(dve.h, 1)
                    self.em("vector", f)
                    cdone = dve.inc(1)

                    def f(eng, cb=cb, cpar=cpar, w=cdone):
                        eng.wait_ge(dve.h, w)
                        eng.activation(out=sc["qkvc"][:, cb * SLAB:(cb + 1) * SLAB],
                                       in_=sc["cvt"][:, cpar * SLAB:(cpar + 1) * SLAB],
                                       func=AF.Silu).then_inc(act.h, 1)
                    self.em("scalar", f)
                    act.inc(1)
                silu_done = act.n

                def f(eng, w=silu_done):
                    eng.wait_ge(act.h, w)
                    for cb in range(16):
                        eng.activation(out=sc["og2"][:, (cb % 2) * SLAB:(cb % 2 + 1) * SLAB],
                                       in_=sc["qkvc"][:, cb * SLAB:(cb + 1) * SLAB],
                                       func=AF.Square).then_inc(act.h, 1)
                self.em("scalar", f)
                sq_base = act.n
                act.inc(16)

                def f(eng, base=sq_base, pf=self.pfree.get("pB0"), OH=self.C_OH):
                    if pf is not None:
                        eng.wait_ge(pf[0], pf[1])
                    ins = None
                    for cb in range(16):
                        eng.wait_ge(act.h, base + cb + 1)
                        ins = eng.matmul(pB[0][0:16, 0:SLAB], OH(cb),
                                         sc["og2"][:, (cb % 2) * SLAB:(cb % 2 + 1) * SLAB],
                                         start=(cb == 0), stop=(cb == 15))
                    ins.then_inc(pe.h, 1)
                self.em("tensor", f)
                l2mm = pe.inc(1)

                def f(eng, w=l2mm):
                    eng.wait_ge(pe.h, w)
                    eng.activation(out=sc["l2r"][0:16, :], in_=pB[0][0:16, 0:SLAB],
                                   func=AF.Ln, bias=self.CF_EPS6[0:16, :])
                    eng.activation(out=sc["l2r"][0:16, :], in_=sc["l2r"][0:16, :],
                                   func=AF.Exp, scale=-0.5).then_inc(act.h, 1)
                self.em("scalar", f)
                rstd_done = act.inc(1)
                self.puse("pB0", act.h, rstd_done)

                def f(eng, w=rstd_done):
                    eng.wait_ge(act.h, w)
                    eng.tensor_scalar_mul(sc["l2r"][0:8, :], sc["l2r"][0:8, :], RSQ128
                                          ).then_inc(dve.h, 1)
                self.em("vector", f)
                rstds = dve.inc(1)

                def f(eng, w=rstds):
                    eng.wait_ge(dve.h, w)
                    eng.dma_start(out=D["l2rD"][:, :], in_=sc["l2r"][0:16, :]).then_inc(sm.h, 16)
                self.em("scalar", f)
                l2st = sm.inc(16)

                def f(eng, w=l2st, pw=pe.n):
                    eng.wait_ge(sm.h, w)
                    outv = sc["rowb"][0:1, :].rearrange("one (b t) -> one b t", t=SLAB)
                    eng.dma_start(out=outv, in_=D["l2rD"][0:16, :].unsqueeze(0)
                                  ).then_inc(self.cast.h, 16)
                self.em("gpsimd", f)
                l2fl = self.cast.inc(16)

                for cb in range(16):
                    def f(eng, cb=cb, w=l2fl, pf=self.pfree.get("pB0"), OR=self.C_OROW):
                        eng.wait_ge(self.cast.h, w)
                        if pf is not None:
                            eng.wait_ge(pf[0], pf[1])
                        eng.matmul(pB[0][:, (cb % 2) * 256:(cb % 2) * 256 + SLAB],
                                   OR, sc["rowb"][0:1, cb * SLAB:(cb + 1) * SLAB],
                                   start=True, stop=True).then_inc(pe.h, 1)
                    self.em("tensor", f)
                    bc = pe.inc(1)

                    def f(eng, cb=cb, w=bc):
                        eng.wait_ge(pe.h, w)
                        sl = sc["qkvc"][:, cb * SLAB:(cb + 1) * SLAB]
                        eng.tensor_mul(sl, sl, pB[0][:, (cb % 2) * 256:(cb % 2) * 256 + SLAB]
                                       ).then_inc(dve.h, 1)
                    self.em("vector", f)
                    self.puse("pB0", dve.h, dve.inc(1))
                l2_applied = dve.n

                for tc in range(SLAB // CH):
                    ci = sb * (SLAB // CH) + tc
                    self._chunk(ci, tc, l2_applied, prev)
                prev_slab_pe[0] = pe.n

            self.barrier()
        self._wout_pass()
        self.barrier()

    def _emit_flats(self, ci, wait_pe):
        D, sc, sm = self.st_, self.sc, self.sm
        c0 = ci * CH

        def f(eng, w=wait_pe, c0=c0):
            eng.wait_ge(self.pe.h, w)
            for (row, col, srcD, r0) in ((64, 0, "cbsplitD", 0), (65, 0, "cbsplitD", 32),
                                         (64, 2048, "csplitD", 0), (65, 2048, "csplitD", 32)):
                outv = D["rhsT"][row:row + 1, col:col + 2048].rearrange(
                    "one (h t) -> one h t", t=CH)
                inv = D[srcD][r0:r0 + 16, c0:c0 + CH].unsqueeze(0)
                eng.dma_start(out=outv, in_=inv).then_inc(sm.h, 16)
            for (row, r0) in ((0, 0), (1, 32)):
                outv = sc["cflat2"][row:row + 1, :].rearrange("one (h t) -> one h t", t=CH)
                inv = D["csplitD"][r0:r0 + 16, c0:c0 + CH].unsqueeze(0)
                eng.dma_start(out=outv, in_=inv).then_inc(sm.h, 16)
        self.em("scalar", f)
        self.flats_done[ci] = sm.inc(96)

    def _chunk(self, ci, tc, l2_applied, prev):
        D, sc = self.st_, self.sc
        pe, dve, act, pool, sm, ld = self.pe, self.dve, self.act, self.pool, self.sm, self.ld
        pA, pB, pT = self.pA, self.pB, self.pT
        c0 = ci * CH
        csplit, cbsplit, rhsT, persF = D["csplit"], D["cbsplit"], D["rhsT"], D["persF"]

        def qc(cb):
            return sc["qkvc"][:, cb * SLAB + tc * CH: cb * SLAB + tc * CH + CH]

        def zc(h):
            return sc["zs_s"][:, h * CH:(h + 1) * CH]

        def blkf(name, h):
            return sc[name][:, h * 128:(h + 1) * 128]

        zv = D["zsD"][:, c0:c0 + CH].rearrange("(b p) t -> p b t", p=128)

        def f(eng, zv=zv, dv=dve.n):
            eng.wait_ge(dve.h, dv)
            eng.dma_start(out=sc["zs_s"][:, :].rearrange("p (b t) -> p b t", t=CH),
                          in_=zv).then_inc(ld.h, 16)
        self.em("sync", f)
        zs_rdy = ld.inc(16)

        if ci == 0:
            self._emit_flats(0, prev["argmm"])
        flats = self.flats_done[ci]

        argev = None
        for roff, dst in ((0, "argA"), (2048, "argQ")):
            for j in range(4):
                def f(eng, j=j, roff=roff, w=flats, c0=c0, pf=self.pfree.get("pA%d" % j)):
                    eng.wait_ge(sm.h, w)
                    if pf is not None:
                        eng.wait_ge(pf[0], pf[1])
                    eng.matmul(pA[j][:, :], csplit[0:66, c0:c0 + CH],
                               rhsT[0:66, roff + j * 512:roff + (j + 1) * 512],
                               start=True, stop=True).then_inc(pe.h, 1)
                self.em("tensor", f)
                mm = pe.inc(1)

                def f(eng, j=j, dst=dst, w=mm):
                    eng.wait_ge(pe.h, w)
                    eng.tensor_scalar(sc[dst][:, j * 512:(j + 1) * 512], pA[j][:, :],
                                      0.0, None, ALU.min).then_inc(dve.h, 1)
                self.em("vector", f)
                argev = dve.inc(1)
                self.puse("pA%d" % j, dve.h, argev)

        def f(eng, w=argev):
            eng.wait_ge(dve.h, w)
            eng.activation(out=sc["argA"][:, :], in_=sc["argA"][:, :], func=AF.Exp)
            eng.activation(out=sc["argQ"][:, :], in_=sc["argQ"][:, :], func=AF.Exp
                           ).then_inc(act.h, 1)
        self.em("scalar", f)
        exp_done = act.inc(1)

        def f(eng, c0=c0, w=prev["stagemm"]):
            eng.wait_ge(pe.h, w)
            eng.tensor_copy(sc["stage"][0:16, :], persF[32:48, c0:c0 + CH])
            eng.tensor_copy(sc["scr16"][0:16, 0:1], persF[0:16, c0 + CH - 1:c0 + CH]
                            ).then_inc(dve.h, 1)
        self.em("vector", f)
        stg1 = dve.inc(1)

        def f(eng, c0=c0, w=stg1):
            eng.wait_ge(dve.h, w)
            eng.activation(out=sc["stage"][32:48, :], in_=persF[0:16, c0:c0 + CH], func=AF.Exp)
            eng.activation(out=sc["stage"][64:80, :], in_=persF[0:16, c0:c0 + CH],
                           func=AF.Exp, scale=-1.0, bias=sc["scr16"][0:16, 0:1]
                           ).then_inc(act.h, 1)
        self.em("scalar", f)
        stg2 = act.inc(1)

        def f(eng, c0=c0, w=stg2):
            eng.wait_ge(act.h, w)
            eng.tensor_mul(sc["stage"][32:48, :], sc["stage"][32:48, :],
                           persF[32:48, c0:c0 + CH]).then_inc(dve.h, 1)
        self.em("vector", f)
        stg3 = dve.inc(1)

        def f(eng, w=stg3, pf=self.pfree.get("pB1"), IDF=self.C_IDF):
            eng.wait_ge(dve.h, w)
            if pf is not None:
                eng.wait_ge(pf[0], pf[1])
            eng.transpose(pB[1][:, 0:128], sc["stage"][:, :], IDF).then_inc(pe.h, 1)
        self.em("tensor", f)
        trs = pe.inc(1)
        prev["stagemm"] = trs

        def f(eng, ci=ci, ORF=self.CF_OROWF):
            eng.matmul(pB[1][:, 128:144], ORF, D["gerow"][0:1, ci * 16:(ci + 1) * 16],
                       start=True, stop=True).then_inc(pe.h, 1)
        self.em("tensor", f)
        gemm = pe.inc(1)

        def f(eng, w=gemm):
            eng.wait_ge(pe.h, w)
            eng.tensor_copy(sc["colsb"][:, :], pB[1][:, 0:128])
            eng.tensor_scalar_mul(sc["colsb"][:, 96:112], sc["colsb"][:, 32:48], -1.0)
            eng.tensor_copy(sc["gendb"][:, :], pB[1][:, 128:144]).then_inc(dve.h, 1)
        self.em("vector", f)
        cols_done = dve.inc(1)
        self.puse("pB1", dve.h, cols_done)

        gexp = None
        for j in range(4):
            def f(eng, j=j, w=flats, pf=self.pfree.get("pA%d" % j), O2=D["ones2"]):
                eng.wait_ge(sm.h, w)
                if pf is not None:
                    eng.wait_ge(pf[0], pf[1])
                eng.matmul(pA[j][:, :], O2[0:2, :],
                           sc["cflat2"][0:2, j * 512:(j + 1) * 512],
                           start=True, stop=True).then_inc(pe.h, 1)
            self.em("tensor", f)
            mm = pe.inc(1)

            def f(eng, j=j, w=mm):
                eng.wait_ge(pe.h, w)
                eng.activation(out=sc["og2"][:, j * 512:(j + 1) * 512], in_=pA[j][:, :],
                               func=AF.Exp).then_inc(act.h, 1)
            self.em("scalar", f)
            gexp = act.inc(1)
            self.puse("pA%d" % j, act.h, gexp)
        prev["argmm"] = pe.n
        if ci + 1 < NCHUNK:
            self._emit_flats(ci + 1, prev["argmm"])

        def f(eng, w=gexp, w2=l2_applied):
            eng.wait_ge(act.h, w)
            eng.wait_ge(dve.h, w2)
            last = None
            for h in range(NH):
                last = eng.tensor_mul(blkf("gqb", h), qc(h // 2), blkf("og2", h))
            last.then_inc(pool.h, 1)
        self.em("gpsimd", f)
        gq_done_pool = pool.inc(1)

        def f(eng, w=dve.n):
            eng.wait_ge(dve.h, w)
            eng.tensor_copy(D["stbf"][:, :], D["state"][:, :]).then_inc(pool.h, 1)
        self.em("gpsimd", f)
        stbf_done = pool.inc(1)

        if ci == 0:
            self.dump("dbg_qkvc", sc["qkvc"][:, :], [128, 32 * SLAB], BF16)
            self.dump("dbg_argA", sc["argA"][:, :], [128, 2048], BF16)
            self.dump("dbg_argQ", sc["argQ"][:, :], [128, 2048], BF16)
            self.dump("dbg_cols", sc["colsb"][:, :], [128, 128], F32)
            self.dump("dbg_gend", sc["gendb"][:, :], [128, 16], F32)
            self.dump("dbg_gqb", sc["gqb"][:, :], [128, 2048], BF16)

        skk_ev = {}
        tev_map = {}
        skbanks = ["pB0", "pB1", "pA0", "pA1"]
        slot_free = [self.pfree.get(b) for b in skbanks]
        skk_last = [0, 0, 0, 0]
        for h in range(NH):
            slot = h % 4
            bank = [pB[0], pB[1], pA[0], pA[1]][slot]
            kb, qb = 8 + h // 2, h // 2

            def f(eng, bank=bank, kb=kb, qb=qb, w=l2_applied, pf=slot_free[slot]):
                eng.wait_ge(dve.h, w)
                if pf is not None:
                    eng.wait_ge(pf[0], pf[1])
                eng.matmul(bank[:, 0:128], qc(kb), qc(kb), start=True, stop=True)
                eng.matmul(bank[:, 128:256], qc(kb), qc(qb), start=True, stop=True
                           ).then_inc(pe.h, 1)
            self.em("tensor", f)
            skmm = pe.inc(1)

            def f(eng, h=h, bank=bank, w=skmm, we=exp_done,
                  MU=self.C_MUSN, MQ=self.C_MUI):
                eng.wait_ge(pe.h, w)
                eng.wait_ge(act.h, we)
                eng.tensor_mul(blkf("AT", h), bank[:, 0:128],
                               sc["argA"][:, h * 128:(h + 1) * 128])
                eng.tensor_mul(blkf("AT", h), blkf("AT", h), MU)
                eng.tensor_mul(blkf("AQ", h), bank[:, 128:256],
                               sc["argQ"][:, h * 128:(h + 1) * 128])
                eng.tensor_mul(blkf("AQ", h), blkf("AQ", h), MQ).then_inc(dve.h, 1)
            self.em("vector", f)
            amul = dve.inc(1)
            slot_free[slot] = (dve.h, amul)
            skk_last[slot] = amul
            skk_ev[h] = amul
        for i, b in enumerate(skbanks):
            self.puse(b, dve.h, skk_last[i])

        for h in range(NH):
            par = h % 2
            kb, vb = 8 + h // 2, 16 + h

            def f(eng, h=h, par=par, kb=kb, vb=vb, w=skk_ev[h],
                  pf=self.pfree.get("pT%d" % par), ID=self.C_ID):
                eng.wait_ge(dve.h, w)
                if pf is not None:
                    eng.wait_ge(pf[0], pf[1])
                eng.transpose(pT[par][:, 0:128], blkf("AT", h), ID)
                eng.transpose(pT[par][:, 128:256], qc(kb), ID)
                eng.transpose(pT[par][:, 256:384], qc(vb), ID).then_inc(pe.h, 1)
            self.em("tensor", f)
            trs = pe.inc(1)

            def f(eng, h=h, par=par, w=trs, wc=cols_done):
                eng.wait_ge(pe.h, w)
                eng.wait_ge(dve.h, wc)
                cb_ = sc["colsb"]
                eng.copy(blkf("AO", h), pT[par][:, 0:128])
                eng.mul(sc["Ysb"][:, h * 256 + 128:h * 256 + 256],
                        pT[par][:, 128:256], cb_[:, 96 + h:97 + h])
                eng.mul(blkf("KW", h), pT[par][:, 128:256], cb_[:, 64 + h:65 + h])
                eng.mul(sc["Ysb"][:, h * 256:h * 256 + 128],
                        pT[par][:, 256:384], cb_[:, h:h + 1]).then_inc(act.h, 1)
            self.em("scalar", f)
            tev = act.inc(1)
            tev_map[h] = (act.h, tev)
            self.puse("pT%d" % par, act.h, tev)

        psrc = ("AO", "AT")
        pwv = {h: tev_map[h] for h in range(NH)}
        yv = {h: tev_map[h] for h in range(NH)}
        for lvl in range(7):
            pdst = ("Pw2", "PwT2") if (lvl % 2 == 1) else ("AO", "AT")
            if lvl >= 1:
                bgroup_free = [self.pfree.get("pB0"), self.pfree.get("pB1")]
                for g in range(4):
                    hs = [4 * g + i for i in range(4)]

                    def f(eng, hs=tuple(hs), psrc=psrc, lvl=lvl,
                          wvs=tuple(pwv[h] for h in hs),
                          pfs=tuple(p for p in bgroup_free if p is not None)):
                        for wv in wvs:
                            eng.wait_ge(wv[0], wv[1])
                        for pf in pfs:
                            eng.wait_ge(pf[0], pf[1])
                        ins = None
                        for i, h in enumerate(hs):
                            ins = eng.matmul(pB[0][:, i * 128:(i + 1) * 128],
                                             blkf(psrc[0], h), blkf(psrc[1], h),
                                             start=True, stop=True)
                            if lvl < 6:
                                ins = eng.matmul(pB[1][:, i * 128:(i + 1) * 128],
                                                 blkf(psrc[1], h), blkf(psrc[0], h),
                                                 start=True, stop=True)
                        ins.then_inc(pe.h, 1)
                    self.em("tensor", f)
                    bmm = pe.inc(1)

                    def f(eng, g=g, pdst=pdst, lvl=lvl, w=bmm):
                        eng.wait_ge(pe.h, w)
                        ins = eng.copy(sc[pdst[1]][:, g * 512:(g + 1) * 512],
                                       pB[0][:, 0:512])
                        if lvl < 6:
                            ins = eng.copy(sc[pdst[0]][:, g * 512:(g + 1) * 512],
                                           pB[1][:, 0:512])
                        ins.then_inc(act.h, 1)
                    self.em("scalar", f)
                    bev = act.inc(1)
                    for h in hs:
                        pwv[h] = (act.h, bev)
                    bgroup_free = [(act.h, bev)]
                self.puse("pB0", act.h, act.n)
                self.puse("pB1", act.h, act.n)
                pw_t = pdst[1]
            else:
                pw_t = "AT"

            aslot_free = [self.pfree.get("pA0"), self.pfree.get("pA1")]
            for p in range(NH // 2):
                h0, h1 = 2 * p, 2 * p + 1
                slot = p % 2
                bank = pA[slot]

                def f(eng, h0=h0, h1=h1, bank=bank, pw_t=pw_t,
                      wp0=pwv[h0], wp1=pwv[h1], wy0=yv[h0], wy1=yv[h1],
                      pf=aslot_free[slot]):
                    eng.wait_ge(wp0[0], wp0[1])
                    eng.wait_ge(wp1[0], wp1[1])
                    eng.wait_ge(wy0[0], wy0[1])
                    eng.wait_ge(wy1[0], wy1[1])
                    if pf is not None:
                        eng.wait_ge(pf[0], pf[1])
                    eng.matmul(bank[:, 0:256], blkf(pw_t, h0),
                               sc["Ysb"][:, h0 * 256:(h0 + 1) * 256],
                               start=True, stop=True)
                    eng.matmul(bank[:, 256:512], blkf(pw_t, h1),
                               sc["Ysb"][:, h1 * 256:(h1 + 1) * 256],
                               start=True, stop=True).then_inc(pe.h, 1)
                self.em("tensor", f)
                amm = pe.inc(1)

                def f(eng, h0=h0, bank=bank, w=amm):
                    eng.wait_ge(pe.h, w)
                    ysl = sc["Ysb"][:, h0 * 256:h0 * 256 + 512]
                    eng.tensor_add(ysl, ysl, bank[:, 0:512]).then_inc(dve.h, 1)
                self.em("vector", f)
                yev = (dve.h, dve.inc(1))
                yv[h0] = yev
                yv[h1] = yev
                aslot_free[slot] = yev
            self.puse("pA0", dve.h, dve.n)
            self.puse("pA1", dve.h, dve.n)
            if lvl >= 1:
                psrc = pdst
        yready = (dve.n, act.n)
        if ci == 0:
            self.dump("dbg_Y", sc["Ysb"][:, :], [128, 4096], BF16)

        for p in range(NH // 2):
            h0, h1 = 2 * p, 2 * p + 1
            par = p % 2

            def f(eng, h0=h0, h1=h1, par=par, wv=yready,
                  pf=self.pfree.get("pT%d" % par), ID=self.C_ID):
                eng.wait_ge(dve.h, wv[0])
                eng.wait_ge(act.h, wv[1])
                if pf is not None:
                    eng.wait_ge(pf[0], pf[1])
                eng.transpose(pT[par][:, 384:512],
                              sc["Ysb"][:, h0 * 256 + 128:h0 * 256 + 256], ID)
                eng.transpose(pT[par][:, 512:640],
                              sc["Ysb"][:, h1 * 256 + 128:h1 * 256 + 256], ID
                              ).then_inc(pe.h, 1)
            self.em("tensor", f)
            w2mm = pe.inc(1)

            def f(eng, par=par, w=w2mm):
                eng.wait_ge(pe.h, w)
                eng.tensor_copy(sc["W2T"][:, par * 256:(par + 1) * 256],
                                pT[par][:, 384:640]).then_inc(dve.h, 1)
            self.em("vector", f)
            w2ev = dve.inc(1)
            self.puse("pT%d" % par, dve.h, w2ev)

            def f(eng, h0=h0, h1=h1, par=par, w=w2ev, ws=stbf_done,
                  pf=self.pfree.get("pA2"), ID=self.C_ID):
                eng.wait_ge(dve.h, w)
                eng.wait_ge(pool.h, ws)
                if pf is not None:
                    eng.wait_ge(pf[0], pf[1])
                for i, h in ((0, h0), (1, h1)):
                    c0_ = par * 256 + i * 128
                    eng.matmul(pA[2][:, c0_:c0_ + 128], ID,
                               sc["Ysb"][:, h * 256:h * 256 + 128], start=True, stop=False)
                    ins = eng.matmul(pA[2][:, c0_:c0_ + 128],
                                     sc["W2T"][:, par * 256 + i * 128:par * 256 + (i + 1) * 128],
                                     D["stbf"][:, h * 128:(h + 1) * 128], start=False,
                                     stop=True)
                ins.then_inc(pe.h, 1)
            self.em("tensor", f)
            umm = pe.inc(1)

            def f(eng, par=par, w=umm):
                eng.wait_ge(pe.h, w)
                eng.tensor_copy(sc["Usb"][:, par * 256:(par + 1) * 256],
                                pA[2][:, par * 256:(par + 1) * 256]).then_inc(dve.h, 1)
            self.em("vector", f)
            uev = dve.inc(1)
            self.puse("pA2", dve.h, uev)

            def f(eng, h0=h0, h1=h1, par=par, w=uev, wg=gq_done_pool,
                  pf=self.pfree.get("pA3")):
                eng.wait_ge(dve.h, w)
                eng.wait_ge(pool.h, wg)
                if pf is not None:
                    eng.wait_ge(pf[0], pf[1])
                for i, h in ((0, h0), (1, h1)):
                    c0_ = par * 256 + i * 128
                    eng.matmul(pA[3][:, c0_:c0_ + 128],
                               D["stbf"][:, h * 128:(h + 1) * 128], blkf("gqb", h),
                               start=True, stop=False)
                    ins = eng.matmul(pA[3][:, c0_:c0_ + 128],
                                     sc["Usb"][:, par * 256 + i * 128:par * 256 + (i + 1) * 128],
                                     blkf("AQ", h), start=False, stop=True)
                ins.then_inc(pe.h, 1)
            self.em("tensor", f)
            omm = pe.inc(1)

            def f(eng, h0=h0, par=par, w=omm, wz=zs_rdy):
                eng.wait_ge(pe.h, w)
                eng.wait_ge(ld.h, wz)
                eng.tensor_mul(sc["ogsb"][:, h0 * 128:h0 * 128 + 256],
                               pA[3][:, par * 256:(par + 1) * 256],
                               sc["zs_s"][:, h0 * CH:h0 * CH + 2 * CH]).then_inc(dve.h, 1)
            self.em("vector", f)
            ogev = dve.inc(1)
            self.puse("pA3", dve.h, ogev)

            def f(eng, h0=h0, h1=h1, par=par, w=uev, pf=self.pfree.get("pB%d" % par)):
                eng.wait_ge(dve.h, w)
                if pf is not None:
                    eng.wait_ge(pf[0], pf[1])
                eng.matmul(pB[par][:, 0:128], blkf("KW", h0),
                           sc["Usb"][:, par * 256:par * 256 + 128],
                           start=True, stop=True)
                eng.matmul(pB[par][:, 128:256], blkf("KW", h1),
                           sc["Usb"][:, par * 256 + 128:par * 256 + 256],
                           start=True, stop=True).then_inc(pe.h, 1)
            self.em("tensor", f)
            smm = pe.inc(1)

            def f(eng, h0=h0, h1=h1, par=par, w=smm):
                eng.wait_ge(pe.h, w)
                eng.scalar_tensor_tensor(D["state"][:, h0 * 128:(h0 + 1) * 128],
                                         D["state"][:, h0 * 128:(h0 + 1) * 128],
                                         sc["gendb"][:, h0:h0 + 1], pB[par][:, 0:128],
                                         ALU.mult, ALU.add)
                eng.scalar_tensor_tensor(D["state"][:, h1 * 128:(h1 + 1) * 128],
                                         D["state"][:, h1 * 128:(h1 + 1) * 128],
                                         sc["gendb"][:, h1:h1 + 1], pB[par][:, 128:256],
                                         ALU.mult, ALU.add).then_inc(dve.h, 1)
            self.em("vector", f)
            sev = dve.inc(1)
            self.puse("pB%d" % par, dve.h, sev)

        if ci == 0:
            self.dump("dbg_og", sc["ogsb"][:, :], [128, 2048], BF16)
            self.dump("dbg_state", D["state"][:, :], [128, 2048], F32)

        def f(eng, w=dve.n):
            eng.wait_ge(dve.h, w)
            eng.activation(out=sc["og2"][:, :], in_=sc["ogsb"][:, :], func=AF.Square
                           ).then_inc(act.h, 1)
        self.em("scalar", f)
        og2_done = act.inc(1)

        for bank in range(4):
            def f(eng, bank=bank, w=og2_done, pf=self.pfree.get("pA%d" % bank),
                  OC=self.C_OCOL):
                eng.wait_ge(act.h, w)
                if pf is not None:
                    eng.wait_ge(pf[0], pf[1])
                ins = None
                for hh in range(4):
                    h = bank * 4 + hh
                    ins = eng.matmul(pA[bank][0:1, hh * 128:(hh + 1) * 128], OC,
                                     blkf("og2", h), start=(hh == 0), stop=(hh == 3))
                ins.then_inc(pe.h, 1)
            self.em("tensor", f)
            summ = pe.inc(1)

            def f(eng, bank=bank, w=summ):
                eng.wait_ge(pe.h, w)
                eng.activation(out=sc["rowb"][0:1, bank * 512:(bank + 1) * 512],
                               in_=pA[bank][0:1, :], func=AF.Ln, bias=self.CF_RMSE[0:1, :])
                eng.activation(out=sc["rowb"][0:1, bank * 512:(bank + 1) * 512],
                               in_=sc["rowb"][0:1, bank * 512:(bank + 1) * 512],
                               func=AF.Exp, scale=-0.5, bias=self.CF_LN128H[0:1, :]
                               ).then_inc(act.h, 1)
            self.em("scalar", f)
            self.puse("pA%d" % bank, act.h, act.inc(1))
        rstdd = act.n

        for j in range(4):
            par = j % 2

            def f(eng, j=j, w=rstdd, pf=self.pfree.get("pA%d" % j), OR=self.C_OROW):
                eng.wait_ge(act.h, w)
                if pf is not None:
                    eng.wait_ge(pf[0], pf[1])
                eng.matmul(pA[j][:, 0:512], OR, sc["rowb"][0:1, j * 512:(j + 1) * 512],
                           start=True, stop=True).then_inc(pe.h, 1)
            self.em("tensor", f)
            bcmm = pe.inc(1)

            def f(eng, j=j, par=par, w=bcmm, NW=self.CF_NRMW, ost=prev["onst"][par]):
                eng.wait_ge(pe.h, w)
                if ost is not None:
                    eng.wait_ge(ost[0], ost[1])
                eng.scalar_tensor_tensor(sc["onstg"][:, par * 512:(par + 1) * 512],
                                         sc["ogsb"][:, j * 512:(j + 1) * 512], NW,
                                         pA[j][:, 0:512], ALU.mult, ALU.mult
                                         ).then_inc(dve.h, 1)
            self.em("vector", f)
            onev = dve.inc(1)
            self.puse("pA%d" % j, dve.h, onev)

            dst = D["onD"][j * 512:(j + 1) * 512, c0:c0 + CH].rearrange(
                "(hh p) t -> p hh t", p=128)

            def f(eng, par=par, w=onev, dst=dst):
                eng.wait_ge(dve.h, w)
                eng.dma_start(out=dst,
                              in_=sc["onstg"][:, par * 512:(par + 1) * 512
                                              ].rearrange("p (hh t) -> p hh t", t=CH)
                              ).then_inc(self.stm.h, 16)
            self.em("sync", f)
            prev["onst"][par] = (self.stm.h, self.stm.inc(16))

    def _wout_pass(self):
        nc, D = self.nc, self.st_
        pe, dve, act, cast, ld, stm = self.pe, self.dve, self.act, self.cast, self.ld, self.stm
        pA = self.pA
        with (
            nc.sbuf_tensor("wo_sb", [128, 16 * H], BF16) as wo_sb,
            nc.sbuf_tensor("on_t", [128, 2 * 16 * 512], BF16) as on_t,
            nc.sbuf_tensor("po_st", [128, 4 * 512], BF16) as po_st,
        ):
            wv = D["wout"][:, :].rearrange("(k p) hcol -> p k hcol", p=128)

            def f(eng, wv=wv):
                eng.dma_start(out=wo_sb[:, :].rearrange("p (k c) -> p k c", c=H),
                              in_=wv).then_inc(cast.h, 16)
            self.em("gpsimd", f)
            wo_done = cast.inc(16)

            prev_use = [0, 0]
            st_free = [None] * 4
            rr = [0]
            for tt in range(S // 512):
                tpar = tt % 2
                onv = D["onD"][:, tt * 512:(tt + 1) * 512].rearrange("(k p) t -> p k t", p=128)

                def f(eng, onv=onv, tpar=tpar, w=prev_use[tpar]):
                    eng.wait_ge(pe.h, w)
                    eng.dma_start(out=on_t[:, tpar * (16 * 512):(tpar + 1) * (16 * 512)
                                          ].rearrange("p (k t) -> p k t", t=512),
                                  in_=onv).then_inc(ld.h, 16)
                self.em("sync", f)
                on_done = ld.inc(16)

                half = tt // 8
                colbase = (tt % 8) * 512
                for hb in range(16):
                    bank = rr[0] % 4
                    sgi = rr[0] % 4
                    rr[0] += 1

                    def f(eng, hb=hb, tpar=tpar, bank=bank, w=on_done, ww=wo_done,
                          pf=self.pfree.get("pA%d" % bank)):
                        eng.wait_ge(ld.h, w)
                        eng.wait_ge(cast.h, ww)
                        if pf is not None:
                            eng.wait_ge(pf[0], pf[1])
                        ins = None
                        for k in range(16):
                            ins = eng.matmul(
                                pA[bank][:, :],
                                wo_sb[:, k * H + hb * 128:k * H + (hb + 1) * 128],
                                on_t[:, tpar * (16 * 512) + k * 512:
                                     tpar * (16 * 512) + (k + 1) * 512],
                                start=(k == 0), stop=(k == 15))
                        ins.then_inc(pe.h, 1)
                    self.em("tensor", f)
                    mm = pe.inc(1)
                    prev_use[tpar] = mm

                    def f(eng, bank=bank, sgi=sgi, w=mm, sf=st_free[sgi]):
                        eng.wait_ge(pe.h, w)
                        if sf is not None:
                            eng.wait_ge(sf[0], sf[1])
                        eng.copy(po_st[:, sgi * 512:(sgi + 1) * 512], pA[bank][:, :]
                                 ).then_inc(act.h, 1)
                    self.em("scalar", f)
                    ev = act.inc(1)
                    self.puse("pA%d" % bank, act.h, ev)

                    dst = D["rsIn"][half * H + hb * 128:half * H + (hb + 1) * 128,
                                    colbase:colbase + 512]

                    def f(eng, dst=dst, sgi=sgi, ev=ev):
                        eng.wait_ge(act.h, ev)
                        eng.dma_start(out=dst, in_=po_st[:, sgi * 512:(sgi + 1) * 512]
                                      ).then_inc(stm.h, 16)
                    self.em("sync", f)
                    st_free[sgi] = (stm.h, stm.inc(16))

    def _phase_D(self):
        nc, D = self.nc, self.st_
        pe, dve, act, pool, ld, cast, stm, sm, ccm = (
            self.pe, self.dve, self.act, self.pool, self.ld, self.cast, self.stm,
            self.sm, self.ccm)
        pA, pB, pT = self.pA, self.pB, self.pT

        def f(eng, w=stm.n):
            eng.wait_ge(stm.h, w)
            eng.collective_compute(
                "ReduceScatter", ALU.add, replica_groups=REPL,
                ins=[D["rsIn"][0:2 * H, :]], outs=[D["rsOut"][0:H, :]],
            ).then_inc(ccm.h)
        self.em("gpsimd", f)
        rs_done = ccm.inc(1)

        with contextlib.ExitStack() as esd:
            x1n = esd.enter_context(nc.sbuf_tensor("x1n", [128, 16 * TH], BF16))
            rs_t = esd.enter_context(nc.sbuf_tensor("rs_t", [128, 16 * 512], BF16))
            xf_t = esd.enter_context(nc.sbuf_tensor("xf_t", [128, 16 * 512], BF16))
            x1b = esd.enter_context(nc.sbuf_tensor("x1b", [128, 16 * 512], BF16))
            sq2 = rs_t
            rowst = esd.enter_context(nc.sbuf_tensor("rowst", [1, 4 * 512], F32))

            for tt in range(TH // 512):
                rsv = D["rsOut"][:, tt * 512:(tt + 1) * 512].rearrange("(k p) t -> p k t", p=128)
                xv = D["xTd"][:, tt * 512:(tt + 1) * 512].rearrange("(k p) t -> p k t", p=128)

                def f(eng, rsv=rsv, w=rs_done, wp=pool.n, wq=pe.n):
                    eng.wait_ge(ccm.h, w)
                    eng.wait_ge(pool.h, wp)
                    eng.wait_ge(pe.h, wq)
                    eng.dma_start(out=rs_t[:, :].rearrange("p (k t) -> p k t", t=512),
                                  in_=rsv).then_inc(ld.h, 16)
                self.em("sync", f)

                def f(eng, xv=xv, wp=pool.n):
                    eng.wait_ge(pool.h, wp)
                    eng.dma_start(out=xf_t[:, :].rearrange("p (k t) -> p k t", t=512),
                                  in_=xv).then_inc(cast.h, 16)
                self.em("gpsimd", f)
                cast.inc(16)
                tiles_rdy = ld.inc(16)
                xt_rdy = cast.n

                def f(eng, w=tiles_rdy, wd=dve.n):
                    eng.wait_ge(ld.h, w)
                    eng.wait_ge(dve.h, wd)
                    last = None
                    for k in range(16):
                        last = eng.tensor_add(x1b[:, k * 512:(k + 1) * 512],
                                              xf_t[:, k * 512:(k + 1) * 512],
                                              rs_t[:, k * 512:(k + 1) * 512])
                    last.then_inc(pool.h, 1)
                self.em("gpsimd", f)
                x1b_done = pool.inc(1)

                def f(eng, w=x1b_done, pf0=self.pfree.get("pB0"), OC=self.C_OCOL):
                    eng.wait_ge(pool.h, w)
                    if pf0 is not None:
                        eng.wait_ge(pf0[0], pf0[1])
                    ins = None
                    for k in range(16):
                        ins = eng.matmul(pB[0][0:1, 0:512], OC, x1b[:, k * 512:(k + 1) * 512],
                                         start=(k == 0), stop=(k == 15))
                    ins.then_inc(pe.h, 1)
                self.em("tensor", f)
                summ = pe.inc(1)

                def f(eng, w=x1b_done, wp=pe.n):
                    eng.wait_ge(pool.h, w)
                    eng.wait_ge(pe.h, wp)
                    last = None
                    for k in range(16):
                        last = eng.activation(out=sq2[:, k * 512:(k + 1) * 512],
                                              in_=x1b[:, k * 512:(k + 1) * 512],
                                              func=AF.Square)
                    last.then_inc(act.h, 1)
                self.em("scalar", f)
                sqb = act.inc(1)

                def f(eng, w=sqb, pf1=self.pfree.get("pB1"), OC=self.C_OCOL):
                    eng.wait_ge(act.h, w)
                    if pf1 is not None:
                        eng.wait_ge(pf1[0], pf1[1])
                    ins = None
                    for k in range(16):
                        ins = eng.matmul(pB[1][0:1, 0:512], OC, sq2[:, k * 512:(k + 1) * 512],
                                         start=(k == 0), stop=(k == 15))
                    ins.then_inc(pe.h, 1)
                self.em("tensor", f)
                sq_mm = pe.inc(1)

                def f(eng, a=summ, b=sq_mm):
                    eng.wait_ge(pe.h, max(a, b))
                    eng.activation(out=rowst[0:1, 0:512], in_=pB[0][0:1, 0:512],
                                   func=AF.Copy, scale=self.CF_I2048[0:1, :])
                    eng.activation(out=rowst[0:1, 512:1024], in_=rowst[0:1, 0:512],
                                   func=AF.Square).then_inc(act.h, 1)
                self.em("scalar", f)
                mrow = act.inc(1)

                def f(eng, w=mrow):
                    eng.wait_ge(act.h, w)
                    eng.scalar_tensor_tensor(rowst[0:1, 1024:1536], pB[1][0:1, 0:512],
                                             1.0 / 2048.0, rowst[0:1, 512:1024],
                                             ALU.mult, ALU.subtract).then_inc(dve.h, 1)
                self.em("vector", f)
                varr = dve.inc(1)
                self.puse("pB1", dve.h, varr)

                def f(eng, w=varr):
                    eng.wait_ge(dve.h, w)
                    eng.activation(out=rowst[0:1, 1536:2048], in_=rowst[0:1, 1024:1536],
                                   func=AF.Ln, bias=self.CF_EPS5[0:1, :])
                    eng.activation(out=rowst[0:1, 1536:2048], in_=rowst[0:1, 1536:2048],
                                   func=AF.Exp, scale=-0.5).then_inc(act.h, 1)
                self.em("scalar", f)
                rrow = act.inc(1)

                def f(eng, w=rrow, ORF=self.CF_OROWF):
                    eng.wait_ge(act.h, w)
                    eng.matmul(pB[0][:, 0:512], ORF, rowst[0:1, 0:512], start=True, stop=True)
                    eng.matmul(pB[1][:, 0:512], ORF, rowst[0:1, 1536:2048], start=True,
                               stop=True).then_inc(pe.h, 1)
                self.em("tensor", f)
                bcs = pe.inc(1)

                def f(eng, tt=tt, w=bcs):
                    eng.wait_ge(pe.h, w)
                    last = None
                    for k in range(16):
                        dst = x1n[:, k * TH + tt * 512:k * TH + (tt + 1) * 512]
                        last = eng.tensor_sub(dst, x1b[:, k * 512:(k + 1) * 512],
                                              pB[0][:, 0:512])
                        last = eng.tensor_mul(dst, dst, pB[1][:, 0:512])
                    last.then_inc(dve.h, 1)
                self.em("vector", f)
                x1n_done = dve.inc(1)
                self.puse("pB0", dve.h, x1n_done)
                self.puse("pB1", dve.h, x1n_done)

                xnv = D["x1nD"][:, tt * 512:(tt + 1) * 512].rearrange("(k p) t -> p k t", p=128)

                def f(eng, tt=tt, xnv=xnv, w=x1n_done):
                    eng.wait_ge(dve.h, w)
                    eng.dma_start(out=xnv,
                                  in_=x1n[:, :].rearrange("p (k t2) -> p k t2", t2=TH)
                                  [:, :, tt * 512:(tt + 1) * 512]).then_inc(stm.h, 16)
                self.em("sync", f)
                stm.inc(16)

            with (
                nc.sbuf_tensor("guw", [128, 2 * 16 * 256], BF16) as guw,
                nc.sbuf_tensor("sgst", [128, 512], BF16) as sgst,
                nc.sbuf_tensor("hstg", [128, 2 * 512], BF16) as hstg,
            ):
                guw_use = [0, 0]
                sg_use = [None]
                hst_use = [None, None]
                for pb in range(INTER // 128):
                    par = pb % 2
                    wv = D["wgu"][:, pb * 256:(pb + 1) * 256].rearrange(
                        "(k p) c -> p k c", p=128)

                    def f(eng, wv=wv, par=par, w=guw_use[par]):
                        eng.wait_ge(pe.h, w)
                        eng.dma_start(out=guw[:, par * (16 * 256):(par + 1) * (16 * 256)
                                              ].rearrange("p (k c) -> p k c", c=256),
                                      in_=wv).then_inc(cast.h, 16)
                    self.em("gpsimd", f)
                    gw_done = cast.inc(16)

                    for tt in range(TH // 512):
                        def f(eng, par=par, tt=tt, w=gw_done,
                              pf0=self.pfree.get("pA0"), pf1=self.pfree.get("pA1")):
                            eng.wait_ge(cast.h, w)
                            if pf0 is not None:
                                eng.wait_ge(pf0[0], pf0[1])
                            if pf1 is not None:
                                eng.wait_ge(pf1[0], pf1[1])
                            base = par * (16 * 256)
                            ins = None
                            for k in range(16):
                                eng.matmul(pA[0][:, :],
                                           guw[:, base + k * 256:base + k * 256 + 128],
                                           x1n[:, k * TH + tt * 512:k * TH + (tt + 1) * 512],
                                           start=(k == 0), stop=(k == 15))
                            for k in range(16):
                                ins = eng.matmul(
                                    pA[1][:, :],
                                    guw[:, base + k * 256 + 128:base + k * 256 + 256],
                                    x1n[:, k * TH + tt * 512:k * TH + (tt + 1) * 512],
                                    start=(k == 0), stop=(k == 15))
                            ins.then_inc(pe.h, 1)
                        self.em("tensor", f)
                        gmm = pe.inc(1)
                        guw_use[par] = gmm

                        def f(eng, w=gmm, sg=sg_use[0]):
                            eng.wait_ge(pe.h, w)
                            if sg is not None:
                                eng.wait_ge(sg[0], sg[1])
                            eng.activation(out=sgst[:, :], in_=pA[0][:, :], func=AF.Silu
                                           ).then_inc(act.h, 1)
                        self.em("scalar", f)
                        sil = act.inc(1)
                        self.puse("pA0", act.h, sil)

                        par2 = tt % 2

                        def f(eng, w=sil, par2=par2, hu=hst_use[par2]):
                            eng.wait_ge(act.h, w)
                            if hu is not None:
                                eng.wait_ge(hu[0], hu[1])
                            eng.tensor_mul(hstg[:, par2 * 512:par2 * 512 + 512],
                                           sgst[:, :], pA[1][:, :]).then_inc(dve.h, 1)
                        self.em("vector", f)
                        hmul = dve.inc(1)
                        self.puse("pA1", dve.h, hmul)
                        sg_use[0] = (dve.h, hmul)

                        def f(eng, pb=pb, tt=tt, par2=par2, w=hmul):
                            eng.wait_ge(dve.h, w)
                            eng.dma_start(out=D["hD"][pb * 128:(pb + 1) * 128,
                                                      tt * 512:(tt + 1) * 512],
                                          in_=hstg[:, par2 * 512:par2 * 512 + 512]
                                          ).then_inc(stm.h, 16)
                        self.em("sync", f)
                        hst_use[par2] = (stm.h, stm.inc(16))

        self.barrier()

        with (
            nc.sbuf_tensor("dw", [128, 64 * 512], BF16) as dw,
            nc.sbuf_tensor("hp", [128, 2 * 8 * 512], BF16) as hp,
            nc.sbuf_tensor("dnst", [128, 4 * 512], F32) as dnst,
        ):
            dn_st = [None] * 4
            hp_mm = [0, 0]
            for qd in range(4):
                wv = D["wdn"][:, qd * 512:(qd + 1) * 512].rearrange("(k p) c -> p k c", p=128)

                def f(eng, wv=wv, w=pe.n):
                    eng.wait_ge(pe.h, w)
                    eng.dma_start(out=dw[:, :].rearrange("p (k c) -> p k c", c=512),
                                  in_=wv).then_inc(cast.h, 16)
                self.em("gpsimd", f)
                dw_done = cast.inc(16)

                for tt in range(TH // 512):
                    hp_done = []
                    for pc in range(8):
                        par = pc % 2
                        hv = D["hD"][pc * 1024:(pc + 1) * 1024, tt * 512:(tt + 1) * 512
                                     ].rearrange("(k p) t -> p k t", p=128)
                        slot_mm = hp_mm[par] if True else 0

                        def f(eng, hv=hv, par=par, w=hp_mm[par]):
                            eng.wait_ge(pe.h, w)
                            eng.dma_start(out=hp[:, par * (8 * 512):(par + 1) * (8 * 512)
                                                  ].rearrange("p (k t) -> p k t", t=512),
                                          in_=hv).then_inc(ld.h, 16)
                        self.em("sync", f)
                        hp_done.append(ld.inc(16))

                        def f(eng, par=par, pc=pc, qd=qd, w=hp_done[-1], ww=dw_done,
                              pfs=tuple(self.pfree.get("pA%d" % b) for b in range(4))):
                            eng.wait_ge(ld.h, w)
                            eng.wait_ge(cast.h, ww)
                            if pc == 0:
                                for pf in pfs:
                                    if pf is not None:
                                        eng.wait_ge(pf[0], pf[1])
                            base = par * (8 * 512)
                            ins = None
                            for hb in range(4):
                                for kk in range(8):
                                    k = pc * 8 + kk
                                    ins = eng.matmul(
                                        pA[hb][:, :],
                                        dw[:, k * 512 + hb * 128:k * 512 + (hb + 1) * 128],
                                        hp[:, base + kk * 512:base + (kk + 1) * 512],
                                        start=(k == 0), stop=(k == 63))
                            ins.then_inc(pe.h, 1)
                        self.em("tensor", f)
                        hp_mm[par] = pe.inc(1)
                    dn_mm = pe.n

                    for hb in range(4):
                        def f(eng, hb=hb, w=dn_mm, ds=dn_st[hb]):
                            eng.wait_ge(pe.h, w)
                            if ds is not None:
                                eng.wait_ge(ds[0], ds[1])
                            eng.copy(dnst[:, hb * 512:(hb + 1) * 512], pA[hb][:, :]
                                     ).then_inc(act.h, 1)
                        self.em("scalar", f)
                        ev = act.inc(1)
                        self.puse("pA%d" % hb, act.h, ev)

                        def f(eng, hb=hb, qd=qd, tt=tt, w=ev):
                            eng.wait_ge(act.h, w)
                            eng.dma_start(out=D["dnD"][qd * 512 + hb * 128:qd * 512 + (hb + 1) * 128,
                                                       tt * 512:(tt + 1) * 512],
                                          in_=dnst[:, hb * 512:(hb + 1) * 512]).then_inc(stm.h, 16)
                        self.em("sync", f)
                        dn_st[hb] = (stm.h, stm.inc(16))

        self.barrier()

        with (
            nc.sbuf_tensor("dn_t", [128, 16 * 512], F32) as dn_t,
            nc.sbuf_tensor("xn_t", [128, 16 * 512], BF16) as xn_t,
            nc.sbuf_tensor("x2f", [128, 16 * 512], F32) as x2f,
            nc.sbuf_tensor("sqf", [128, 16 * 512], F32) as sqf,
            nc.sbuf_tensor("rows2", [1, 4 * 512], F32) as rows2,
            nc.sbuf_tensor("ostg", [128, 16 * 512], F32) as ostg,
        ):
            for tt in range(TH // 512):
                dnv = D["dnD"][:, tt * 512:(tt + 1) * 512].rearrange("(k p) t -> p k t", p=128)
                xnv = D["x1nD"][:, tt * 512:(tt + 1) * 512].rearrange("(k p) t -> p k t", p=128)

                def f(eng, dnv=dnv, xnv=xnv, wp=pool.n, wpe=pe.n):
                    eng.wait_ge(pool.h, wp)
                    eng.dma_start(out=dn_t[:, :].rearrange("p (k t) -> p k t", t=512),
                                  in_=dnv).then_inc(ld.h, 16)
                    eng.dma_start(out=xn_t[:, :].rearrange("p (k t) -> p k t", t=512),
                                  in_=xnv).then_inc(ld.h, 16)
                self.em("sync", f)
                t_rdy = ld.inc(32)

                def f(eng, w=t_rdy, wd=dve.n):
                    eng.wait_ge(ld.h, w)
                    eng.wait_ge(dve.h, wd)
                    last = None
                    for k in range(16):
                        last = eng.tensor_add(x2f[:, k * 512:(k + 1) * 512],
                                              dn_t[:, k * 512:(k + 1) * 512],
                                              xn_t[:, k * 512:(k + 1) * 512])
                    last.then_inc(pool.h, 1)
                self.em("gpsimd", f)
                x2_done = pool.inc(1)

                def f(eng, w=x2_done, pf0=self.pfree.get("pB0"), OCF=self.C_OCOLF):
                    eng.wait_ge(pool.h, w)
                    if pf0 is not None:
                        eng.wait_ge(pf0[0], pf0[1])
                    ins = None
                    for k in range(16):
                        ins = eng.matmul(pB[0][0:1, 0:512], OCF, x2f[:, k * 512:(k + 1) * 512],
                                         start=(k == 0), stop=(k == 15))
                    ins.then_inc(pe.h, 1)
                self.em("tensor", f)
                summ = pe.inc(1)

                def f(eng, w=x2_done, wp=pe.n):
                    eng.wait_ge(pool.h, w)
                    eng.wait_ge(pe.h, wp)
                    last = None
                    for k in range(16):
                        last = eng.activation(out=sqf[:, k * 512:(k + 1) * 512],
                                              in_=x2f[:, k * 512:(k + 1) * 512],
                                              func=AF.Square)
                    last.then_inc(act.h, 1)
                self.em("scalar", f)
                sqb = act.inc(1)

                def f(eng, w=sqb, pf1=self.pfree.get("pB1"), OCF=self.C_OCOLF):
                    eng.wait_ge(act.h, w)
                    if pf1 is not None:
                        eng.wait_ge(pf1[0], pf1[1])
                    ins = None
                    for k in range(16):
                        ins = eng.matmul(pB[1][0:1, 0:512], OCF, sqf[:, k * 512:(k + 1) * 512],
                                         start=(k == 0), stop=(k == 15))
                    ins.then_inc(pe.h, 1)
                self.em("tensor", f)
                sq_mm = pe.inc(1)

                def f(eng, a=summ, b=sq_mm):
                    eng.wait_ge(pe.h, max(a, b))
                    eng.activation(out=rows2[0:1, 0:512], in_=pB[0][0:1, 0:512],
                                   func=AF.Copy, scale=self.CF_I2048[0:1, :])
                    eng.activation(out=rows2[0:1, 512:1024], in_=rows2[0:1, 0:512],
                                   func=AF.Square).then_inc(act.h, 1)
                self.em("scalar", f)
                mrow = act.inc(1)

                def f(eng, w=mrow):
                    eng.wait_ge(act.h, w)
                    eng.scalar_tensor_tensor(rows2[0:1, 1024:1536], pB[1][0:1, 0:512],
                                             1.0 / 2048.0, rows2[0:1, 512:1024],
                                             ALU.mult, ALU.subtract).then_inc(dve.h, 1)
                self.em("vector", f)
                varr = dve.inc(1)
                self.puse("pB1", dve.h, varr)

                def f(eng, w=varr):
                    eng.wait_ge(dve.h, w)
                    eng.activation(out=rows2[0:1, 1536:2048], in_=rows2[0:1, 1024:1536],
                                   func=AF.Ln, bias=self.CF_EPS5[0:1, :])
                    eng.activation(out=rows2[0:1, 1536:2048], in_=rows2[0:1, 1536:2048],
                                   func=AF.Exp, scale=-0.5).then_inc(act.h, 1)
                self.em("scalar", f)
                rrow = act.inc(1)

                def f(eng, w=rrow, ORF=self.CF_OROWF):
                    eng.wait_ge(act.h, w)
                    eng.matmul(pB[0][:, 0:512], ORF, rows2[0:1, 0:512], start=True, stop=True)
                    eng.matmul(pB[1][:, 0:512], ORF, rows2[0:1, 1536:2048], start=True,
                               stop=True).then_inc(pe.h, 1)
                self.em("tensor", f)
                bcs = pe.inc(1)

                def f(eng, w=bcs, ws=stm.n):
                    eng.wait_ge(pe.h, w)
                    eng.wait_ge(stm.h, ws)
                    last = None
                    for k in range(16):
                        dst = ostg[:, k * 512:(k + 1) * 512]
                        last = eng.tensor_sub(dst, x2f[:, k * 512:(k + 1) * 512], pB[0][:, 0:512])
                        last = eng.tensor_mul(dst, dst, pB[1][:, 0:512])
                    last.then_inc(dve.h, 1)
                self.em("vector", f)
                odone = dve.inc(1)
                self.puse("pB0", dve.h, odone)
                self.puse("pB1", dve.h, odone)

                ov = D["outD"][:, tt * 512:(tt + 1) * 512].rearrange("(k p) t -> p k t", p=128)

                def f(eng, ov=ov, w=odone):
                    eng.wait_ge(dve.h, w)
                    eng.dma_start(out=ov, in_=ostg[:, :].rearrange("p (k t) -> p k t", t=512)
                                  ).then_inc(stm.h, 16)
                self.em("sync", f)
                stm.inc(16)


def build_nc(debug_taps=(), stop_after=None):
    return KB(debug_taps, stop_after).build()


def kernel(**inputs):
    consts = _host_consts()
    in_maps = [_host_core_inputs(c, inputs, consts) for c in range(8)]
    nc = build_nc()
    res = run_bass_kernel_spmd(nc, in_maps, core_ids=list(range(8)))
    out = np.empty((4, S, H), np.float32)
    for c in range(8):
        b, hg = c // 2, c % 2
        out[b, hg * TH:(hg + 1) * TH, :] = np.asarray(res.results[c]["out"]).T
    return out
